# version 06
# baseline (speedup 1.0000x reference)
import numpy as np
import concourse.bass as bass
import concourse.mybir as mybir
from concourse.bass_utils import run_bass_kernel_spmd
from contextlib import ExitStack

F32 = mybir.dt.float32
BF16 = mybir.dt.bfloat16
U32 = mybir.dt.uint32
AF = mybir.ActivationFunctionType
ALU = mybir.AluOpType
AX = mybir.AxisListType.X

DEV_DEBUG = False
D = 1024
DI = 2048
DS = 16
NE = 8
DE = 2048
DFF = 4096

CA = 256
CB = 256
CC = 512

ENGS = ("sync", "vector", "scalar", "gpsimd", "tensor")


class Plan:
    def __init__(self, nc):
        self.nc = nc
        self.ops = {e: [] for e in ENGS}
        self.ctr = {}
        self.sem_handles = {}
        self.waited = {e: {} for e in ENGS}
        self.pending = {e: [] for e in ENGS}
        self.since_bar = {e: False for e in ENGS}

    def sem(self, name):
        self.ctr[name] = 0
        return name

    def tok(self, s):
        return (s, self.ctr[s])

    def emit(self, eng, fn, waits=(), inc=None, dma=False):
        ws = list(self.pending[eng]) + [w for w in waits if w is not None]
        self.pending[eng] = []
        eff = []
        for (s, v) in ws:
            if v <= 0 or self.waited[eng].get(s, -1) >= v:
                continue
            self.waited[eng][s] = v
            eff.append((s, v))
        incs = []
        t = None
        if inc is not None:
            n = 16 if dma else 1
            self.ctr[inc] += n
            incs.append((inc, n))
            t = (inc, self.ctr[inc])
        self.ops[eng].append([eff, fn, incs, dma])
        self.since_bar[eng] = True
        return t

    def mark(self, eng, sem):
        item = self.ops[eng][-1]
        n = 16 if item[3] else 1
        self.ctr[sem] += n
        for i, (s0, n0) in enumerate(item[2]):
            if s0 == sem:
                item[2][i] = (s0, n0 + n)
                break
        else:
            item[2].append((sem, n))
        return (sem, self.ctr[sem])

    ENG_SEM = {"vector": "sV", "scalar": "sA", "tensor": "sM"}

    def barrier(self, bar):
        waits = []
        for e in ("vector", "scalar", "tensor"):
            if self.since_bar[e] and self.ops[e]:
                item = self.ops[e][-1]
                if item[2]:
                    s0 = item[2][0][0]
                    waits.append((s0, self.ctr[s0]))
                else:
                    waits.append(self.mark(e, self.ENG_SEM[e]))
        for e in ENGS:
            self.since_bar[e] = False
        for s in ("dW", "dG"):
            if s in self.ctr:
                waits.append((s, self.ctr[s]))
        for e in ENGS:
            self.pending[e].extend(waits)

    def replay(self, block, regfns):
        def mk(ename):
            def body(eng):
                if ename in regfns:
                    regfns[ename](eng)
                for ws, fn, incs, _dma in self.ops[ename]:
                    for (s, v) in ws:
                        eng.wait_ge(self.sem_handles[s], v)
                    inst = fn(eng)
                    for (s, n) in incs:
                        inst.then_inc(self.sem_handles[s], n)
            return body
        for ename in ENGS:
            getattr(block, ename)(mk(ename))


def _r128(ap_2d):
    return ap_2d.rearrange("(k p) m -> p k m", p=128)


def _fm3(sbuf_buf, a):
    return sbuf_buf.rearrange("p (a q) -> p a q", a=a)


def _dr3(dram_2d, colslice, a=None):
    ap = dram_2d[:, colslice]
    n = ap.shape[0] // 128
    return ap.rearrange("(a p) q -> p a q", p=128)


def build_program(t=4096):
    nca = t // CA
    ncb = t // CB
    th = t // 2
    cc = min(CC, th)
    ncc = th // cc
    ntt = th // 128
    ntc = cc // 128

    nc = bass.Bass("TRN2", target_bir_lowering=False, detect_race_conditions=False)

    def di(name, shape, dtype=F32):
        return nc.dram_tensor(name, shape, dtype, kind="ExternalInput").ap()

    x_fm = di("x_fm", [D, t])
    hoff = di("hoff", [1, 1], U32)
    in_w = di("in_w", [D, 2 * DI])
    xproj_w = di("xproj_w", [DI, 33])
    out_w = di("out_w", [DI, D])
    lif_in_w = di("lif_in_w", [D, D])
    lif_out_w = di("lif_out_w", [D, D])
    router_w = di("router_w", [D, NE])
    gate_w = di("gate_w", [NE, D, DE])
    up_w = di("up_w", [NE, D, DE])
    down_w = di("down_w", [NE, DE, D])
    fgate_w = di("fgate_w", [D, NE])
    w1 = di("w1", [NE, D, DFF])
    w2 = di("w2", [NE, DFF, D])
    b2r = di("b2r", [NE, D])
    conv_w_fm = di("conv_w_fm", [128, 64])
    conv_b_fm = di("conv_b_fm", [128, 16])
    alog_fm = di("alog_fm", [128, 256])
    dskip_fm = di("dskip_fm", [128, 16])
    dtw_fm = di("dtw_fm", [128, 16])
    dtb_fm = di("dtb_fm", [128, 16])
    ssmw_fm = di("ssmw_fm", [128, 8])
    lifw_fm = di("lifw_fm", [128, 8])
    moew_fm = di("moew_fm", [128, 8])
    b1_fm = di("b1_fm", [128, 256])
    gb_rep = di("gb_rep", [128, NE])
    c_inv1024 = di("c_inv1024", [128, 1])
    eps5 = di("eps5", [128, 1])
    eps6 = di("eps6", [128, 1])
    ones1 = di("ones1", [1, 128])
    iden = di("iden", [128, 128])

    out_fm = nc.dram_tensor("out_fm", [D, th], F32, kind="ExternalOutput").ap()

    s_xc = nc.dram_tensor("s_xc", [DI, t], BF16).ap()
    s_zs = nc.dram_tensor("s_zs", [DI, t], BF16).ap()
    s_res1 = nc.dram_tensor("s_res1", [D, t], BF16).ap()
    s_x2 = nc.dram_tensor("s_x2", [D, t], F32).ap()
    s_x4 = nc.dram_tensor("s_x4", [D, th], F32).ap()
    s_dbg1 = nc.dram_tensor("s_dbg1", [D, t], F32).ap()
    s_dbg2 = nc.dram_tensor("s_dbg2", [D, t], BF16).ap()
    s_xdbl = nc.dram_tensor("s_xdbl", [33, CB], F32).ap()
    s_xdbl_bf = nc.dram_tensor("s_xdbl_bf", [32, CB], BF16).ap()

    es = ExitStack()
    sb = lambda *a: es.enter_context(nc.sbuf_tensor(*a))
    psb = lambda *a: es.enter_context(nc.psum_tensor(*a))

    pl = Plan(nc)
    dW = pl.sem("dW")
    dG = pl.sem("dG")
    sV = pl.sem("sV")
    sA = pl.sem("sA")
    sM = pl.sem("sM")
    bar = pl.sem("bar")
    SEMS = [dW, dG, sV, sA, sM, bar]
    E = pl.emit

    C = {}
    C["c1k_bf"] = sb("c1k_bf", [128, 1], BF16)
    C["ones_bf"] = sb("ones_bf", [1, 128], BF16)
    C["iden_f"] = sb("iden_f", [128, 128], F32)
    C["convw"] = sb("convw", [128, 64], F32)
    C["convb"] = sb("convb", [128, 16], F32)
    C["A_sb"] = sb("A_sb", [128, 256], F32)
    C["dskip"] = sb("dskip", [128, 16], F32)
    C["dtw"] = sb("dtw", [128, 16], F32)
    C["dtb"] = sb("dtb", [128, 16], F32)
    C["ssmw"] = sb("ssmw", [128, 8], F32)
    C["lifw"] = sb("lifw", [128, 8], F32)
    C["moew"] = sb("moew", [128, 8], F32)
    C["b1c"] = sb("b1c", [128, 256], F32)
    C["gbr"] = sb("gbr", [128, NE], F32)
    C["b2_bf"] = sb("b2_bf", [NE, D], BF16)
    C["rout_bf"] = sb("rout_bf", [128, 8 * NE], BF16)
    C["fgat_bf"] = sb("fgat_bf", [128, 8 * NE], BF16)
    C["hoff_sb"] = sb("hoff_sb", [1, 1], U32)
    C["eps5"] = sb("eps5c", [128, 1], F32)
    C["eps6"] = sb("eps6c", [128, 1], F32)

    pm = [psb(f"pm{i}", [128, 512], F32) for i in range(3)]
    pt = psb("pt", [128, 128], F32)
    prow = psb("prow", [1, 1024], F32)
    prep = psb("prep", [128, 512], F32)
    px33 = psb("px33", [33, 512], F32)

    for nm, src in [("iden_f", iden), ("convw", conv_w_fm), ("convb", conv_b_fm),
                    ("dskip", dskip_fm), ("dtw", dtw_fm), ("dtb", dtb_fm),
                    ("ssmw", ssmw_fm), ("lifw", lifw_fm), ("moew", moew_fm),
                    ("b1c", b1_fm), ("gbr", gb_rep)]:
        E("sync", (lambda d_, s_: lambda e: e.dma_start(out=d_[:], in_=s_))(C[nm], src), inc=dW, dma=True)
    E("sync", lambda e: e.dma_start(out=C["hoff_sb"][:], in_=hoff), inc=dW, dma=True)
    t_hoff = pl.tok(dW)
    E("sync", lambda e: e.dma_start(out=C["eps5"][:], in_=eps5), inc=dW, dma=True)
    E("sync", lambda e: e.dma_start(out=C["eps6"][:], in_=eps6), inc=dW, dma=True)
    E("gpsimd", lambda e: e.dma_start(out=C["c1k_bf"][:], in_=c_inv1024), inc=dG, dma=True)
    E("gpsimd", lambda e: e.dma_start(out=C["ones_bf"][:], in_=ones1), inc=dG, dma=True)
    E("gpsimd", lambda e: e.dma_start(out=C["b2_bf"][:], in_=b2r), inc=dG, dma=True)
    E("gpsimd", lambda e: e.dma_start(out=_fm3(C["rout_bf"], 8), in_=_r128(router_w)), inc=dG, dma=True)
    E("gpsimd", lambda e: e.dma_start(out=_fm3(C["fgat_bf"], 8), in_=_r128(fgate_w)), inc=dG, dma=True)
    E("sync", lambda e: e.dma_start(out=C["A_sb"][:], in_=alog_fm), inc=dW, dma=True)
    tA = pl.tok(dW)
    tA2 = E("scalar", lambda e: e.activation(out=C["A_sb"][:], in_=C["A_sb"][:], func=AF.Exp),
            waits=[tA], inc=sA)
    E("vector", lambda e: e.tensor_scalar(out=C["A_sb"][:], in0=C["A_sb"][:], scalar1=-1.0,
                                          scalar2=None, op0=ALU.mult), waits=[tA2], inc=sV)
    t_consts = pl.tok(dW)

    regholder = {}

    with ExitStack() as phA:
        asb = lambda *a: phA.enter_context(nc.sbuf_tensor(*a))
        inw_bf = asb("inw_bf", [128, 8 * 2 * DI], BF16)
        xA = asb("xA", [128, 8 * CA], F32)
        xbfA = asb("xbfA", [128, 8 * CA], BF16)
        sqA = asb("sqA", [128, 8 * CA], BF16)
        rows = asb("rows", [1, 8 * CA], F32)
        rows_bf = asb("rows_bf", [1, 4 * CA], BF16)
        murep = asb("murep", [128, CA], F32)
        arep = asb("arep", [128, CA], F32)
        abrep = asb("abrep", [128, CA], F32)
        tmpA = asb("tmpA", [128, 8 * CA], F32)
        res1A = asb("res1A", [128, 8 * CA], BF16)
        xrA = asb("xrA", [128, 8 * CA], BF16)
        xbA = asb("xbA", [128, 16 * (CA + 3)], F32)
        ctmp = asb("ctmp", [128, 2 * CA], F32)
        xcA = asb("xcA", [128, 16 * CA], BF16)
        zsA = asb("zsA", [128, 16 * CA], BF16)

        E("gpsimd", lambda e: e.dma_start(out=_fm3(inw_bf, 8), in_=_r128(in_w)), inc=dG, dma=True)
        t_inw = pl.tok(dG)
        E("vector", lambda e: e.memset(xbA.rearrange("p (j q) -> p j q", j=16)[:, :, 0:3], 0.0), inc=sV)

        t_ln_done = None
        t_rowact_prev = None
        t_sq_done = None
        t_mm_done = None
        t_conv_done = None
        t_zs_st = None
        t_xc_st = None
        t_res_st = None
        bank_free = {}

        for c in range(nca):
            col = slice(c * CA, (c + 1) * CA)
            tx = E("sync", (lambda cl: lambda e: e.dma_start(
                out=_fm3(xA, 8), in_=_dr3(x_fm, cl)))(col),
                waits=[t_ln_done, t_sq_done, t_res_st], inc=dW, dma=True)
            tcast = E("vector", lambda e: e.tensor_copy(out=xbfA[:], in_=xA[:]), waits=[tx], inc=sV)
            tsq = E("scalar", lambda e: e.activation(out=sqA[:], in_=xA[:], func=AF.Square),
                    waits=[tx], inc=sA)
            t_sq_done = tsq
            for k in range(8):
                E("tensor", (lambda k_: lambda e: e.matmul(
                    prow[:, 0:CA], C["c1k_bf"][:], xbfA[:, k_ * CA:(k_ + 1) * CA],
                    start=(k_ == 0), stop=(k_ == 7)))(k),
                    waits=[tcast, pl.tok(dG), bank_free.get("prow")] if k == 0 else ())
            for k in range(8):
                E("tensor", (lambda k_: lambda e: e.matmul(
                    prow[:, CA:2 * CA], C["c1k_bf"][:], sqA[:, k_ * CA:(k_ + 1) * CA],
                    start=(k_ == 0), stop=(k_ == 7)))(k), waits=[tsq] if k == 0 else ())
            tstat = pl.mark("tensor", sM)
            R = lambda i: rows[:, i * CA:(i + 1) * CA]
            E("vector", lambda e: e.tensor_copy(out=R(0), in_=prow[:, 0:CA]),
              waits=[tstat, t_rowact_prev])
            tevr = E("vector", lambda e: e.tensor_copy(out=R(1), in_=prow[:, CA:2 * CA]), inc=sV)
            bank_free["prow"] = tevr
            tv0 = E("vector", lambda e: e.tensor_tensor(out=R(2), in0=R(0), in1=R(0), op=ALU.mult), inc=sV)
            tv = E("vector", lambda e: e.tensor_tensor(out=R(2), in0=R(1), in1=R(2), op=ALU.subtract), waits=[tv0], inc=sV)
            E("scalar", lambda e: e.activation(out=R(3), in_=R(2), func=AF.Ln, bias=C["eps5"][0:1, 0:1]), waits=[tv])
            ta = E("scalar", lambda e: e.activation(out=R(3), in_=R(3), func=AF.Exp, scale=-0.5), inc=sA)
            tv0 = E("vector", lambda e: e.tensor_tensor(out=R(5), in0=R(3), in1=R(3), op=ALU.mult), waits=[ta], inc=sV)
            tv = E("vector", lambda e: e.tensor_tensor(out=R(5), in0=R(2), in1=R(5), op=ALU.mult), waits=[tv0], inc=sV)
            E("scalar", lambda e: e.activation(out=R(5), in_=R(5), func=AF.Ln, bias=C["eps6"][0:1, 0:1]), waits=[tv])
            tb = E("scalar", lambda e: e.activation(out=R(5), in_=R(5), func=AF.Exp, scale=-0.5), inc=sA)
            E("vector", lambda e: e.tensor_tensor(out=R(4), in0=R(3), in1=R(5), op=ALU.mult), waits=[tb])
            t_rowact_prev = tb
            RB = lambda i: rows_bf[:, i * CA:(i + 1) * CA]
            E("vector", lambda e: e.tensor_copy(out=RB(0), in_=R(0)))
            E("vector", lambda e: e.tensor_copy(out=RB(1), in_=R(3)))
            trb = E("vector", lambda e: e.tensor_copy(out=RB(2), in_=R(4)), inc=sV)
            tm1 = E("tensor", lambda e: e.matmul(prep[:, 0:CA], C["ones_bf"][:], RB(0),
                                                 start=True, stop=True),
                    waits=[trb, bank_free.get("prep")], inc=sM)
            tev1 = E("vector", lambda e: e.tensor_copy(out=murep[:], in_=prep[:, 0:CA]), waits=[tm1], inc=sV)
            tm2 = E("tensor", lambda e: e.matmul(prep[:, 0:CA], C["ones_bf"][:], RB(1),
                                                 start=True, stop=True), waits=[tev1], inc=sM)
            tev2 = E("vector", lambda e: e.tensor_copy(out=arep[:], in_=prep[:, 0:CA]), waits=[tm2], inc=sV)
            tm3 = E("tensor", lambda e: e.matmul(prep[:, 0:CA], C["ones_bf"][:], RB(2),
                                                 start=True, stop=True), waits=[tev2], inc=sM)
            tev3 = E("vector", lambda e: e.tensor_copy(out=abrep[:], in_=prep[:, 0:CA]), waits=[tm3], inc=sV)
            bank_free["prep"] = tev3
            for k in range(8):
                kc = slice(k * CA, (k + 1) * CA)
                E("vector", (lambda kc_: lambda e: e.tensor_tensor(
                    out=tmpA[:, kc_], in0=xA[:, kc_], in1=murep[:], op=ALU.subtract))(kc),
                    waits=[t_mm_done] if k == 0 else ())
                E("vector", (lambda kc_: lambda e: e.tensor_tensor(
                    out=res1A[:, kc_], in0=tmpA[:, kc_], in1=arep[:], op=ALU.mult))(kc))
                E("vector", (lambda kc_: lambda e: e.tensor_tensor(
                    out=tmpA[:, kc_], in0=tmpA[:, kc_], in1=abrep[:], op=ALU.mult))(kc))
                E("vector", (lambda kc_, k_: lambda e: e.tensor_scalar(
                    out=xrA[:, kc_], in0=tmpA[:, kc_], scalar1=C["ssmw"][:, k_:k_ + 1],
                    scalar2=None, op0=ALU.mult))(kc, k))
            t_xr = pl.mark("vector", sV)
            t_ln_done = t_xr
            t_res_st = E("sync", (lambda cl: lambda e: e.dma_start(
                out=_dr3(s_res1, cl), in_=_fm3(res1A, 8)))(col),
                waits=[t_xr], inc=dW, dma=True)
            if c > 0:
                E("vector", lambda e: e.tensor_copy(
                    out=xbA.rearrange("p (j q) -> p j q", j=16)[:, :, 0:3],
                    in_=xbA.rearrange("p (j q) -> p j q", j=16)[:, :, CA:CA + 3]),
                    waits=[t_conv_done])
            t_halo = pl.mark("vector", sV)
            for m in range(32):
                bank = pm[m % 3]
                for k in range(8):
                    E("tensor", (lambda m_, k_, bank_: lambda e: e.matmul(
                        bank_[:, 0:CA],
                        inw_bf[:, k_ * 4096 + m_ * 128: k_ * 4096 + m_ * 128 + 128],
                        xrA[:, k_ * CA:(k_ + 1) * CA],
                        start=(k_ == 0), stop=(k_ == 7)))(m, k, bank),
                        waits=[t_xr, t_inw, t_halo, bank_free.get(m % 3)] if k == 0 else ())
                tmm = pl.mark("tensor", sM)
                if m < 16:
                    tev = E("vector", (lambda m_, bank_: lambda e: e.tensor_copy(
                        out=xbA.rearrange("p (j q) -> p j q", j=16)[:, m_, 3:3 + CA],
                        in_=bank_[:, 0:CA]))(m, bank), waits=[tmm], inc=sV)
                else:
                    tev = E("scalar", (lambda m_, bank_: lambda e: e.activation(
                        out=zsA[:, (m_ - 16) * CA:(m_ - 15) * CA], in_=bank_[:, 0:CA],
                        func=AF.Silu))(m, bank), waits=[tmm], inc=sA)
                bank_free[m % 3] = tev
            t_mm_done = pl.tok(sM)
            t_xb = pl.tok(sV)
            t_zs = pl.tok(sA)
            t_zs_st = E("sync", (lambda cl: lambda e: e.dma_start(
                out=_dr3(s_zs, cl), in_=_fm3(zsA, 16)))(col),
                waits=[t_zs], inc=dW, dma=True)
            t_silu_prev = None
            for j in range(16):
                xbj = xbA.rearrange("p (j q) -> p j q", j=16)[:, j, :]
                E("vector", (lambda j_, xbj_: lambda e: e.tensor_scalar(
                    out=ctmp[:, 0:CA], in0=xbj_[:, 0:CA],
                    scalar1=C["convw"][:, j_ * 4:j_ * 4 + 1], scalar2=None, op0=ALU.mult))(j, xbj),
                    waits=[t_xb] if j == 0 else ())
                for kk in range(1, 4):
                    E("vector", (lambda j_, kk_, xbj_: lambda e: e.scalar_tensor_tensor(
                        out=ctmp[:, 0:CA], in0=xbj_[:, kk_:kk_ + CA],
                        scalar=C["convw"][:, j_ * 4 + kk_:j_ * 4 + kk_ + 1],
                        in1=ctmp[:, 0:CA], op0=ALU.mult, op1=ALU.add))(j, kk, xbj))
                tc2 = E("vector", lambda e: e.tensor_copy(out=ctmp[:, CA:2 * CA], in_=ctmp[:, 0:CA]),
                        waits=[t_silu_prev], inc=sV)
                t_silu_prev = E("scalar", (lambda j_: lambda e: e.activation(
                    out=xcA[:, j_ * CA:(j_ + 1) * CA], in_=ctmp[:, CA:2 * CA], func=AF.Silu,
                    bias=C["convb"][:, j_:j_ + 1]))(j), waits=[tc2], inc=sA)
            t_conv_done = pl.mark("vector", sV)
            t_xc = pl.tok(sA)
            t_xc_st = E("sync", (lambda cl: lambda e: e.dma_start(
                out=_dr3(s_xc, cl), in_=_fm3(xcA, 16)))(col),
                waits=[t_xc], inc=dW, dma=True)
        pl.barrier(bar)

    with ExitStack() as phB:
        bsb = lambda *a: phB.enter_context(nc.sbuf_tensor(*a))
        outw_bf = bsb("outw_bf", [128, 16 * D], BF16)
        lifi_bf = bsb("lifi_bf", [128, 8 * D], BF16)
        lifo_bf = bsb("lifo_bf", [128, 8 * D], BF16)
        xprj_bf = bsb("xprj_bf", [128, 16 * 33], BF16)
        xcB = bsb("xcB", [128, 16 * CB], BF16)
        zsB = bsb("zsB", [128, 16 * CB], BF16)
        res1B = bsb("res1B", [128, 8 * CB], BF16)
        xdblB = bsb("xdblB", [33, CB], F32)
        dt0rep = bsb("dt0rep", [128, CB], F32)
        bcrep = bsb("bcrep", [128, 32 * CB], BF16)
        dtB = bsb("dtB", [128, 16 * CB], F32)
        wB = bsb("wB", [128, CB], BF16)
        bufA = bsb("bufA", [128, 16 * (CB + 1)], F32)
        bufB = bsb("bufB", [128, 16 * (CB + 1)], F32)
        bufC = bsb("bufC", [128, 16 * (CB + 1)], F32)
        hstate = bsb("hstate", [128, 256], F32)
        yj = bsb("yj", [128, CB], F32)
        y2B = bsb("y2B", [128, 16 * CB], BF16)
        x1B = bsb("x1B", [128, 8 * CB], F32)
        rowsB = bsb("rowsB", [1, 2 * CB], F32)
        rowsBbf = bsb("rowsBbf", [1, CB], BF16)
        brepB = bsb("brepB", [128, CB], F32)
        xr1B = bsb("xr1B", [128, 8 * CB], BF16)
        mstate = bsb("mstate", [128, 8], F32)
        cBuf = bufC
        x2B = bsb("x2B", [128, 8 * CB], F32)
        spkB = bufB.bitcast(BF16)[:, 0:8 * CB]

        E("gpsimd", lambda e: e.dma_start(out=_fm3(outw_bf, 16), in_=_r128(out_w)), inc=dG, dma=True)
        E("gpsimd", lambda e: e.dma_start(out=_fm3(lifi_bf, 8), in_=_r128(lif_in_w)), inc=dG, dma=True)
        E("gpsimd", lambda e: e.dma_start(out=_fm3(lifo_bf, 8), in_=_r128(lif_out_w)), inc=dG, dma=True)
        E("gpsimd", lambda e: e.dma_start(out=_fm3(xprj_bf, 16), in_=_r128(xproj_w)), inc=dG, dma=True)
        t_wB = pl.tok(dG)
        E("vector", lambda e: e.memset(hstate[:], 0.0))
        E("vector", lambda e: e.memset(mstate[:], 0.0))
        E("vector", lambda e: e.memset(
            bufA.rearrange("p (n q) -> p n q", n=16)[:, :, 0:1], 0.0))
        t_init = pl.mark("vector", sV)

        t_v_done = None
        t_rowactB = None
        t_xp_prev = None
        t_red_prev = None
        t_x1_prev = None
        t_dt_prev = None
        t_bc_used = None
        t_c_done = None
        t_x2_st = None
        t_spk_used = None
        t_xr1_used = None
        bfree = {}

        for c in range(ncb):
            col = slice(c * CB, (c + 1) * CB)
            tld1 = E("sync", (lambda cl: lambda e: e.dma_start(
                out=_fm3(xcB, 16), in_=_dr3(s_xc, cl)))(col),
                waits=[t_v_done, t_xp_prev] + ([pl.tok(dW), pl.tok(dG)] if c == 0 else []),
                inc=dW, dma=True)
            tld2 = E("sync", (lambda cl: lambda e: e.dma_start(
                out=_fm3(zsB, 16), in_=_dr3(s_zs, cl)))(col),
                inc=dW, dma=True)
            tld3 = E("sync", (lambda cl: lambda e: e.dma_start(
                out=_fm3(res1B, 8), in_=_dr3(s_res1, cl)))(col),
                waits=[t_x1_prev], inc=dW, dma=True)
            for k in range(16):
                E("tensor", (lambda k_: lambda e: e.matmul(
                    px33[:, 0:CB], xprj_bf[:, k_ * 33:(k_ + 1) * 33],
                    xcB[:, k_ * CB:(k_ + 1) * CB], start=(k_ == 0), stop=(k_ == 15)))(k),
                    waits=[tld1, t_wB, bfree.get("px")] if k == 0 else ())
            t_xp = pl.mark("tensor", sM)
            t_xp_prev = t_xp
            tevx = E("vector", lambda e: e.tensor_copy(out=xdblB[:], in_=px33[:, 0:CB]),
                     waits=[t_xp], inc=sV)
            bfree["px"] = tevx
            tw1 = E("sync", lambda e: e.dma_start(out=s_xdbl[:, 0:CB], in_=xdblB[:]),
                    waits=[tevx], inc=dW, dma=True)
            tw2 = E("gpsimd", lambda e: e.dma_start(out=s_xdbl_bf[:, 0:CB], in_=xdblB[1:33, :]),
                    waits=[tevx, t_bc_used], inc=dG, dma=True)
            tb1 = E("sync", lambda e: e.dma_start(
                out=dt0rep[:],
                in_=s_xdbl.rearrange("r q -> (r q)")[0:CB][None, :].partition_broadcast(128)),
                waits=[tw1, t_dt_prev], inc=dW, dma=True)
            tb2 = E("sync", lambda e: e.dma_start(
                out=bcrep[:],
                in_=s_xdbl_bf.rearrange("r q -> (r q)")[None, :].partition_broadcast(128)),
                waits=[tw2], inc=dW, dma=True)
            for j in range(16):
                E("vector", (lambda j_: lambda e: e.tensor_scalar(
                    out=dtB[:, j_ * CB:(j_ + 1) * CB], in0=dt0rep[:],
                    scalar1=C["dtw"][:, j_:j_ + 1], scalar2=C["dtb"][:, j_:j_ + 1],
                    op0=ALU.mult, op1=ALU.add))(j), waits=[tb1] if j == 0 else ())
            tpre = pl.mark("vector", sV)
            E("vector", lambda e: e.tensor_scalar(out=bufC[:, 0:16 * CB], in0=dtB[:], scalar1=0.0,
                                                  scalar2=None, op0=ALU.max))
            E("scalar", lambda e: e.activation(out=dtB[:], in_=dtB[:], func=AF.Abs), waits=[tpre])
            E("scalar", lambda e: e.activation(out=dtB[:], in_=dtB[:], func=AF.Exp, scale=-1.0))
            tsp = E("scalar", lambda e: e.activation(out=dtB[:], in_=dtB[:], func=AF.Ln, bias=1.0), inc=sA)
            tdt = E("vector", lambda e: e.tensor_tensor(out=dtB[:], in0=dtB[:], in1=bufC[:, 0:16 * CB],
                                                        op=ALU.add), waits=[tsp], inc=sV)
            t_dt_prev = tdt
            slotted = lambda buf: buf.rearrange("p (n q) -> p n q", n=16)
            for j in range(16):
                jc = slice(j * CB, (j + 1) * CB)
                for n in range(16):
                    E("vector", (lambda j_, n_: lambda e: e.tensor_scalar(
                        out=slotted(bufA)[:, n_, 1:CB + 1], in0=dtB[:, j_ * CB:(j_ + 1) * CB],
                        scalar1=C["A_sb"][:, j_ * 16 + n_:j_ * 16 + n_ + 1],
                        scalar2=None, op0=ALU.mult))(j, n),
                        waits=[tdt, bfree.get("bufA")] if n == 0 else ())
                ta1 = pl.mark("vector", sV)
                tda = E("scalar", lambda e: e.activation(out=bufB[:], in_=bufA[:], func=AF.Exp),
                        waits=[ta1, t_red_prev] + ([t_spk_used] if j == 0 else []), inc=sA)
                E("vector", lambda e: e.memset(slotted(bufB)[:, :, 0:1], 0.0), waits=[tda])
                E("vector", (lambda jc_: lambda e: e.tensor_tensor(
                    out=wB[:], in0=dtB[:, jc_], in1=xcB[:, jc_], op=ALU.mult))(jc))
                for n in range(16):
                    E("vector", (lambda n_: lambda e: e.tensor_tensor(
                        out=slotted(bufA)[:, n_, 1:CB + 1],
                        in0=bcrep[:, n_ * CB:(n_ + 1) * CB], in1=wB[:], op=ALU.mult))(n),
                        waits=[tb2] if n == 0 else ())
                t_h0 = E("vector", (lambda j_: lambda e: e.tensor_copy(
                    out=slotted(bufA)[:, :, 0], in_=hstate[:, j_ * 16:(j_ + 1) * 16]))(j),
                    waits=[t_init] if (c == 0 and j == 0) else (), inc=sV)
                E("vector", lambda e: e.tensor_tensor_scan(
                    out=bufC[:], data0=bufB[:], data1=bufA[:], initial=0.0,
                    op0=ALU.mult, op1=ALU.add), waits=[t_h0])
                bfree["bufA"] = pl.mark("vector", sV)
                E("vector", (lambda j_: lambda e: e.tensor_copy(
                    out=hstate[:, j_ * 16:(j_ + 1) * 16], in_=slotted(bufC)[:, :, CB]))(j),
                    waits=[bfree["bufA"]])
                for n in range(16):
                    E("vector", (lambda n_: lambda e: e.tensor_tensor(
                        out=bufB[:, n_ * CB:(n_ + 1) * CB], in0=slotted(bufC)[:, n_, 1:CB + 1],
                        in1=bcrep[:, (16 + n_) * CB:(17 + n_) * CB], op=ALU.mult))(n))
                tym = pl.mark("vector", sV)
                trd = E("vector", lambda e: e.reduce_sum(
                    out=yj[:], in_=bufB[:, 0:16 * CB].rearrange("p (n q) -> p q n", n=16),
                    axis=AX), waits=[tym], inc=sV)
                t_red_prev = trd
                t_yd = E("vector", (lambda j_, jc_: lambda e: e.scalar_tensor_tensor(
                    out=yj[:], in0=xcB[:, jc_], scalar=C["dskip"][:, j_:j_ + 1], in1=yj[:],
                    op0=ALU.mult, op1=ALU.add))(j, jc), waits=[trd], inc=sV)
                E("vector", (lambda jc_: lambda e: e.tensor_tensor(
                    out=y2B[:, jc_], in0=yj[:], in1=zsB[:, jc_], op=ALU.mult))(jc),
                    waits=[tld2, t_yd])
            t_y2 = pl.mark("vector", sV)
            for m in range(8):
                bank = pm[m % 2]
                for k in range(16):
                    E("tensor", (lambda m_, k_, bank_: lambda e: e.matmul(
                        bank_[:, 0:CB], outw_bf[:, k_ * D + m_ * 128: k_ * D + m_ * 128 + 128],
                        y2B[:, k_ * CB:(k_ + 1) * CB], start=(k_ == 0), stop=(k_ == 15)))(m, k, bank),
                        waits=[t_y2, bfree.get(m % 2)] if k == 0 else ())
                tmm = pl.mark("tensor", sM)
                tev = E("vector", (lambda m_, bank_: lambda e: e.tensor_tensor(
                    out=x1B[:, m_ * CB:(m_ + 1) * CB], in0=bank_[:, 0:CB],
                    in1=res1B[:, m_ * CB:(m_ + 1) * CB], op=ALU.add))(m, bank),
                    waits=[tmm, tld3], inc=sV)
                bfree[m % 2] = tev
            t_x1 = pl.tok(sV)
            t_x1_prev = t_x1
            sqB = y2B
            tsq = E("scalar", lambda e: e.activation(out=sqB[:, 0:8 * CB], in_=x1B[:], func=AF.Square),
                    waits=[t_x1], inc=sA)
            for k in range(8):
                E("tensor", (lambda k_: lambda e: e.matmul(
                    prow[:, 0:CB], C["c1k_bf"][:], sqB[:, k_ * CB:(k_ + 1) * CB],
                    start=(k_ == 0), stop=(k_ == 7)))(k),
                    waits=[tsq, bfree.get("prow")] if k == 0 else ())
            tms = pl.mark("tensor", sM)
            RB2 = lambda i: rowsB[:, i * CB:(i + 1) * CB]
            tev = E("vector", lambda e: e.tensor_copy(out=RB2(0), in_=prow[:, 0:CB]),
                    waits=[tms, t_rowactB], inc=sV)
            bfree["prow"] = tev
            E("scalar", lambda e: e.activation(out=RB2(1), in_=RB2(0), func=AF.Ln, bias=C["eps6"][0:1, 0:1]),
              waits=[tev])
            tr2 = E("scalar", lambda e: e.activation(out=RB2(1), in_=RB2(1), func=AF.Exp, scale=-0.5), inc=sA)
            t_rowactB = tr2
            trb = E("vector", lambda e: e.tensor_copy(out=rowsBbf[:], in_=RB2(1)), waits=[tr2], inc=sV)
            tm1 = E("tensor", lambda e: e.matmul(prep[:, 0:CB], C["ones_bf"][:], rowsBbf[:],
                                                 start=True, stop=True),
                    waits=[trb, bfree.get("prep")], inc=sM)
            tbr = E("vector", lambda e: e.tensor_copy(out=brepB[:], in_=prep[:, 0:CB]),
                    waits=[tm1], inc=sV)
            bfree["prep"] = tbr
            for k in range(8):
                kc = slice(k * CB, (k + 1) * CB)
                E("vector", (lambda kc_: lambda e: e.tensor_tensor(
                    out=x2B[:, kc_], in0=x1B[:, kc_], in1=brepB[:], op=ALU.mult))(kc),
                    waits=[tbr, t_xr1_used, t_x2_st] if k == 0 else ())
                E("vector", (lambda kc_, k_: lambda e: e.tensor_scalar(
                    out=xr1B[:, kc_], in0=x2B[:, kc_], scalar1=C["lifw"][:, k_:k_ + 1],
                    scalar2=None, op0=ALU.mult))(kc, k))
            t_xr1 = pl.mark("vector", sV)
            for m in range(8):
                bank = pm[m % 2]
                for k in range(8):
                    E("tensor", (lambda m_, k_, bank_: lambda e: e.matmul(
                        bank_[:, 0:CB], lifi_bf[:, k_ * D + m_ * 128: k_ * D + m_ * 128 + 128],
                        xr1B[:, k_ * CB:(k_ + 1) * CB], start=(k_ == 0), stop=(k_ == 7)))(m, k, bank),
                        waits=[t_xr1, bfree.get(m % 2), t_c_done] if k == 0 else ())
                tmm = pl.mark("tensor", sM)
                tev = E("vector", (lambda m_, bank_: lambda e: e.tensor_copy(
                    out=cBuf[:, m_ * CB:(m_ + 1) * CB], in_=bank_[:, 0:CB]))(m, bank),
                    waits=[tmm], inc=sV)
                bfree[m % 2] = tev
            t_xr1_used = pl.tok(sM)
            t_c = pl.tok(sV)
            cview = cBuf[:, 0:8 * CB].rearrange("p (m q) -> p q m", m=8)
            sview = spkB.rearrange("p (m q) -> p q m", m=8)
            for tt in range(CB):
                for gi in range(2):
                    sl = slice(gi * 4, (gi + 1) * 4)
                    E("vector", (lambda tt_, sl_: lambda e: e.scalar_tensor_tensor(
                        out=mstate[:, sl_], in0=mstate[:, sl_], scalar=0.9,
                        in1=cview[:, tt_, sl_], op0=ALU.mult, op1=ALU.add))(tt, sl),
                        waits=[t_c, t_spk_used] if (tt == 0 and gi == 0) else ())
                for gi in range(2):
                    sl = slice(gi * 4, (gi + 1) * 4)
                    E("vector", (lambda tt_, sl_: lambda e: e.tensor_scalar(
                        out=sview[:, tt_, sl_], in0=mstate[:, sl_], scalar1=1.0,
                        scalar2=None, op0=ALU.is_ge))(tt, sl))
                for gi in range(2):
                    sl = slice(gi * 4, (gi + 1) * 4)
                    E("vector", (lambda tt_, sl_: lambda e: e.tensor_tensor(
                        out=mstate[:, sl_], in0=mstate[:, sl_],
                        in1=sview[:, tt_, sl_], op=ALU.subtract))(tt, sl))
            t_spk = pl.mark("vector", sV)
            t_c_done = t_spk
            for m in range(8):
                bank = pm[m % 2]
                for k in range(8):
                    E("tensor", (lambda m_, k_, bank_: lambda e: e.matmul(
                        bank_[:, 0:CB], lifo_bf[:, k_ * D + m_ * 128: k_ * D + m_ * 128 + 128],
                        spkB[:, k_ * CB:(k_ + 1) * CB], start=(k_ == 0), stop=(k_ == 7)))(m, k, bank),
                        waits=[t_spk, bfree.get(m % 2)] if k == 0 else ())
                tmm = pl.mark("tensor", sM)
                tev = E("vector", (lambda m_, bank_: lambda e: e.tensor_tensor(
                    out=x2B[:, m_ * CB:(m_ + 1) * CB], in0=bank_[:, 0:CB],
                    in1=x1B[:, m_ * CB:(m_ + 1) * CB], op=ALU.add))(m, bank),
                    waits=[tmm], inc=sV)
                bfree[m % 2] = tev
            t_spk_used = pl.tok(sM)
            t_x2 = pl.tok(sV)
            t_v_done = t_x2
            t_x2_st = E("sync", (lambda cl: lambda e: e.dma_start(
                out=_dr3(s_x2, cl), in_=_fm3(x2B, 8)))(col),
                waits=[t_x2], inc=dW, dma=True)
            if DEV_DEBUG:
                E("sync", (lambda cl: lambda e: e.dma_start(
                    out=_dr3(s_dbg1, cl), in_=_fm3(x1B, 8)))(col), waits=[t_x2], inc=dW, dma=True)
                E("sync", (lambda cl: lambda e: e.dma_start(
                    out=_dr3(s_dbg2, cl), in_=_fm3(spkB, 8)))(col), waits=[t_x2], inc=dW, dma=True)
        pl.barrier(bar)

    def moe_ffn_phase(is_ffn, half, npass):
        nf = DFF if is_ffn else DE
        nfs = nf // 512
        thp = th // npass
        nttp = thp // 128
        nccp = max(thp // cc, 1)
        q0 = half * nccp
        pref = ("d_" if is_ffn else "c_") + str(half) + "_"
        with ExitStack() as ph:
            csb = lambda nm, *a: ph.enter_context(nc.sbuf_tensor(pref + nm, *a))
            xn = csb("xn", [128, 8 * thp], BF16)
            acc = csb("acc", [128, nttp * D], F32)
            wtm = csb("wtm", [128, nttp * 16], F32)
            xw = csb("xw", [128, 8 * cc], F32)
            xwb = csb("xwb", [128, 8 * cc], BF16)
            sqC = csb("sqC", [128, 8 * cc], BF16)
            rowsC = csb("rowsC", [1, 4 * cc], F32)
            rowsCbf = csb("rowsCbf", [1, 2 * cc], BF16)
            repC = csb("repC", [128, 2 * cc], F32)
            wsl = [csb(f"wsl{i}", [128, 8 * 512], BF16) for i in range(2)]
            usl = [csb(f"usl{i}", [128, 8 * 512], BF16) for i in range(2)]
            dsl = [csb(f"dsl{i}", [128, 4 * 1024], BF16) for i in range(2)]
            hb = csb("hb", [128, 4 * 512], BF16)
            gsb = csb("gsb", [128, 4 * 512], BF16)
            usb = csb("usb", [128, 4 * 512], BF16)
            smt = csb("smt", [128, 64], F32)
            wrow_bf = csb("wrow_bf", [8, 128], BF16)
            outc = csb("outc", [128, 8 * cc], F32)

            tE = {}
            t_src_used = None
            t_q0 = [pl.tok(dW), pl.tok(dG)]
            for qq in range(nccp):
                q = q0 + qq
                qa = slice(q * cc, (q + 1) * cc)
                if is_ffn:
                    tld = E("sync", (lambda qa_: lambda e: e.dma_start(
                        out=_fm3(xw, 8), in_=_dr3(s_x4, qa_)))(qa),
                        waits=[t_src_used] + (t_q0 if q == 0 else []), inc=dW, dma=True)
                else:
                    base = _dr3(s_x2, slice(0, cc))
                    tld = E("gpsimd", (lambda base_, q_: lambda e: e.dma_start(
                        out=_fm3(xw, 8), in_=bass.AP(base_.tensor, regholder[f"r{q_}"], base_.ap)))(base, q),
                        waits=[t_hoff, t_src_used] + (t_q0 if q == 0 else []), inc=dG, dma=True)
                tsq = E("scalar", lambda e: e.activation(out=sqC[:], in_=xw[:], func=AF.Square),
                        waits=[tld], inc=sA)
                tcst = E("vector", lambda e: e.tensor_copy(out=xwb[:], in_=xw[:]),
                         waits=[tld], inc=sV)
                for k in range(8):
                    E("tensor", (lambda k_: lambda e: e.matmul(
                        prow[:, 0:cc], C["c1k_bf"][:], sqC[:, k_ * cc:(k_ + 1) * cc],
                        start=(k_ == 0), stop=(k_ == 7)))(k),
                        waits=[tsq, tE.get("prow")] if k == 0 else ())
                if is_ffn:
                    for k in range(8):
                        E("tensor", (lambda k_: lambda e: e.matmul(
                            prow[:, cc:2 * cc], C["c1k_bf"][:], xwb[:, k_ * cc:(k_ + 1) * cc],
                            start=(k_ == 0), stop=(k_ == 7)))(k), waits=[tcst] if k == 0 else ())
                tms = pl.mark("tensor", sM)
                RC = lambda i: rowsC[:, i * cc:(i + 1) * cc]
                tev = E("vector", lambda e: e.tensor_copy(out=RC(0), in_=prow[:, 0:cc]),
                        waits=[tms, tE.get("rowact")], inc=sV)
                if is_ffn:
                    tev = E("vector", lambda e: e.tensor_copy(out=RC(1), in_=prow[:, cc:2 * cc]), inc=sV)
                tE["prow"] = tev
                if is_ffn:
                    tv0 = E("vector", lambda e: e.tensor_tensor(out=RC(2), in0=RC(1), in1=RC(1), op=ALU.mult), waits=[tev], inc=sV)
                    tv = E("vector", lambda e: e.tensor_tensor(out=RC(2), in0=RC(0), in1=RC(2), op=ALU.subtract), waits=[tv0], inc=sV)
                    E("scalar", lambda e: e.activation(out=RC(3), in_=RC(2), func=AF.Ln, bias=C["eps5"][0:1, 0:1]), waits=[tv])
                    ta = E("scalar", lambda e: e.activation(out=RC(3), in_=RC(3), func=AF.Exp, scale=-0.5), inc=sA)
                    tE["rowact"] = ta
                    E("vector", lambda e: e.tensor_copy(out=rowsCbf[:, 0:cc], in_=RC(1)), waits=[ta])
                    trb = E("vector", lambda e: e.tensor_copy(out=rowsCbf[:, cc:2 * cc], in_=RC(3)), inc=sV)
                    tm1 = E("tensor", lambda e: e.matmul(prep[:, 0:cc], C["ones_bf"][:],
                                                         rowsCbf[:, 0:cc], start=True, stop=True),
                            waits=[trb, tE.get("prep")], inc=sM)
                    te1 = E("vector", lambda e: e.tensor_copy(out=repC[:, 0:cc], in_=prep[:, 0:cc]),
                            waits=[tm1], inc=sV)
                    tm2 = E("tensor", lambda e: e.matmul(prep[:, 0:cc], C["ones_bf"][:],
                                                         rowsCbf[:, cc:2 * cc], start=True, stop=True),
                            waits=[te1], inc=sM)
                    te2 = E("vector", lambda e: e.tensor_copy(out=repC[:, cc:2 * cc], in_=prep[:, 0:cc]),
                            waits=[tm2], inc=sV)
                    tE["prep"] = te2
                    for k in range(8):
                        kc = slice(k * cc, (k + 1) * cc)
                        xnc = slice(k * thp + qq * cc, k * thp + (qq + 1) * cc)
                        E("vector", (lambda kc_: lambda e: e.tensor_tensor(
                            out=xw[:, kc_], in0=xw[:, kc_], in1=repC[:, 0:cc], op=ALU.subtract))(kc),
                            waits=[te2] if k == 0 else ())
                        E("vector", (lambda kc_, xnc_: lambda e: e.tensor_tensor(
                            out=xn[:, xnc_], in0=xw[:, kc_], in1=repC[:, cc:2 * cc], op=ALU.mult))(kc, xnc))
                else:
                    E("scalar", lambda e: e.activation(out=RC(1), in_=RC(0), func=AF.Ln, bias=C["eps6"][0:1, 0:1]),
                      waits=[tev])
                    tr2 = E("scalar", lambda e: e.activation(out=RC(1), in_=RC(1), func=AF.Exp, scale=-0.5), inc=sA)
                    tE["rowact"] = tr2
                    trb = E("vector", lambda e: e.tensor_copy(out=rowsCbf[:, 0:cc], in_=RC(1)),
                            waits=[tr2], inc=sV)
                    tm1 = E("tensor", lambda e: e.matmul(prep[:, 0:cc], C["ones_bf"][:],
                                                         rowsCbf[:, 0:cc], start=True, stop=True),
                            waits=[trb, tE.get("prep")], inc=sM)
                    te1 = E("vector", lambda e: e.tensor_copy(out=repC[:, 0:cc], in_=prep[:, 0:cc]),
                            waits=[tm1], inc=sV)
                    tE["prep"] = te1
                    for k in range(8):
                        kc = slice(k * cc, (k + 1) * cc)
                        xnc = slice(k * thp + qq * cc, k * thp + (qq + 1) * cc)
                        E("vector", (lambda kc_: lambda e: e.tensor_tensor(
                            out=xw[:, kc_], in0=xw[:, kc_], in1=repC[:, 0:cc], op=ALU.mult))(kc),
                            waits=[te1] if k == 0 else ())
                        E("vector", (lambda kc_, xnc_, k_: lambda e: e.tensor_scalar(
                            out=xn[:, xnc_], in0=xw[:, kc_], scalar1=C["moew"][:, k_:k_ + 1],
                            scalar2=None, op0=ALU.mult))(kc, xnc, k))
                t_src_used = pl.mark("vector", sV)
            t_xn = t_src_used
            gmat = C["fgat_bf"] if is_ffn else C["rout_bf"]
            for ti in range(nttp):
                for k in range(8):
                    E("tensor", (lambda ti_, k_: lambda e: e.matmul(
                        pt[:, 0:NE], xn[:, k_ * thp + ti_ * 128: k_ * thp + ti_ * 128 + 128],
                        gmat[:, k_ * NE:(k_ + 1) * NE], start=(k_ == 0), stop=(k_ == 7)))(ti, k),
                        waits=[t_xn, tE.get("pt")] if k == 0 else ())
                tmg = pl.mark("tensor", sM)
                S = lambda i: smt[:, i * 8:(i + 1) * 8]
                tev = E("vector", lambda e: e.tensor_copy(out=S(0), in_=pt[:, 0:NE]),
                        waits=[tmg], inc=sV)
                tE["pt"] = tev
                if is_ffn:
                    tev = E("vector", lambda e: e.tensor_tensor(out=S(0), in0=S(0), in1=C["gbr"][:],
                                                                op=ALU.add), waits=[tev], inc=sV)
                tmx = E("vector", lambda e: e.reduce_max(out=smt[:, 56:57], in_=S(0), axis=AX),
                        waits=[tev], inc=sV)
                tsb = E("vector", lambda e: e.tensor_tensor(
                    out=S(1), in0=S(0), in1=smt[:, 56:57].broadcast_to((128, 8)), op=ALU.subtract),
                    waits=[tmx], inc=sV)
                tex = E("scalar", lambda e: e.activation(out=S(2), in_=S(1), func=AF.Exp),
                        waits=[tsb], inc=sA)
                tsm = E("vector", lambda e: e.reduce_sum(out=smt[:, 57:58], in_=S(2), axis=AX),
                        waits=[tex], inc=sV)
                trc = E("vector", lambda e: e.reciprocal(out=smt[:, 58:59], in_=smt[:, 57:58]),
                        waits=[tsm], inc=sV)
                tpb = E("vector", lambda e: e.tensor_tensor(
                    out=S(3), in0=S(2), in1=smt[:, 58:59].broadcast_to((128, 8)), op=ALU.mult),
                    waits=[trc], inc=sV)
                if is_ffn:
                    tm2_ = E("vector", lambda e: e.reduce_max(out=smt[:, 59:60], in_=S(3), axis=AX),
                             waits=[tpb], inc=sV)
                    tmk = E("vector", lambda e: e.tensor_tensor(
                        out=S(4), in0=S(3), in1=smt[:, 59:60].broadcast_to((128, 8)), op=ALU.is_ge),
                        waits=[tm2_], inc=sV)
                else:
                    tm1_ = E("vector", lambda e: e.reduce_max(out=smt[:, 59:60], in_=S(3), axis=AX),
                             waits=[tpb], inc=sV)
                    tlt = E("vector", lambda e: e.tensor_tensor(
                        out=S(5), in0=S(3), in1=smt[:, 59:60].broadcast_to((128, 8)), op=ALU.is_lt),
                        waits=[tm1_], inc=sV)
                    tz = E("vector", lambda e: e.tensor_tensor(out=S(5), in0=S(3), in1=S(5),
                                                               op=ALU.mult), waits=[tlt], inc=sV)
                    tm2_ = E("vector", lambda e: e.reduce_max(out=smt[:, 60:61], in_=S(5), axis=AX),
                             waits=[tz], inc=sV)
                    tmk = E("vector", lambda e: e.tensor_tensor(
                        out=S(4), in0=S(3), in1=smt[:, 60:61].broadcast_to((128, 8)), op=ALU.is_ge),
                        waits=[tm2_], inc=sV)
                E("vector", (lambda ti_: lambda e: e.tensor_tensor(
                    out=wtm[:, ti_ * 16:ti_ * 16 + 8], in0=S(3), in1=S(4), op=ALU.mult))(ti),
                    waits=[tmk])
            t_w = pl.mark("vector", sV)
            E("vector", lambda e: e.memset(acc[:], 0.0))
            t_acc0 = pl.mark("vector", sV)
            t_par_used = [None, None]
            for ee in range(NE):
                for fs in range(nfs):
                    par = (ee * nfs + fs) % 2
                    if is_ffn:
                        wsrc = w1[ee, :, fs * 512:(fs + 1) * 512]
                        dsrc = w2[ee, fs * 512:(fs + 1) * 512, :]
                    else:
                        wsrc = gate_w[ee, :, fs * 512:(fs + 1) * 512]
                        dsrc = down_w[ee, fs * 512:(fs + 1) * 512, :]
                    tl1 = E("gpsimd", (lambda w_, p_: lambda e: e.dma_start(
                        out=_fm3(wsl[p_], 8), in_=_r128(w_)))(wsrc, par),
                        waits=[t_par_used[par]], inc=dG, dma=True)
                    if not is_ffn:
                        tl2 = E("gpsimd", (lambda w_, p_: lambda e: e.dma_start(
                            out=_fm3(usl[p_], 8), in_=_r128(w_)))(up_w[ee, :, fs * 512:(fs + 1) * 512], par),
                            inc=dG, dma=True)
                    tl3 = E("gpsimd", (lambda w_, p_: lambda e: e.dma_start(
                        out=_fm3(dsl[p_], 4), in_=_r128(w_)))(dsrc, par), inc=dG, dma=True)
                    t_wload = pl.tok(dG)
                    for qq in range(nccp):
                        q = q0 + qq
                        for mf in range(4):
                            bank = pm[0]
                            for k in range(8):
                                E("tensor", (lambda k_, mf_, q_, p_, bank_: lambda e: e.matmul(
                                    bank_[:, 0:cc],
                                    wsl[p_][:, k_ * 512 + mf_ * 128: k_ * 512 + mf_ * 128 + 128],
                                    xn[:, k_ * thp + q_ * cc: k_ * thp + (q_ + 1) * cc],
                                    start=(k_ == 0), stop=(k_ == 7)))(k, mf, qq, par, bank),
                                    waits=[t_wload, t_xn, t_acc0, tE.get("h")] if k == 0 else ())
                            tmmg = pl.mark("tensor", sM)
                            if is_ffn:
                                tg = E("scalar", (lambda mf_, bank_, e_, fs_: lambda e: e.activation(
                                    out=hb[:, mf_ * cc:(mf_ + 1) * cc], in_=bank_[:, 0:cc],
                                    func=AF.Gelu, bias=C["b1c"][:, e_ * 32 + fs_ * 4 + mf_:
                                                               e_ * 32 + fs_ * 4 + mf_ + 1]))(mf, bank, ee, fs),
                                    waits=[tmmg, tE.get("hbfree")], inc=sA)
                            else:
                                tg = E("scalar", (lambda mf_, bank_: lambda e: e.activation(
                                    out=gsb[:, mf_ * cc:(mf_ + 1) * cc], in_=bank_[:, 0:cc],
                                    func=AF.Silu))(mf, bank), waits=[tmmg], inc=sA)
                            tE["h"] = tg
                        if not is_ffn:
                            for mf in range(4):
                                bank = pm[1]
                                for k in range(8):
                                    E("tensor", (lambda k_, mf_, q_, p_, bank_: lambda e: e.matmul(
                                        bank_[:, 0:cc],
                                        usl[p_][:, k_ * 512 + mf_ * 128: k_ * 512 + mf_ * 128 + 128],
                                        xn[:, k_ * thp + q_ * cc: k_ * thp + (q_ + 1) * cc],
                                        start=(k_ == 0), stop=(k_ == 7)))(k, mf, qq, par, bank),
                                        waits=[tE.get("u")] if k == 0 else ())
                                tmmg = pl.mark("tensor", sM)
                                tu = E("vector", (lambda mf_, bank_: lambda e: e.tensor_copy(
                                    out=usb[:, mf_ * cc:(mf_ + 1) * cc], in_=bank_[:, 0:cc]))(mf, bank),
                                    waits=[tmmg], inc=sV)
                                tE["u"] = tu
                            tgs = pl.tok(sA)
                            for mf in range(4):
                                E("vector", (lambda mf_: lambda e: e.tensor_tensor(
                                    out=hb[:, mf_ * cc:(mf_ + 1) * cc],
                                    in0=gsb[:, mf_ * cc:(mf_ + 1) * cc],
                                    in1=usb[:, mf_ * cc:(mf_ + 1) * cc], op=ALU.mult))(mf),
                                    waits=[tgs, tE.get("hbfree")] if mf == 0 else ())
                            t_h = pl.mark("vector", sV)
                        else:
                            t_h = pl.tok(sA)
                        for tm in range(ntc):
                            for dh in range(2):
                                bank = pm[2]
                                for kf in range(4):
                                    E("tensor", (lambda kf_, tm_, dh_, p_, bank_: lambda e: e.matmul(
                                        bank_[:, 0:512],
                                        hb[:, kf_ * cc + tm_ * 128: kf_ * cc + tm_ * 128 + 128],
                                        dsl[p_][:, kf_ * 1024 + dh_ * 512: kf_ * 1024 + (dh_ + 1) * 512],
                                        start=(kf_ == 0), stop=(kf_ == 3)))(kf, tm, dh, par, bank),
                                        waits=[t_h, t_w, tE.get("d")] if kf == 0 else ())
                                tmmg = pl.mark("tensor", sM)
                                ti_g = qq * ntc + tm
                                td = E("vector", (lambda ti__, dh_, e_, bank_: lambda e: e.scalar_tensor_tensor(
                                    out=acc[:, ti__ * D + dh_ * 512: ti__ * D + (dh_ + 1) * 512],
                                    in0=bank_[:, 0:512],
                                    scalar=wtm[:, ti__ * 16 + e_:ti__ * 16 + e_ + 1],
                                    in1=acc[:, ti__ * D + dh_ * 512: ti__ * D + (dh_ + 1) * 512],
                                    op0=ALU.mult, op1=ALU.add))(ti_g, dh, ee, bank),
                                    waits=[tmmg], inc=sV)
                                tE["d"] = td
                                tE["hbfree"] = pl.tok(sM)
                    t_par_used[par] = pl.tok(sM)
            t_expert_done = pl.tok(sV)
            if is_ffn:
                for ti in range(nttp):
                    ttr = E("tensor", (lambda ti_: lambda e: e.transpose(
                        pt[0:8, 0:128], wtm[:, ti_ * 16:ti_ * 16 + 8], C["iden_f"][:]))(ti),
                        waits=[t_expert_done, tE.get("pt2")], inc=sM)
                    tevt = E("vector", lambda e: e.tensor_copy(out=wrow_bf[:], in_=pt[0:8, 0:128]),
                             waits=[ttr], inc=sV)
                    tE["pt2"] = tevt
                    for dh in range(2):
                        bank = pm[dh]
                        tmb = E("tensor", (lambda dh_, bank_: lambda e: e.matmul(
                            bank_[:, 0:512], wrow_bf[:],
                            C["b2_bf"][:, dh_ * 512:(dh_ + 1) * 512],
                            start=True, stop=True))(dh, bank),
                            waits=[tevt, tE.get(("b2", dh))], inc=sM)
                        tb2e = E("vector", (lambda ti_, dh_, bank_: lambda e: e.tensor_tensor(
                            out=acc[:, ti_ * D + dh_ * 512: ti_ * D + (dh_ + 1) * 512],
                            in0=acc[:, ti_ * D + dh_ * 512: ti_ * D + (dh_ + 1) * 512],
                            in1=bank_[:, 0:512], op=ALU.add))(ti, dh, bank),
                            waits=[tmb], inc=sV)
                        tE[("b2", dh)] = tb2e
            t_acc_done = pl.mark("vector", sV)
            t_out_used = None
            for qq in range(nccp):
                q = q0 + qq
                qa = slice(q * cc, (q + 1) * cc)
                if is_ffn:
                    tld = E("sync", (lambda qa_: lambda e: e.dma_start(
                        out=_fm3(xw, 8), in_=_dr3(s_x4, qa_)))(qa),
                        waits=[t_out_used, t_xn], inc=dW, dma=True)
                    tld2 = None
                else:
                    base = _dr3(s_x2, slice(0, cc))
                    tld = E("gpsimd", (lambda base_, q_: lambda e: e.dma_start(
                        out=_fm3(xw, 8), in_=bass.AP(base_.tensor, regholder[f"r{q_}"], base_.ap)))(base, q),
                        waits=[t_hoff, t_out_used, t_xn], inc=dG, dma=True)
                    base2 = _dr3(x_fm, slice(0, cc))
                    tld2 = E("gpsimd", (lambda base_, q_: lambda e: e.dma_start(
                        out=_fm3(outc, 8), in_=bass.AP(base_.tensor, regholder[f"r{q_}"], base_.ap)))(base2, q),
                        inc=dG, dma=True)
                for tm in range(ntc):
                    ti_g = qq * ntc + tm
                    for j in range(8):
                        ttr = E("tensor", (lambda ti__, j_: lambda e: e.transpose(
                            pt[:], acc[:, ti__ * D + j_ * 128: ti__ * D + (j_ + 1) * 128],
                            C["iden_f"][:]))(ti_g, j),
                            waits=[t_acc_done, tE.get("ptt")], inc=sM)
                        if is_ffn:
                            tvv = E("vector", (lambda j_, tm_: lambda e: e.tensor_tensor(
                                out=outc[:, j_ * cc + tm_ * 128: j_ * cc + tm_ * 128 + 128],
                                in0=pt[:], in1=xw[:, j_ * cc + tm_ * 128: j_ * cc + tm_ * 128 + 128],
                                op=ALU.add))(j, tm), waits=[ttr, tld], inc=sV)
                        else:
                            E("vector", (lambda j_, tm_: lambda e: e.tensor_tensor(
                                out=xw[:, j_ * cc + tm_ * 128: j_ * cc + tm_ * 128 + 128],
                                in0=pt[:], in1=xw[:, j_ * cc + tm_ * 128: j_ * cc + tm_ * 128 + 128],
                                op=ALU.add))(j, tm), waits=[ttr, tld, tld2])
                            tvv = E("vector", (lambda j_, tm_: lambda e: e.tensor_tensor(
                                out=outc[:, j_ * cc + tm_ * 128: j_ * cc + tm_ * 128 + 128],
                                in0=outc[:, j_ * cc + tm_ * 128: j_ * cc + tm_ * 128 + 128],
                                in1=xw[:, j_ * cc + tm_ * 128: j_ * cc + tm_ * 128 + 128],
                                op=ALU.add))(j, tm), inc=sV)
                        tE["ptt"] = tvv
                t_wr = pl.mark("vector", sV)
                if is_ffn:
                    t_st = E("sync", (lambda qa_: lambda e: e.dma_start(
                        out=_dr3(out_fm, qa_), in_=_fm3(outc, 8)))(qa),
                        waits=[t_wr], inc=dW, dma=True)
                else:
                    t_st = E("sync", (lambda qa_: lambda e: e.dma_start(
                        out=_dr3(s_x4, qa_), in_=_fm3(outc, 8)))(qa),
                        waits=[t_wr], inc=dW, dma=True)
                t_out_used = t_st
        pl.barrier(bar)

    npass = 2 if th >= 2 * cc else 1
    for hp in range(npass):
        moe_ffn_phase(False, hp, npass)
    for hp in range(npass):
        moe_ffn_phase(True, hp, npass)

    tall_w = pl.tok(dW)
    tall_g = pl.tok(dG)
    E("sync", lambda e: e.nop(), waits=[tall_w, tall_g])

    def gpregs(eng):
        regs = [eng.register(f"roff{q}").__enter__() for q in range(ncc)]
        eng.wait_ge(pl.sem_handles[dW], t_hoff[1])
        eng.reg_load(regs[0], C["hoff_sb"][0:1, 0:1])
        for q in range(1, ncc):
            eng.reg_alu(regs[q], regs[0], q * cc, ALU.add)
        for q in range(ncc):
            regholder[f"r{q}"] = regs[q]

    with nc.Block() as block:
        for s in SEMS:
            pl.sem_handles[s] = es.enter_context(nc.semaphore(s))
        pl.replay(block, {"gpsimd": gpregs})

    es.close()
    return nc


def _stage(inputs, t):
    th = t // 2
    x = np.asarray(inputs["L0_attn_x"])
    f = {}
    f["in_w"] = np.ascontiguousarray(inputs["L0_attn_in_w"], np.float32)
    f["xproj_w"] = np.ascontiguousarray(inputs["L0_attn_x_proj_w"], np.float32)
    f["out_w"] = np.ascontiguousarray(inputs["L0_attn_out_w"], np.float32)
    f["lif_in_w"] = np.ascontiguousarray(inputs["L0_attn_lif_in_w"], np.float32)
    f["lif_out_w"] = np.ascontiguousarray(inputs["L0_attn_lif_out_w"], np.float32)
    f["router_w"] = np.ascontiguousarray(inputs["L0_attn_router_w"], np.float32)
    f["gate_w"] = np.ascontiguousarray(inputs["L0_attn_gate_w"], np.float32)
    f["up_w"] = np.ascontiguousarray(inputs["L0_attn_up_w"], np.float32)
    f["down_w"] = np.ascontiguousarray(inputs["L0_attn_down_w"], np.float32)
    f["fgate_w"] = np.ascontiguousarray(inputs["L0_ffn_gate_w"], np.float32)
    f["w1"] = np.ascontiguousarray(inputs["L0_ffn_w1"], np.float32)
    f["w2"] = np.ascontiguousarray(inputs["L0_ffn_w2"], np.float32)
    f["b2r"] = np.ascontiguousarray(inputs["L0_ffn_b2"], np.float32)
    cw = np.asarray(inputs["L0_attn_conv_w"], np.float32)[:, 0, :]
    f["conv_w_fm"] = np.ascontiguousarray(cw.reshape(16, 128, 4).transpose(1, 0, 2).reshape(128, 64))
    f["conv_b_fm"] = np.ascontiguousarray(np.asarray(inputs["L0_attn_conv_b"], np.float32).reshape(16, 128).T)
    al = np.asarray(inputs["L0_attn_A_log"], np.float32)
    f["alog_fm"] = np.ascontiguousarray(al.reshape(16, 128, 16).transpose(1, 0, 2).reshape(128, 256))
    f["dskip_fm"] = np.ascontiguousarray(np.asarray(inputs["L0_attn_Dskip"], np.float32).reshape(16, 128).T)
    f["dtw_fm"] = np.ascontiguousarray(np.asarray(inputs["L0_attn_dt_w"], np.float32)[0].reshape(16, 128).T)
    f["dtb_fm"] = np.ascontiguousarray(np.asarray(inputs["L0_attn_dt_b"], np.float32).reshape(16, 128).T)
    f["ssmw_fm"] = np.ascontiguousarray(np.asarray(inputs["L0_attn_ssm_norm_w"], np.float32).reshape(8, 128).T)
    f["lifw_fm"] = np.ascontiguousarray(np.asarray(inputs["L0_attn_lif_norm_w"], np.float32).reshape(8, 128).T)
    f["moew_fm"] = np.ascontiguousarray(np.asarray(inputs["L0_attn_moe_norm_w"], np.float32).reshape(8, 128).T)
    b1 = np.asarray(inputs["L0_ffn_b1"], np.float32)
    f["b1_fm"] = np.ascontiguousarray(b1.reshape(8, 32, 128).transpose(2, 0, 1).reshape(128, 256))
    f["gb_rep"] = np.ascontiguousarray(np.tile(np.asarray(inputs["L0_ffn_gate_b"], np.float32)[None, :], (128, 1)))
    f["c_inv1024"] = np.full((128, 1), 1.0 / 1024, np.float32)
    f["eps5"] = np.full((128, 1), 1e-5, np.float32)
    f["eps6"] = np.full((128, 1), 1e-6, np.float32)
    f["ones1"] = np.ones((1, 128), np.float32)
    f["iden"] = np.eye(128, dtype=np.float32)

    feeds = []
    for c in range(8):
        b, h = c // 2, c % 2
        fc = dict(f)
        fc["x_fm"] = np.ascontiguousarray(x[b].T.astype(np.float32))
        fc["hoff"] = np.full((1, 1), h * th, np.uint32)
        feeds.append(fc)
    return feeds


def kernel(**inputs):
    t = 4096
    th = t // 2
    feeds = _stage(inputs, t)
    nc = build_program(t)
    res = run_bass_kernel_spmd(nc, feeds, core_ids=list(range(8)))
    out = np.empty((4, t, D), np.float32)
    for c in range(8):
        b, h = c // 2, c % 2
        out[b, h * th:(h + 1) * th, :] = np.asarray(res.results[c]["out_fm"]).T
    return out


# version 07 (final)
# speedup vs baseline: 1.5569x; 1.5569x over previous
import numpy as np
import concourse.bass as bass
import concourse.mybir as mybir
from concourse.bass_utils import run_bass_kernel_spmd
from contextlib import ExitStack

F32 = mybir.dt.float32
BF16 = mybir.dt.bfloat16
U32 = mybir.dt.uint32
AF = mybir.ActivationFunctionType
ALU = mybir.AluOpType
AX = mybir.AxisListType.X

DEV_DEBUG = False
D = 1024
DI = 2048
DS = 16
NE = 8
DE = 2048
DFF = 4096

CA = 256
CB = 256
CC = 512

ENGS = ("sync", "vector", "scalar", "gpsimd", "tensor")


class Plan:
    def __init__(self, nc):
        self.nc = nc
        self.ops = {e: [] for e in ENGS}
        self.ctr = {}
        self.sem_handles = {}
        self.waited = {e: {} for e in ENGS}
        self.pending = {e: [] for e in ENGS}
        self.since_bar = {e: False for e in ENGS}

    def sem(self, name):
        self.ctr[name] = 0
        return name

    def tok(self, s):
        return (s, self.ctr[s])

    def emit(self, eng, fn, waits=(), inc=None, dma=False):
        ws = list(self.pending[eng]) + [w for w in waits if w is not None]
        self.pending[eng] = []
        eff = []
        for (s, v) in ws:
            if v <= 0 or self.waited[eng].get(s, -1) >= v:
                continue
            self.waited[eng][s] = v
            eff.append((s, v))
        incs = []
        t = None
        if inc is not None:
            n = 16 if dma else 1
            self.ctr[inc] += n
            incs.append((inc, n))
            t = (inc, self.ctr[inc])
        self.ops[eng].append([eff, fn, incs, dma])
        self.since_bar[eng] = True
        return t

    def mark(self, eng, sem):
        item = self.ops[eng][-1]
        n = 16 if item[3] else 1
        self.ctr[sem] += n
        for i, (s0, n0) in enumerate(item[2]):
            if s0 == sem:
                item[2][i] = (s0, n0 + n)
                break
        else:
            item[2].append((sem, n))
        return (sem, self.ctr[sem])

    ENG_SEM = {"vector": "sV", "scalar": "sA", "tensor": "sM"}

    def barrier(self, bar):
        waits = []
        for e in ("vector", "scalar", "tensor"):
            if self.since_bar[e] and self.ops[e]:
                item = self.ops[e][-1]
                if item[2]:
                    s0 = item[2][0][0]
                    waits.append((s0, self.ctr[s0]))
                else:
                    waits.append(self.mark(e, self.ENG_SEM[e]))
        for e in ENGS:
            self.since_bar[e] = False
        for s in ("dW", "dG"):
            if s in self.ctr:
                waits.append((s, self.ctr[s]))
        for e in ENGS:
            self.pending[e].extend(waits)

    def replay(self, block, regfns):
        def mk(ename):
            def body(eng):
                if ename in regfns:
                    regfns[ename](eng)
                for ws, fn, incs, _dma in self.ops[ename]:
                    for (s, v) in ws:
                        eng.wait_ge(self.sem_handles[s], v)
                    inst = fn(eng)
                    for (s, n) in incs:
                        inst.then_inc(self.sem_handles[s], n)
            return body
        for ename in ENGS:
            getattr(block, ename)(mk(ename))


def _r128(ap_2d):
    return ap_2d.rearrange("(k p) m -> p k m", p=128)


def _fm3(sbuf_buf, a):
    return sbuf_buf.rearrange("p (a q) -> p a q", a=a)


def _dr3(dram_2d, colslice, a=None):
    ap = dram_2d[:, colslice]
    n = ap.shape[0] // 128
    return ap.rearrange("(a p) q -> p a q", p=128)


def build_program(t=4096):
    nca = t // CA
    ncb = t // CB
    th = t // 2
    cc = min(CC, th)
    ncc = th // cc
    ntt = th // 128
    ntc = cc // 128

    nc = bass.Bass("TRN2", target_bir_lowering=False, detect_race_conditions=False)

    def di(name, shape, dtype=F32):
        return nc.dram_tensor(name, shape, dtype, kind="ExternalInput").ap()

    x_fm = di("x_fm", [D, t])
    hoff = di("hoff", [1, 1], U32)
    in_w = di("in_w", [D, 2 * DI])
    xproj_w = di("xproj_w", [DI, 33])
    out_w = di("out_w", [DI, D])
    lif_in_w = di("lif_in_w", [D, D])
    lif_out_w = di("lif_out_w", [D, D])
    router_w = di("router_w", [D, NE])
    gate_w = di("gate_w", [NE, D, DE])
    up_w = di("up_w", [NE, D, DE])
    down_w = di("down_w", [NE, DE, D])
    fgate_w = di("fgate_w", [D, NE])
    w1 = di("w1", [NE, D, DFF])
    w2 = di("w2", [NE, DFF, D])
    b2r = di("b2r", [NE, D])
    conv_w_fm = di("conv_w_fm", [128, 64])
    conv_b_fm = di("conv_b_fm", [128, 16])
    alog_fm = di("alog_fm", [128, 256])
    dskip_fm = di("dskip_fm", [128, 16])
    dtw_fm = di("dtw_fm", [128, 16])
    dtb_fm = di("dtb_fm", [128, 16])
    ssmw_fm = di("ssmw_fm", [128, 8])
    lifw_fm = di("lifw_fm", [128, 8])
    moew_fm = di("moew_fm", [128, 8])
    b1_fm = di("b1_fm", [128, 256])
    gb_rep = di("gb_rep", [128, NE])
    c_inv1024 = di("c_inv1024", [128, 1])
    eps5 = di("eps5", [128, 1])
    eps6 = di("eps6", [128, 1])
    ones1 = di("ones1", [1, 128])
    iden = di("iden", [128, 128])

    out_fm = nc.dram_tensor("out_fm", [D, th], F32, kind="ExternalOutput").ap()

    s_xc = nc.dram_tensor("s_xc", [DI, t], BF16).ap()
    s_zs = nc.dram_tensor("s_zs", [DI, t], BF16).ap()
    s_res1 = nc.dram_tensor("s_res1", [D, t], BF16).ap()
    s_x2 = nc.dram_tensor("s_x2", [D, t], F32).ap()
    s_x4 = nc.dram_tensor("s_x4", [D, th], F32).ap()
    s_dbg1 = nc.dram_tensor("s_dbg1", [D, t], F32).ap()
    s_dbg2 = nc.dram_tensor("s_dbg2", [D, t], BF16).ap()
    s_xdbl = nc.dram_tensor("s_xdbl", [33, CB], F32).ap()
    s_xdbl_bf = nc.dram_tensor("s_xdbl_bf", [32, CB], BF16).ap()

    es = ExitStack()
    sb = lambda *a: es.enter_context(nc.sbuf_tensor(*a))
    psb = lambda *a: es.enter_context(nc.psum_tensor(*a))

    pl = Plan(nc)
    dW = pl.sem("dW")
    dG = pl.sem("dG")
    sV = pl.sem("sV")
    sA = pl.sem("sA")
    sM = pl.sem("sM")
    bar = pl.sem("bar")
    SEMS = [dW, dG, sV, sA, sM, bar]
    E = pl.emit

    C = {}
    C["c1k_bf"] = sb("c1k_bf", [128, 1], BF16)
    C["ones_bf"] = sb("ones_bf", [1, 128], BF16)
    C["iden_f"] = sb("iden_f", [128, 128], F32)
    C["convw"] = sb("convw", [128, 64], F32)
    C["convb"] = sb("convb", [128, 16], F32)
    C["A_sb"] = sb("A_sb", [128, 256], F32)
    C["dskip"] = sb("dskip", [128, 16], F32)
    C["dtw"] = sb("dtw", [128, 16], F32)
    C["dtb"] = sb("dtb", [128, 16], F32)
    C["ssmw"] = sb("ssmw", [128, 8], F32)
    C["lifw"] = sb("lifw", [128, 8], F32)
    C["moew"] = sb("moew", [128, 8], F32)
    C["b1c"] = sb("b1c", [128, 256], F32)
    C["gbr"] = sb("gbr", [128, NE], F32)
    C["b2_bf"] = sb("b2_bf", [NE, D], BF16)
    C["rout_bf"] = sb("rout_bf", [128, 8 * NE], BF16)
    C["fgat_bf"] = sb("fgat_bf", [128, 8 * NE], BF16)
    C["hoff_sb"] = sb("hoff_sb", [1, 1], U32)
    C["eps5"] = sb("eps5c", [128, 1], F32)
    C["eps6"] = sb("eps6c", [128, 1], F32)

    pm = [psb(f"pm{i}", [128, 512], F32) for i in range(3)]
    pt = psb("pt", [128, 128], F32)
    prow = psb("prow", [1, 1024], F32)
    prep = psb("prep", [128, 512], F32)
    px33 = psb("px33", [33, 512], F32)

    for nm, src in [("iden_f", iden), ("convw", conv_w_fm), ("convb", conv_b_fm),
                    ("dskip", dskip_fm), ("dtw", dtw_fm), ("dtb", dtb_fm),
                    ("ssmw", ssmw_fm), ("lifw", lifw_fm), ("moew", moew_fm),
                    ("b1c", b1_fm), ("gbr", gb_rep)]:
        E("sync", (lambda d_, s_: lambda e: e.dma_start(out=d_[:], in_=s_))(C[nm], src), inc=dW, dma=True)
    E("sync", lambda e: e.dma_start(out=C["hoff_sb"][:], in_=hoff), inc=dW, dma=True)
    t_hoff = pl.tok(dW)
    E("sync", lambda e: e.dma_start(out=C["eps5"][:], in_=eps5), inc=dW, dma=True)
    E("sync", lambda e: e.dma_start(out=C["eps6"][:], in_=eps6), inc=dW, dma=True)
    E("gpsimd", lambda e: e.dma_start(out=C["c1k_bf"][:], in_=c_inv1024), inc=dG, dma=True)
    E("gpsimd", lambda e: e.dma_start(out=C["ones_bf"][:], in_=ones1), inc=dG, dma=True)
    E("gpsimd", lambda e: e.dma_start(out=C["b2_bf"][:], in_=b2r), inc=dG, dma=True)
    E("gpsimd", lambda e: e.dma_start(out=_fm3(C["rout_bf"], 8), in_=_r128(router_w)), inc=dG, dma=True)
    E("gpsimd", lambda e: e.dma_start(out=_fm3(C["fgat_bf"], 8), in_=_r128(fgate_w)), inc=dG, dma=True)
    E("sync", lambda e: e.dma_start(out=C["A_sb"][:], in_=alog_fm), inc=dW, dma=True)
    tA = pl.tok(dW)
    tA2 = E("scalar", lambda e: e.activation(out=C["A_sb"][:], in_=C["A_sb"][:], func=AF.Exp),
            waits=[tA], inc=sA)
    E("vector", lambda e: e.tensor_scalar(out=C["A_sb"][:], in0=C["A_sb"][:], scalar1=-1.0,
                                          scalar2=None, op0=ALU.mult), waits=[tA2], inc=sV)
    t_consts = pl.tok(dW)

    regholder = {}

    with ExitStack() as phA:
        asb = lambda *a: phA.enter_context(nc.sbuf_tensor(*a))
        inw_bf = asb("inw_bf", [128, 8 * 2 * DI], BF16)
        xA = asb("xA", [128, 8 * CA], F32)
        xbfA = asb("xbfA", [128, 8 * CA], BF16)
        sqA = asb("sqA", [128, 8 * CA], BF16)
        rows = asb("rows", [1, 8 * CA], F32)
        rows_bf = asb("rows_bf", [1, 4 * CA], BF16)
        murep = asb("murep", [128, CA], F32)
        arep = asb("arep", [128, CA], F32)
        abrep = asb("abrep", [128, CA], F32)
        tmpA = asb("tmpA", [128, 8 * CA], F32)
        res1A = asb("res1A", [128, 8 * CA], BF16)
        xrA = asb("xrA", [128, 8 * CA], BF16)
        xbA = asb("xbA", [128, 16 * (CA + 3)], F32)
        ctmp = asb("ctmp", [128, 2 * CA], F32)
        xcA = asb("xcA", [128, 16 * CA], BF16)
        zsA = asb("zsA", [128, 16 * CA], BF16)

        E("gpsimd", lambda e: e.dma_start(out=_fm3(inw_bf, 8), in_=_r128(in_w)), inc=dG, dma=True)
        t_inw = pl.tok(dG)
        E("vector", lambda e: e.memset(xbA.rearrange("p (j q) -> p j q", j=16)[:, :, 0:3], 0.0), inc=sV)

        t_ln_done = None
        t_rowact_prev = None
        t_sq_done = None
        t_mm_done = None
        t_conv_done = None
        t_zs_st = None
        t_xc_st = None
        t_res_st = None
        bank_free = {}

        for c in range(nca):
            col = slice(c * CA, (c + 1) * CA)
            tx = E("sync", (lambda cl: lambda e: e.dma_start(
                out=_fm3(xA, 8), in_=_dr3(x_fm, cl)))(col),
                waits=[t_ln_done, t_sq_done, t_res_st], inc=dW, dma=True)
            tcast = E("vector", lambda e: e.tensor_copy(out=xbfA[:], in_=xA[:]), waits=[tx], inc=sV)
            tsq = E("scalar", lambda e: e.activation(out=sqA[:], in_=xA[:], func=AF.Square),
                    waits=[tx], inc=sA)
            t_sq_done = tsq
            for k in range(8):
                E("tensor", (lambda k_: lambda e: e.matmul(
                    prow[:, 0:CA], C["c1k_bf"][:], xbfA[:, k_ * CA:(k_ + 1) * CA],
                    start=(k_ == 0), stop=(k_ == 7)))(k),
                    waits=[tcast, pl.tok(dG), bank_free.get("prow")] if k == 0 else ())
            for k in range(8):
                E("tensor", (lambda k_: lambda e: e.matmul(
                    prow[:, CA:2 * CA], C["c1k_bf"][:], sqA[:, k_ * CA:(k_ + 1) * CA],
                    start=(k_ == 0), stop=(k_ == 7)))(k), waits=[tsq] if k == 0 else ())
            tstat = pl.mark("tensor", sM)
            R = lambda i: rows[:, i * CA:(i + 1) * CA]
            E("vector", lambda e: e.tensor_copy(out=R(0), in_=prow[:, 0:CA]),
              waits=[tstat, t_rowact_prev])
            tevr = E("vector", lambda e: e.tensor_copy(out=R(1), in_=prow[:, CA:2 * CA]), inc=sV)
            bank_free["prow"] = tevr
            tv0 = E("vector", lambda e: e.tensor_tensor(out=R(2), in0=R(0), in1=R(0), op=ALU.mult), inc=sV)
            tv = E("vector", lambda e: e.tensor_tensor(out=R(2), in0=R(1), in1=R(2), op=ALU.subtract), waits=[tv0], inc=sV)
            E("scalar", lambda e: e.activation(out=R(3), in_=R(2), func=AF.Ln, bias=C["eps5"][0:1, 0:1]), waits=[tv])
            ta = E("scalar", lambda e: e.activation(out=R(3), in_=R(3), func=AF.Exp, scale=-0.5), inc=sA)
            tv0 = E("vector", lambda e: e.tensor_tensor(out=R(5), in0=R(3), in1=R(3), op=ALU.mult), waits=[ta], inc=sV)
            tv = E("vector", lambda e: e.tensor_tensor(out=R(5), in0=R(2), in1=R(5), op=ALU.mult), waits=[tv0], inc=sV)
            E("scalar", lambda e: e.activation(out=R(5), in_=R(5), func=AF.Ln, bias=C["eps6"][0:1, 0:1]), waits=[tv])
            tb = E("scalar", lambda e: e.activation(out=R(5), in_=R(5), func=AF.Exp, scale=-0.5), inc=sA)
            E("vector", lambda e: e.tensor_tensor(out=R(4), in0=R(3), in1=R(5), op=ALU.mult), waits=[tb])
            t_rowact_prev = tb
            RB = lambda i: rows_bf[:, i * CA:(i + 1) * CA]
            E("vector", lambda e: e.tensor_copy(out=RB(0), in_=R(0)))
            E("vector", lambda e: e.tensor_copy(out=RB(1), in_=R(3)))
            trb = E("vector", lambda e: e.tensor_copy(out=RB(2), in_=R(4)), inc=sV)
            tm1 = E("tensor", lambda e: e.matmul(prep[:, 0:CA], C["ones_bf"][:], RB(0),
                                                 start=True, stop=True),
                    waits=[trb, bank_free.get("prep")], inc=sM)
            tev1 = E("vector", lambda e: e.tensor_copy(out=murep[:], in_=prep[:, 0:CA]), waits=[tm1], inc=sV)
            tm2 = E("tensor", lambda e: e.matmul(prep[:, 0:CA], C["ones_bf"][:], RB(1),
                                                 start=True, stop=True), waits=[tev1], inc=sM)
            tev2 = E("vector", lambda e: e.tensor_copy(out=arep[:], in_=prep[:, 0:CA]), waits=[tm2], inc=sV)
            tm3 = E("tensor", lambda e: e.matmul(prep[:, 0:CA], C["ones_bf"][:], RB(2),
                                                 start=True, stop=True), waits=[tev2], inc=sM)
            tev3 = E("vector", lambda e: e.tensor_copy(out=abrep[:], in_=prep[:, 0:CA]), waits=[tm3], inc=sV)
            bank_free["prep"] = tev3
            for k in range(8):
                kc = slice(k * CA, (k + 1) * CA)
                E("vector", (lambda kc_: lambda e: e.tensor_tensor(
                    out=tmpA[:, kc_], in0=xA[:, kc_], in1=murep[:], op=ALU.subtract))(kc),
                    waits=[t_mm_done] if k == 0 else ())
                E("vector", (lambda kc_: lambda e: e.tensor_tensor(
                    out=res1A[:, kc_], in0=tmpA[:, kc_], in1=arep[:], op=ALU.mult))(kc))
                E("vector", (lambda kc_: lambda e: e.tensor_tensor(
                    out=tmpA[:, kc_], in0=tmpA[:, kc_], in1=abrep[:], op=ALU.mult))(kc))
                E("vector", (lambda kc_, k_: lambda e: e.tensor_scalar(
                    out=xrA[:, kc_], in0=tmpA[:, kc_], scalar1=C["ssmw"][:, k_:k_ + 1],
                    scalar2=None, op0=ALU.mult))(kc, k))
            t_xr = pl.mark("vector", sV)
            t_ln_done = t_xr
            t_res_st = E("sync", (lambda cl: lambda e: e.dma_start(
                out=_dr3(s_res1, cl), in_=_fm3(res1A, 8)))(col),
                waits=[t_xr], inc=dW, dma=True)
            if c > 0:
                E("vector", lambda e: e.tensor_copy(
                    out=xbA.rearrange("p (j q) -> p j q", j=16)[:, :, 0:3],
                    in_=xbA.rearrange("p (j q) -> p j q", j=16)[:, :, CA:CA + 3]),
                    waits=[t_conv_done])
            t_halo = pl.mark("vector", sV)
            for m in range(32):
                bank = pm[m % 3]
                for k in range(8):
                    E("tensor", (lambda m_, k_, bank_: lambda e: e.matmul(
                        bank_[:, 0:CA],
                        inw_bf[:, k_ * 4096 + m_ * 128: k_ * 4096 + m_ * 128 + 128],
                        xrA[:, k_ * CA:(k_ + 1) * CA],
                        start=(k_ == 0), stop=(k_ == 7)))(m, k, bank),
                        waits=[t_xr, t_inw, t_halo, bank_free.get(m % 3)] if k == 0 else ())
                tmm = pl.mark("tensor", sM)
                if m < 16:
                    tev = E("vector", (lambda m_, bank_: lambda e: e.tensor_copy(
                        out=xbA.rearrange("p (j q) -> p j q", j=16)[:, m_, 3:3 + CA],
                        in_=bank_[:, 0:CA]))(m, bank), waits=[tmm], inc=sV)
                else:
                    tev = E("scalar", (lambda m_, bank_: lambda e: e.activation(
                        out=zsA[:, (m_ - 16) * CA:(m_ - 15) * CA], in_=bank_[:, 0:CA],
                        func=AF.Silu))(m, bank), waits=[tmm], inc=sA)
                bank_free[m % 3] = tev
            t_mm_done = pl.tok(sM)
            t_xb = pl.tok(sV)
            t_zs = pl.tok(sA)
            t_zs_st = E("sync", (lambda cl: lambda e: e.dma_start(
                out=_dr3(s_zs, cl), in_=_fm3(zsA, 16)))(col),
                waits=[t_zs], inc=dW, dma=True)
            t_silu_prev = None
            for j in range(16):
                xbj = xbA.rearrange("p (j q) -> p j q", j=16)[:, j, :]
                E("vector", (lambda j_, xbj_: lambda e: e.tensor_scalar(
                    out=ctmp[:, 0:CA], in0=xbj_[:, 0:CA],
                    scalar1=C["convw"][:, j_ * 4:j_ * 4 + 1], scalar2=None, op0=ALU.mult))(j, xbj),
                    waits=[t_xb] if j == 0 else ())
                for kk in range(1, 4):
                    E("vector", (lambda j_, kk_, xbj_: lambda e: e.scalar_tensor_tensor(
                        out=ctmp[:, 0:CA], in0=xbj_[:, kk_:kk_ + CA],
                        scalar=C["convw"][:, j_ * 4 + kk_:j_ * 4 + kk_ + 1],
                        in1=ctmp[:, 0:CA], op0=ALU.mult, op1=ALU.add))(j, kk, xbj))
                tc2 = E("vector", lambda e: e.tensor_copy(out=ctmp[:, CA:2 * CA], in_=ctmp[:, 0:CA]),
                        waits=[t_silu_prev], inc=sV)
                t_silu_prev = E("scalar", (lambda j_: lambda e: e.activation(
                    out=xcA[:, j_ * CA:(j_ + 1) * CA], in_=ctmp[:, CA:2 * CA], func=AF.Silu,
                    bias=C["convb"][:, j_:j_ + 1]))(j), waits=[tc2], inc=sA)
            t_conv_done = pl.mark("vector", sV)
            t_xc = pl.tok(sA)
            t_xc_st = E("sync", (lambda cl: lambda e: e.dma_start(
                out=_dr3(s_xc, cl), in_=_fm3(xcA, 16)))(col),
                waits=[t_xc], inc=dW, dma=True)
        pl.barrier(bar)

    with ExitStack() as phB:
        bsb = lambda *a: phB.enter_context(nc.sbuf_tensor(*a))
        outw_bf = bsb("outw_bf", [128, 16 * D], BF16)
        lifi_bf = bsb("lifi_bf", [128, 8 * D], BF16)
        lifo_bf = bsb("lifo_bf", [128, 8 * D], BF16)
        xprj_bf = bsb("xprj_bf", [128, 16 * 33], BF16)
        xcB = bsb("xcB", [128, 16 * CB], BF16)
        zsB = bsb("zsB", [128, 16 * CB], BF16)
        res1B = bsb("res1B", [128, 8 * CB], BF16)
        xdblB = bsb("xdblB", [33, CB], F32)
        dt0rep = bsb("dt0rep", [128, CB], F32)
        bcrep = bsb("bcrep", [128, 32 * CB], BF16)
        dtB = bsb("dtB", [128, 16 * CB], F32)
        wB = bsb("wB", [128, CB], BF16)
        bufA = bsb("bufA", [128, 16 * (CB + 1)], F32)
        bufB = bsb("bufB", [128, 16 * (CB + 1)], F32)
        bufC = bsb("bufC", [128, 16 * (CB + 1)], F32)
        hstate = bsb("hstate", [128, 256], F32)
        yj = bsb("yj", [128, CB], F32)
        y2B = bsb("y2B", [128, 16 * CB], BF16)
        x1B = bsb("x1B", [128, 8 * CB], F32)
        rowsB = bsb("rowsB", [1, 2 * CB], F32)
        rowsBbf = bsb("rowsBbf", [1, CB], BF16)
        brepB = bsb("brepB", [128, CB], F32)
        xr1B = bsb("xr1B", [128, 8 * CB], BF16)
        mstate = bsb("mstate", [128, 8], F32)
        cBuf = bufC
        x2B = bsb("x2B", [128, 8 * CB], F32)
        spkB = bufB.bitcast(BF16)[:, 0:8 * CB]

        E("gpsimd", lambda e: e.dma_start(out=_fm3(outw_bf, 16), in_=_r128(out_w)), inc=dG, dma=True)
        E("gpsimd", lambda e: e.dma_start(out=_fm3(lifi_bf, 8), in_=_r128(lif_in_w)), inc=dG, dma=True)
        E("gpsimd", lambda e: e.dma_start(out=_fm3(lifo_bf, 8), in_=_r128(lif_out_w)), inc=dG, dma=True)
        E("gpsimd", lambda e: e.dma_start(out=_fm3(xprj_bf, 16), in_=_r128(xproj_w)), inc=dG, dma=True)
        t_wB = pl.tok(dG)
        E("vector", lambda e: e.memset(hstate[:], 0.0))
        E("vector", lambda e: e.memset(mstate[:], 0.0))
        E("vector", lambda e: e.memset(
            bufA.rearrange("p (n q) -> p n q", n=16)[:, :, 0:1], 0.0))
        t_init = pl.mark("vector", sV)

        t_v_done = None
        t_rowactB = None
        t_xp_prev = None
        t_red_prev = None
        t_x1_prev = None
        t_dt_prev = None
        t_bc_used = None
        t_c_done = None
        t_x2_st = None
        t_spk_used = None
        t_xr1_used = None
        bfree = {}

        for c in range(ncb):
            col = slice(c * CB, (c + 1) * CB)
            tld1 = E("sync", (lambda cl: lambda e: e.dma_start(
                out=_fm3(xcB, 16), in_=_dr3(s_xc, cl)))(col),
                waits=[t_v_done, t_xp_prev] + ([pl.tok(dW), pl.tok(dG)] if c == 0 else []),
                inc=dW, dma=True)
            tld2 = E("sync", (lambda cl: lambda e: e.dma_start(
                out=_fm3(zsB, 16), in_=_dr3(s_zs, cl)))(col),
                inc=dW, dma=True)
            tld3 = E("sync", (lambda cl: lambda e: e.dma_start(
                out=_fm3(res1B, 8), in_=_dr3(s_res1, cl)))(col),
                waits=[t_x1_prev], inc=dW, dma=True)
            for k in range(16):
                E("tensor", (lambda k_: lambda e: e.matmul(
                    px33[:, 0:CB], xprj_bf[:, k_ * 33:(k_ + 1) * 33],
                    xcB[:, k_ * CB:(k_ + 1) * CB], start=(k_ == 0), stop=(k_ == 15)))(k),
                    waits=[tld1, t_wB, bfree.get("px")] if k == 0 else ())
            t_xp = pl.mark("tensor", sM)
            t_xp_prev = t_xp
            tevx = E("vector", lambda e: e.tensor_copy(out=xdblB[:], in_=px33[:, 0:CB]),
                     waits=[t_xp], inc=sV)
            bfree["px"] = tevx
            tw1 = E("sync", lambda e: e.dma_start(out=s_xdbl[:, 0:CB], in_=xdblB[:]),
                    waits=[tevx], inc=dW, dma=True)
            tw2 = E("gpsimd", lambda e: e.dma_start(out=s_xdbl_bf[:, 0:CB], in_=xdblB[1:33, :]),
                    waits=[tevx, t_bc_used], inc=dG, dma=True)
            tb1 = E("sync", lambda e: e.dma_start(
                out=dt0rep[:],
                in_=s_xdbl.rearrange("r q -> (r q)")[0:CB][None, :].partition_broadcast(128)),
                waits=[tw1, t_dt_prev], inc=dW, dma=True)
            tb2 = E("sync", lambda e: e.dma_start(
                out=bcrep[:],
                in_=s_xdbl_bf.rearrange("r q -> (r q)")[None, :].partition_broadcast(128)),
                waits=[tw2], inc=dW, dma=True)
            for j in range(16):
                E("vector", (lambda j_: lambda e: e.tensor_scalar(
                    out=dtB[:, j_ * CB:(j_ + 1) * CB], in0=dt0rep[:],
                    scalar1=C["dtw"][:, j_:j_ + 1], scalar2=C["dtb"][:, j_:j_ + 1],
                    op0=ALU.mult, op1=ALU.add))(j), waits=[tb1] if j == 0 else ())
            tpre = pl.mark("vector", sV)
            E("vector", lambda e: e.tensor_scalar(out=bufC[:, 0:16 * CB], in0=dtB[:], scalar1=0.0,
                                                  scalar2=None, op0=ALU.max))
            E("scalar", lambda e: e.activation(out=dtB[:], in_=dtB[:], func=AF.Abs), waits=[tpre])
            E("scalar", lambda e: e.activation(out=dtB[:], in_=dtB[:], func=AF.Exp, scale=-1.0))
            tsp = E("scalar", lambda e: e.activation(out=dtB[:], in_=dtB[:], func=AF.Ln, bias=1.0), inc=sA)
            tdt = E("vector", lambda e: e.tensor_tensor(out=dtB[:], in0=dtB[:], in1=bufC[:, 0:16 * CB],
                                                        op=ALU.add), waits=[tsp], inc=sV)
            t_dt_prev = tdt
            slotted = lambda buf: buf.rearrange("p (n q) -> p n q", n=16)
            for j in range(16):
                jc = slice(j * CB, (j + 1) * CB)
                for n in range(16):
                    E("vector", (lambda j_, n_: lambda e: e.tensor_scalar(
                        out=slotted(bufA)[:, n_, 1:CB + 1], in0=dtB[:, j_ * CB:(j_ + 1) * CB],
                        scalar1=C["A_sb"][:, j_ * 16 + n_:j_ * 16 + n_ + 1],
                        scalar2=None, op0=ALU.mult))(j, n),
                        waits=[tdt, bfree.get("bufA")] if n == 0 else ())
                ta1 = pl.mark("vector", sV)
                tda = E("scalar", lambda e: e.activation(out=bufB[:], in_=bufA[:], func=AF.Exp),
                        waits=[ta1, t_red_prev] + ([t_spk_used] if j == 0 else []), inc=sA)
                E("vector", lambda e: e.memset(slotted(bufB)[:, :, 0:1], 0.0), waits=[tda])
                E("vector", (lambda jc_: lambda e: e.tensor_tensor(
                    out=wB[:], in0=dtB[:, jc_], in1=xcB[:, jc_], op=ALU.mult))(jc))
                for n in range(16):
                    E("vector", (lambda n_: lambda e: e.tensor_tensor(
                        out=slotted(bufA)[:, n_, 1:CB + 1],
                        in0=bcrep[:, n_ * CB:(n_ + 1) * CB], in1=wB[:], op=ALU.mult))(n),
                        waits=[tb2] if n == 0 else ())
                t_h0 = E("vector", (lambda j_: lambda e: e.tensor_copy(
                    out=slotted(bufA)[:, :, 0], in_=hstate[:, j_ * 16:(j_ + 1) * 16]))(j),
                    waits=[t_init] if (c == 0 and j == 0) else (), inc=sV)
                E("vector", lambda e: e.tensor_tensor_scan(
                    out=bufC[:], data0=bufB[:], data1=bufA[:], initial=0.0,
                    op0=ALU.mult, op1=ALU.add), waits=[t_h0])
                bfree["bufA"] = pl.mark("vector", sV)
                E("vector", (lambda j_: lambda e: e.tensor_copy(
                    out=hstate[:, j_ * 16:(j_ + 1) * 16], in_=slotted(bufC)[:, :, CB]))(j),
                    waits=[bfree["bufA"]])
                for n in range(16):
                    E("vector", (lambda n_: lambda e: e.tensor_tensor(
                        out=bufB[:, n_ * CB:(n_ + 1) * CB], in0=slotted(bufC)[:, n_, 1:CB + 1],
                        in1=bcrep[:, (16 + n_) * CB:(17 + n_) * CB], op=ALU.mult))(n))
                tym = pl.mark("vector", sV)
                trd = E("vector", lambda e: e.reduce_sum(
                    out=yj[:], in_=bufB[:, 0:16 * CB].rearrange("p (n q) -> p q n", n=16),
                    axis=AX), waits=[tym], inc=sV)
                t_red_prev = trd
                t_yd = E("vector", (lambda j_, jc_: lambda e: e.scalar_tensor_tensor(
                    out=yj[:], in0=xcB[:, jc_], scalar=C["dskip"][:, j_:j_ + 1], in1=yj[:],
                    op0=ALU.mult, op1=ALU.add))(j, jc), waits=[trd], inc=sV)
                E("vector", (lambda jc_: lambda e: e.tensor_tensor(
                    out=y2B[:, jc_], in0=yj[:], in1=zsB[:, jc_], op=ALU.mult))(jc),
                    waits=[tld2, t_yd])
            t_y2 = pl.mark("vector", sV)
            for m in range(8):
                bank = pm[m % 2]
                for k in range(16):
                    E("tensor", (lambda m_, k_, bank_: lambda e: e.matmul(
                        bank_[:, 0:CB], outw_bf[:, k_ * D + m_ * 128: k_ * D + m_ * 128 + 128],
                        y2B[:, k_ * CB:(k_ + 1) * CB], start=(k_ == 0), stop=(k_ == 15)))(m, k, bank),
                        waits=[t_y2, bfree.get(m % 2)] if k == 0 else ())
                tmm = pl.mark("tensor", sM)
                tev = E("vector", (lambda m_, bank_: lambda e: e.tensor_tensor(
                    out=x1B[:, m_ * CB:(m_ + 1) * CB], in0=bank_[:, 0:CB],
                    in1=res1B[:, m_ * CB:(m_ + 1) * CB], op=ALU.add))(m, bank),
                    waits=[tmm, tld3], inc=sV)
                bfree[m % 2] = tev
            t_x1 = pl.tok(sV)
            t_x1_prev = t_x1
            sqB = y2B
            tsq = E("scalar", lambda e: e.activation(out=sqB[:, 0:8 * CB], in_=x1B[:], func=AF.Square),
                    waits=[t_x1], inc=sA)
            for k in range(8):
                E("tensor", (lambda k_: lambda e: e.matmul(
                    prow[:, 0:CB], C["c1k_bf"][:], sqB[:, k_ * CB:(k_ + 1) * CB],
                    start=(k_ == 0), stop=(k_ == 7)))(k),
                    waits=[tsq, bfree.get("prow")] if k == 0 else ())
            tms = pl.mark("tensor", sM)
            RB2 = lambda i: rowsB[:, i * CB:(i + 1) * CB]
            tev = E("vector", lambda e: e.tensor_copy(out=RB2(0), in_=prow[:, 0:CB]),
                    waits=[tms, t_rowactB], inc=sV)
            bfree["prow"] = tev
            E("scalar", lambda e: e.activation(out=RB2(1), in_=RB2(0), func=AF.Ln, bias=C["eps6"][0:1, 0:1]),
              waits=[tev])
            tr2 = E("scalar", lambda e: e.activation(out=RB2(1), in_=RB2(1), func=AF.Exp, scale=-0.5), inc=sA)
            t_rowactB = tr2
            trb = E("vector", lambda e: e.tensor_copy(out=rowsBbf[:], in_=RB2(1)), waits=[tr2], inc=sV)
            tm1 = E("tensor", lambda e: e.matmul(prep[:, 0:CB], C["ones_bf"][:], rowsBbf[:],
                                                 start=True, stop=True),
                    waits=[trb, bfree.get("prep")], inc=sM)
            tbr = E("vector", lambda e: e.tensor_copy(out=brepB[:], in_=prep[:, 0:CB]),
                    waits=[tm1], inc=sV)
            bfree["prep"] = tbr
            for k in range(8):
                kc = slice(k * CB, (k + 1) * CB)
                E("vector", (lambda kc_: lambda e: e.tensor_tensor(
                    out=x2B[:, kc_], in0=x1B[:, kc_], in1=brepB[:], op=ALU.mult))(kc),
                    waits=[tbr, t_xr1_used, t_x2_st] if k == 0 else ())
                E("vector", (lambda kc_, k_: lambda e: e.tensor_scalar(
                    out=xr1B[:, kc_], in0=x2B[:, kc_], scalar1=C["lifw"][:, k_:k_ + 1],
                    scalar2=None, op0=ALU.mult))(kc, k))
            t_xr1 = pl.mark("vector", sV)
            for m in range(8):
                bank = pm[m % 2]
                for k in range(8):
                    E("tensor", (lambda m_, k_, bank_: lambda e: e.matmul(
                        bank_[:, 0:CB], lifi_bf[:, k_ * D + m_ * 128: k_ * D + m_ * 128 + 128],
                        xr1B[:, k_ * CB:(k_ + 1) * CB], start=(k_ == 0), stop=(k_ == 7)))(m, k, bank),
                        waits=[t_xr1, bfree.get(m % 2), t_c_done] if k == 0 else ())
                tmm = pl.mark("tensor", sM)
                tev = E("vector", (lambda m_, bank_: lambda e: e.tensor_copy(
                    out=cBuf[:, m_ * CB:(m_ + 1) * CB], in_=bank_[:, 0:CB]))(m, bank),
                    waits=[tmm], inc=sV)
                bfree[m % 2] = tev
            t_xr1_used = pl.tok(sM)
            t_c = pl.tok(sV)
            cview = cBuf[:, 0:8 * CB].rearrange("p (m q) -> p q m", m=8)
            sview = spkB.rearrange("p (m q) -> p q m", m=8)
            for tt in range(CB):
                for gi in range(2):
                    sl = slice(gi * 4, (gi + 1) * 4)
                    E("vector", (lambda tt_, sl_: lambda e: e.scalar_tensor_tensor(
                        out=mstate[:, sl_], in0=mstate[:, sl_], scalar=0.9,
                        in1=cview[:, tt_, sl_], op0=ALU.mult, op1=ALU.add))(tt, sl),
                        waits=[t_c, t_spk_used] if (tt == 0 and gi == 0) else ())
                for gi in range(2):
                    sl = slice(gi * 4, (gi + 1) * 4)
                    E("vector", (lambda tt_, sl_: lambda e: e.tensor_scalar(
                        out=sview[:, tt_, sl_], in0=mstate[:, sl_], scalar1=1.0,
                        scalar2=None, op0=ALU.is_ge))(tt, sl))
                for gi in range(2):
                    sl = slice(gi * 4, (gi + 1) * 4)
                    E("vector", (lambda tt_, sl_: lambda e: e.tensor_tensor(
                        out=mstate[:, sl_], in0=mstate[:, sl_],
                        in1=sview[:, tt_, sl_], op=ALU.subtract))(tt, sl))
            t_spk = pl.mark("vector", sV)
            t_c_done = t_spk
            for m in range(8):
                bank = pm[m % 2]
                for k in range(8):
                    E("tensor", (lambda m_, k_, bank_: lambda e: e.matmul(
                        bank_[:, 0:CB], lifo_bf[:, k_ * D + m_ * 128: k_ * D + m_ * 128 + 128],
                        spkB[:, k_ * CB:(k_ + 1) * CB], start=(k_ == 0), stop=(k_ == 7)))(m, k, bank),
                        waits=[t_spk, bfree.get(m % 2)] if k == 0 else ())
                tmm = pl.mark("tensor", sM)
                tev = E("vector", (lambda m_, bank_: lambda e: e.tensor_tensor(
                    out=x2B[:, m_ * CB:(m_ + 1) * CB], in0=bank_[:, 0:CB],
                    in1=x1B[:, m_ * CB:(m_ + 1) * CB], op=ALU.add))(m, bank),
                    waits=[tmm], inc=sV)
                bfree[m % 2] = tev
            t_spk_used = pl.tok(sM)
            t_x2 = pl.tok(sV)
            t_v_done = t_x2
            t_x2_st = E("sync", (lambda cl: lambda e: e.dma_start(
                out=_dr3(s_x2, cl), in_=_fm3(x2B, 8)))(col),
                waits=[t_x2], inc=dW, dma=True)
            if DEV_DEBUG:
                E("sync", (lambda cl: lambda e: e.dma_start(
                    out=_dr3(s_dbg1, cl), in_=_fm3(x1B, 8)))(col), waits=[t_x2], inc=dW, dma=True)
                E("sync", (lambda cl: lambda e: e.dma_start(
                    out=_dr3(s_dbg2, cl), in_=_fm3(spkB, 8)))(col), waits=[t_x2], inc=dW, dma=True)
        pl.barrier(bar)

    def moe_ffn_phase(is_ffn, half, npass):
        nf = DFF if is_ffn else DE
        nfs = nf // 512
        thp = th // npass
        nttp = thp // 128
        nccp = max(thp // cc, 1)
        q0 = half * nccp
        pref = ("d_" if is_ffn else "c_") + str(half) + "_"
        with ExitStack() as ph:
            csb = lambda nm, *a: ph.enter_context(nc.sbuf_tensor(pref + nm, *a))
            xn = csb("xn", [128, 8 * thp], BF16)
            acc = csb("acc", [128, nttp * D], F32)
            wtm = csb("wtm", [128, nttp * 16], F32)
            xw = csb("xw", [128, 8 * cc], F32)
            xwb = csb("xwb", [128, 8 * cc], BF16)
            sqC = csb("sqC", [128, 8 * cc], BF16)
            rowsC = csb("rowsC", [1, 4 * cc], F32)
            rowsCbf = csb("rowsCbf", [1, 2 * cc], BF16)
            repC = csb("repC", [128, 2 * cc], F32)
            wsl = [csb(f"wsl{i}", [128, 8 * 512], BF16) for i in range(2)]
            usl = [csb(f"usl{i}", [128, 8 * 512], BF16) for i in range(2)]
            dsl = [csb(f"dsl{i}", [128, 4 * 1024], BF16) for i in range(2)]
            hb = csb("hb", [128, 4 * 512], BF16)
            gsb = csb("gsb", [128, 4 * 512], BF16)
            usb = csb("usb", [128, 4 * 512], BF16)
            smt = csb("smt", [128, 64], F32)
            wrow_bf = csb("wrow_bf", [8, 128], BF16)
            outc = csb("outc", [128, 8 * cc], F32)

            tE = {}
            t_src_used = None
            t_q0 = [pl.tok(dW), pl.tok(dG)]
            for qq in range(nccp):
                q = q0 + qq
                qa = slice(q * cc, (q + 1) * cc)
                if is_ffn:
                    tld = E("sync", (lambda qa_: lambda e: e.dma_start(
                        out=_fm3(xw, 8), in_=_dr3(s_x4, qa_)))(qa),
                        waits=[t_src_used] + (t_q0 if q == 0 else []), inc=dW, dma=True)
                else:
                    base = _dr3(s_x2, slice(0, cc))
                    tld = E("gpsimd", (lambda base_, q_: lambda e: e.dma_start(
                        out=_fm3(xw, 8), in_=bass.AP(base_.tensor, regholder[f"r{q_}"], base_.ap)))(base, q),
                        waits=[t_hoff, t_src_used] + (t_q0 if q == 0 else []), inc=dG, dma=True)
                tsq = E("scalar", lambda e: e.activation(out=sqC[:], in_=xw[:], func=AF.Square),
                        waits=[tld], inc=sA)
                tcst = E("vector", lambda e: e.tensor_copy(out=xwb[:], in_=xw[:]),
                         waits=[tld], inc=sV)
                for k in range(8):
                    E("tensor", (lambda k_: lambda e: e.matmul(
                        prow[:, 0:cc], C["c1k_bf"][:], sqC[:, k_ * cc:(k_ + 1) * cc],
                        start=(k_ == 0), stop=(k_ == 7)))(k),
                        waits=[tsq, tE.get("prow")] if k == 0 else ())
                if is_ffn:
                    for k in range(8):
                        E("tensor", (lambda k_: lambda e: e.matmul(
                            prow[:, cc:2 * cc], C["c1k_bf"][:], xwb[:, k_ * cc:(k_ + 1) * cc],
                            start=(k_ == 0), stop=(k_ == 7)))(k), waits=[tcst] if k == 0 else ())
                tms = pl.mark("tensor", sM)
                RC = lambda i: rowsC[:, i * cc:(i + 1) * cc]
                tev = E("vector", lambda e: e.tensor_copy(out=RC(0), in_=prow[:, 0:cc]),
                        waits=[tms, tE.get("rowact")], inc=sV)
                if is_ffn:
                    tev = E("vector", lambda e: e.tensor_copy(out=RC(1), in_=prow[:, cc:2 * cc]), inc=sV)
                tE["prow"] = tev
                if is_ffn:
                    tv0 = E("vector", lambda e: e.tensor_tensor(out=RC(2), in0=RC(1), in1=RC(1), op=ALU.mult), waits=[tev], inc=sV)
                    tv = E("vector", lambda e: e.tensor_tensor(out=RC(2), in0=RC(0), in1=RC(2), op=ALU.subtract), waits=[tv0], inc=sV)
                    E("scalar", lambda e: e.activation(out=RC(3), in_=RC(2), func=AF.Ln, bias=C["eps5"][0:1, 0:1]), waits=[tv])
                    ta = E("scalar", lambda e: e.activation(out=RC(3), in_=RC(3), func=AF.Exp, scale=-0.5), inc=sA)
                    tE["rowact"] = ta
                    E("vector", lambda e: e.tensor_copy(out=rowsCbf[:, 0:cc], in_=RC(1)), waits=[ta])
                    trb = E("vector", lambda e: e.tensor_copy(out=rowsCbf[:, cc:2 * cc], in_=RC(3)), inc=sV)
                    tm1 = E("tensor", lambda e: e.matmul(prep[:, 0:cc], C["ones_bf"][:],
                                                         rowsCbf[:, 0:cc], start=True, stop=True),
                            waits=[trb, tE.get("prep")], inc=sM)
                    te1 = E("vector", lambda e: e.tensor_copy(out=repC[:, 0:cc], in_=prep[:, 0:cc]),
                            waits=[tm1], inc=sV)
                    tm2 = E("tensor", lambda e: e.matmul(prep[:, 0:cc], C["ones_bf"][:],
                                                         rowsCbf[:, cc:2 * cc], start=True, stop=True),
                            waits=[te1], inc=sM)
                    te2 = E("vector", lambda e: e.tensor_copy(out=repC[:, cc:2 * cc], in_=prep[:, 0:cc]),
                            waits=[tm2], inc=sV)
                    tE["prep"] = te2
                    for k in range(8):
                        kc = slice(k * cc, (k + 1) * cc)
                        xnc = slice(k * thp + qq * cc, k * thp + (qq + 1) * cc)
                        E("vector", (lambda kc_: lambda e: e.tensor_tensor(
                            out=xw[:, kc_], in0=xw[:, kc_], in1=repC[:, 0:cc], op=ALU.subtract))(kc),
                            waits=[te2] if k == 0 else ())
                        E("vector", (lambda kc_, xnc_: lambda e: e.tensor_tensor(
                            out=xn[:, xnc_], in0=xw[:, kc_], in1=repC[:, cc:2 * cc], op=ALU.mult))(kc, xnc))
                else:
                    E("scalar", lambda e: e.activation(out=RC(1), in_=RC(0), func=AF.Ln, bias=C["eps6"][0:1, 0:1]),
                      waits=[tev])
                    tr2 = E("scalar", lambda e: e.activation(out=RC(1), in_=RC(1), func=AF.Exp, scale=-0.5), inc=sA)
                    tE["rowact"] = tr2
                    trb = E("vector", lambda e: e.tensor_copy(out=rowsCbf[:, 0:cc], in_=RC(1)),
                            waits=[tr2], inc=sV)
                    tm1 = E("tensor", lambda e: e.matmul(prep[:, 0:cc], C["ones_bf"][:],
                                                         rowsCbf[:, 0:cc], start=True, stop=True),
                            waits=[trb, tE.get("prep")], inc=sM)
                    te1 = E("vector", lambda e: e.tensor_copy(out=repC[:, 0:cc], in_=prep[:, 0:cc]),
                            waits=[tm1], inc=sV)
                    tE["prep"] = te1
                    for k in range(8):
                        kc = slice(k * cc, (k + 1) * cc)
                        xnc = slice(k * thp + qq * cc, k * thp + (qq + 1) * cc)
                        E("vector", (lambda kc_: lambda e: e.tensor_tensor(
                            out=xw[:, kc_], in0=xw[:, kc_], in1=repC[:, 0:cc], op=ALU.mult))(kc),
                            waits=[te1] if k == 0 else ())
                        E("vector", (lambda kc_, xnc_, k_: lambda e: e.tensor_scalar(
                            out=xn[:, xnc_], in0=xw[:, kc_], scalar1=C["moew"][:, k_:k_ + 1],
                            scalar2=None, op0=ALU.mult))(kc, xnc, k))
                t_src_used = pl.mark("vector", sV)
            t_xn = t_src_used
            gmat = C["fgat_bf"] if is_ffn else C["rout_bf"]
            for ti in range(nttp):
                for k in range(8):
                    E("tensor", (lambda ti_, k_: lambda e: e.matmul(
                        pt[:, 0:NE], xn[:, k_ * thp + ti_ * 128: k_ * thp + ti_ * 128 + 128],
                        gmat[:, k_ * NE:(k_ + 1) * NE], start=(k_ == 0), stop=(k_ == 7)))(ti, k),
                        waits=[t_xn, tE.get("pt")] if k == 0 else ())
                tmg = pl.mark("tensor", sM)
                S = lambda i: smt[:, i * 8:(i + 1) * 8]
                tev = E("vector", lambda e: e.tensor_copy(out=S(0), in_=pt[:, 0:NE]),
                        waits=[tmg], inc=sV)
                tE["pt"] = tev
                if is_ffn:
                    tev = E("vector", lambda e: e.tensor_tensor(out=S(0), in0=S(0), in1=C["gbr"][:],
                                                                op=ALU.add), waits=[tev], inc=sV)
                tmx = E("vector", lambda e: e.reduce_max(out=smt[:, 56:57], in_=S(0), axis=AX),
                        waits=[tev], inc=sV)
                tsb = E("vector", lambda e: e.tensor_tensor(
                    out=S(1), in0=S(0), in1=smt[:, 56:57].broadcast_to((128, 8)), op=ALU.subtract),
                    waits=[tmx], inc=sV)
                tex = E("scalar", lambda e: e.activation(out=S(2), in_=S(1), func=AF.Exp),
                        waits=[tsb], inc=sA)
                tsm = E("vector", lambda e: e.reduce_sum(out=smt[:, 57:58], in_=S(2), axis=AX),
                        waits=[tex], inc=sV)
                trc = E("vector", lambda e: e.reciprocal(out=smt[:, 58:59], in_=smt[:, 57:58]),
                        waits=[tsm], inc=sV)
                tpb = E("vector", lambda e: e.tensor_tensor(
                    out=S(3), in0=S(2), in1=smt[:, 58:59].broadcast_to((128, 8)), op=ALU.mult),
                    waits=[trc], inc=sV)
                if is_ffn:
                    tm2_ = E("vector", lambda e: e.reduce_max(out=smt[:, 59:60], in_=S(3), axis=AX),
                             waits=[tpb], inc=sV)
                    tmk = E("vector", lambda e: e.tensor_tensor(
                        out=S(4), in0=S(3), in1=smt[:, 59:60].broadcast_to((128, 8)), op=ALU.is_ge),
                        waits=[tm2_], inc=sV)
                else:
                    tm1_ = E("vector", lambda e: e.reduce_max(out=smt[:, 59:60], in_=S(3), axis=AX),
                             waits=[tpb], inc=sV)
                    tlt = E("vector", lambda e: e.tensor_tensor(
                        out=S(5), in0=S(3), in1=smt[:, 59:60].broadcast_to((128, 8)), op=ALU.is_lt),
                        waits=[tm1_], inc=sV)
                    tz = E("vector", lambda e: e.tensor_tensor(out=S(5), in0=S(3), in1=S(5),
                                                               op=ALU.mult), waits=[tlt], inc=sV)
                    tm2_ = E("vector", lambda e: e.reduce_max(out=smt[:, 60:61], in_=S(5), axis=AX),
                             waits=[tz], inc=sV)
                    tmk = E("vector", lambda e: e.tensor_tensor(
                        out=S(4), in0=S(3), in1=smt[:, 60:61].broadcast_to((128, 8)), op=ALU.is_ge),
                        waits=[tm2_], inc=sV)
                E("vector", (lambda ti_: lambda e: e.tensor_tensor(
                    out=wtm[:, ti_ * 16:ti_ * 16 + 8], in0=S(3), in1=S(4), op=ALU.mult))(ti),
                    waits=[tmk])
            t_w = pl.mark("vector", sV)
            tE[("pd", 1)] = tE.get("prep")
            E("vector", lambda e: e.memset(acc[:], 0.0))
            t_acc0 = pl.mark("vector", sV)
            t_par_used = [None, None]
            for ee in range(NE):
                for fs in range(nfs):
                    par = (ee * nfs + fs) % 2
                    if is_ffn:
                        wsrc = w1[ee, :, fs * 512:(fs + 1) * 512]
                        dsrc = w2[ee, fs * 512:(fs + 1) * 512, :]
                    else:
                        wsrc = gate_w[ee, :, fs * 512:(fs + 1) * 512]
                        dsrc = down_w[ee, fs * 512:(fs + 1) * 512, :]
                    tl1 = E("gpsimd", (lambda w_, p_: lambda e: e.dma_start(
                        out=_fm3(wsl[p_], 8), in_=_r128(w_)))(wsrc, par),
                        waits=[t_par_used[par]], inc=dG, dma=True)
                    if not is_ffn:
                        tl2 = E("gpsimd", (lambda w_, p_: lambda e: e.dma_start(
                            out=_fm3(usl[p_], 8), in_=_r128(w_)))(up_w[ee, :, fs * 512:(fs + 1) * 512], par),
                            inc=dG, dma=True)
                    tl3 = E("gpsimd", (lambda w_, p_: lambda e: e.dma_start(
                        out=_fm3(dsl[p_], 4), in_=_r128(w_)))(dsrc, par), inc=dG, dma=True)
                    t_wload = pl.tok(dG)
                    for qq in range(nccp):
                        q = q0 + qq
                        for mf in range(4):
                            hbk = mf % 2
                            bank = pm[hbk]
                            for k in range(8):
                                E("tensor", (lambda k_, mf_, q_, p_, bank_: lambda e: e.matmul(
                                    bank_[:, 0:cc],
                                    wsl[p_][:, k_ * 512 + mf_ * 128: k_ * 512 + mf_ * 128 + 128],
                                    xn[:, k_ * thp + q_ * cc: k_ * thp + (q_ + 1) * cc],
                                    start=(k_ == 0), stop=(k_ == 7)))(k, mf, qq, par, bank),
                                    waits=[t_wload, t_xn, t_acc0, tE.get(("pb", hbk))] if k == 0 else ())
                            tmmg = pl.mark("tensor", sM)
                            if is_ffn:
                                tg = E("scalar", (lambda mf_, bank_, e_, fs_: lambda e: e.activation(
                                    out=hb[:, mf_ * cc:(mf_ + 1) * cc], in_=bank_[:, 0:cc],
                                    func=AF.Gelu, bias=C["b1c"][:, e_ * 32 + fs_ * 4 + mf_:
                                                               e_ * 32 + fs_ * 4 + mf_ + 1]))(mf, bank, ee, fs),
                                    waits=[tmmg, tE.get("hbfree")], inc=sA)
                            else:
                                tg = E("scalar", (lambda mf_, bank_: lambda e: e.activation(
                                    out=gsb[:, mf_ * cc:(mf_ + 1) * cc], in_=bank_[:, 0:cc],
                                    func=AF.Silu))(mf, bank), waits=[tmmg], inc=sA)
                            tE[("pb", hbk)] = tg
                            tE["h"] = tg
                        if not is_ffn:
                            for mf in range(4):
                                ubk = mf % 2
                                bank = pm[ubk]
                                for k in range(8):
                                    E("tensor", (lambda k_, mf_, q_, p_, bank_: lambda e: e.matmul(
                                        bank_[:, 0:cc],
                                        usl[p_][:, k_ * 512 + mf_ * 128: k_ * 512 + mf_ * 128 + 128],
                                        xn[:, k_ * thp + q_ * cc: k_ * thp + (q_ + 1) * cc],
                                        start=(k_ == 0), stop=(k_ == 7)))(k, mf, qq, par, bank),
                                        waits=[tE.get(("pb", ubk))] if k == 0 else ())
                                tmmg = pl.mark("tensor", sM)
                                tu = E("vector", (lambda mf_, bank_: lambda e: e.tensor_copy(
                                    out=usb[:, mf_ * cc:(mf_ + 1) * cc], in_=bank_[:, 0:cc]))(mf, bank),
                                    waits=[tmmg], inc=sV)
                                tE[("pb", ubk)] = tu
                            tgs = pl.tok(sA)
                            for mf in range(4):
                                E("vector", (lambda mf_: lambda e: e.tensor_tensor(
                                    out=hb[:, mf_ * cc:(mf_ + 1) * cc],
                                    in0=gsb[:, mf_ * cc:(mf_ + 1) * cc],
                                    in1=usb[:, mf_ * cc:(mf_ + 1) * cc], op=ALU.mult))(mf),
                                    waits=[tgs, tE.get("hbfree")] if mf == 0 else ())
                            t_h = pl.mark("vector", sV)
                        else:
                            t_h = pl.tok(sA)
                        for tm in range(ntc):
                            for dh in range(2):
                                dbk = (tm * 2 + dh) % 2
                                bank = (pm[2], prep)[dbk]
                                for kf in range(4):
                                    E("tensor", (lambda kf_, tm_, dh_, p_, bank_: lambda e: e.matmul(
                                        bank_[:, 0:512],
                                        hb[:, kf_ * cc + tm_ * 128: kf_ * cc + tm_ * 128 + 128],
                                        dsl[p_][:, kf_ * 1024 + dh_ * 512: kf_ * 1024 + (dh_ + 1) * 512],
                                        start=(kf_ == 0), stop=(kf_ == 3)))(kf, tm, dh, par, bank),
                                        waits=[t_h, t_w, tE.get(("pd", dbk))] if kf == 0 else ())
                                tmmg = pl.mark("tensor", sM)
                                ti_g = qq * ntc + tm
                                td = E("vector", (lambda ti__, dh_, e_, bank_: lambda e: e.scalar_tensor_tensor(
                                    out=acc[:, ti__ * D + dh_ * 512: ti__ * D + (dh_ + 1) * 512],
                                    in0=bank_[:, 0:512],
                                    scalar=wtm[:, ti__ * 16 + e_:ti__ * 16 + e_ + 1],
                                    in1=acc[:, ti__ * D + dh_ * 512: ti__ * D + (dh_ + 1) * 512],
                                    op0=ALU.mult, op1=ALU.add))(ti_g, dh, ee, bank),
                                    waits=[tmmg], inc=sV)
                                tE[("pd", dbk)] = td
                                tE["hbfree"] = pl.tok(sM)
                    t_par_used[par] = pl.tok(sM)
            t_expert_done = pl.tok(sV)
            if is_ffn:
                for ti in range(nttp):
                    ttr = E("tensor", (lambda ti_: lambda e: e.transpose(
                        pt[0:8, 0:128], wtm[:, ti_ * 16:ti_ * 16 + 8], C["iden_f"][:]))(ti),
                        waits=[t_expert_done, tE.get("pt2")], inc=sM)
                    tevt = E("vector", lambda e: e.tensor_copy(out=wrow_bf[:], in_=pt[0:8, 0:128]),
                             waits=[ttr], inc=sV)
                    tE["pt2"] = tevt
                    for dh in range(2):
                        bank = pm[dh]
                        tmb = E("tensor", (lambda dh_, bank_: lambda e: e.matmul(
                            bank_[:, 0:512], wrow_bf[:],
                            C["b2_bf"][:, dh_ * 512:(dh_ + 1) * 512],
                            start=True, stop=True))(dh, bank),
                            waits=[tevt, tE.get(("b2", dh))], inc=sM)
                        tb2e = E("vector", (lambda ti_, dh_, bank_: lambda e: e.tensor_tensor(
                            out=acc[:, ti_ * D + dh_ * 512: ti_ * D + (dh_ + 1) * 512],
                            in0=acc[:, ti_ * D + dh_ * 512: ti_ * D + (dh_ + 1) * 512],
                            in1=bank_[:, 0:512], op=ALU.add))(ti, dh, bank),
                            waits=[tmb], inc=sV)
                        tE[("b2", dh)] = tb2e
            t_acc_done = pl.mark("vector", sV)
            t_out_used = None
            for qq in range(nccp):
                q = q0 + qq
                qa = slice(q * cc, (q + 1) * cc)
                if is_ffn:
                    tld = E("sync", (lambda qa_: lambda e: e.dma_start(
                        out=_fm3(xw, 8), in_=_dr3(s_x4, qa_)))(qa),
                        waits=[t_out_used, t_xn], inc=dW, dma=True)
                    tld2 = None
                else:
                    base = _dr3(s_x2, slice(0, cc))
                    tld = E("gpsimd", (lambda base_, q_: lambda e: e.dma_start(
                        out=_fm3(xw, 8), in_=bass.AP(base_.tensor, regholder[f"r{q_}"], base_.ap)))(base, q),
                        waits=[t_hoff, t_out_used, t_xn], inc=dG, dma=True)
                    base2 = _dr3(x_fm, slice(0, cc))
                    tld2 = E("gpsimd", (lambda base_, q_: lambda e: e.dma_start(
                        out=_fm3(outc, 8), in_=bass.AP(base_.tensor, regholder[f"r{q_}"], base_.ap)))(base2, q),
                        inc=dG, dma=True)
                for tm in range(ntc):
                    ti_g = qq * ntc + tm
                    for j in range(8):
                        ttr = E("tensor", (lambda ti__, j_: lambda e: e.transpose(
                            pt[:], acc[:, ti__ * D + j_ * 128: ti__ * D + (j_ + 1) * 128],
                            C["iden_f"][:]))(ti_g, j),
                            waits=[t_acc_done, tE.get("ptt")], inc=sM)
                        if is_ffn:
                            tvv = E("vector", (lambda j_, tm_: lambda e: e.tensor_tensor(
                                out=outc[:, j_ * cc + tm_ * 128: j_ * cc + tm_ * 128 + 128],
                                in0=pt[:], in1=xw[:, j_ * cc + tm_ * 128: j_ * cc + tm_ * 128 + 128],
                                op=ALU.add))(j, tm), waits=[ttr, tld], inc=sV)
                        else:
                            E("vector", (lambda j_, tm_: lambda e: e.tensor_tensor(
                                out=xw[:, j_ * cc + tm_ * 128: j_ * cc + tm_ * 128 + 128],
                                in0=pt[:], in1=xw[:, j_ * cc + tm_ * 128: j_ * cc + tm_ * 128 + 128],
                                op=ALU.add))(j, tm), waits=[ttr, tld, tld2])
                            tvv = E("vector", (lambda j_, tm_: lambda e: e.tensor_tensor(
                                out=outc[:, j_ * cc + tm_ * 128: j_ * cc + tm_ * 128 + 128],
                                in0=outc[:, j_ * cc + tm_ * 128: j_ * cc + tm_ * 128 + 128],
                                in1=xw[:, j_ * cc + tm_ * 128: j_ * cc + tm_ * 128 + 128],
                                op=ALU.add))(j, tm), inc=sV)
                        tE["ptt"] = tvv
                t_wr = pl.mark("vector", sV)
                if is_ffn:
                    t_st = E("sync", (lambda qa_: lambda e: e.dma_start(
                        out=_dr3(out_fm, qa_), in_=_fm3(outc, 8)))(qa),
                        waits=[t_wr], inc=dW, dma=True)
                else:
                    t_st = E("sync", (lambda qa_: lambda e: e.dma_start(
                        out=_dr3(s_x4, qa_), in_=_fm3(outc, 8)))(qa),
                        waits=[t_wr], inc=dW, dma=True)
                t_out_used = t_st
        pl.barrier(bar)

    npass = 2 if th >= 2 * cc else 1
    for hp in range(npass):
        moe_ffn_phase(False, hp, npass)
    for hp in range(npass):
        moe_ffn_phase(True, hp, npass)

    tall_w = pl.tok(dW)
    tall_g = pl.tok(dG)
    E("sync", lambda e: e.nop(), waits=[tall_w, tall_g])

    def gpregs(eng):
        regs = [eng.register(f"roff{q}").__enter__() for q in range(ncc)]
        eng.wait_ge(pl.sem_handles[dW], t_hoff[1])
        eng.reg_load(regs[0], C["hoff_sb"][0:1, 0:1])
        for q in range(1, ncc):
            eng.reg_alu(regs[q], regs[0], q * cc, ALU.add)
        for q in range(ncc):
            regholder[f"r{q}"] = regs[q]

    with nc.Block() as block:
        for s in SEMS:
            pl.sem_handles[s] = es.enter_context(nc.semaphore(s))
        pl.replay(block, {"gpsimd": gpregs})

    es.close()
    return nc


def _stage(inputs, t):
    th = t // 2
    x = np.asarray(inputs["L0_attn_x"])
    f = {}
    f["in_w"] = np.ascontiguousarray(inputs["L0_attn_in_w"], np.float32)
    f["xproj_w"] = np.ascontiguousarray(inputs["L0_attn_x_proj_w"], np.float32)
    f["out_w"] = np.ascontiguousarray(inputs["L0_attn_out_w"], np.float32)
    f["lif_in_w"] = np.ascontiguousarray(inputs["L0_attn_lif_in_w"], np.float32)
    f["lif_out_w"] = np.ascontiguousarray(inputs["L0_attn_lif_out_w"], np.float32)
    f["router_w"] = np.ascontiguousarray(inputs["L0_attn_router_w"], np.float32)
    f["gate_w"] = np.ascontiguousarray(inputs["L0_attn_gate_w"], np.float32)
    f["up_w"] = np.ascontiguousarray(inputs["L0_attn_up_w"], np.float32)
    f["down_w"] = np.ascontiguousarray(inputs["L0_attn_down_w"], np.float32)
    f["fgate_w"] = np.ascontiguousarray(inputs["L0_ffn_gate_w"], np.float32)
    f["w1"] = np.ascontiguousarray(inputs["L0_ffn_w1"], np.float32)
    f["w2"] = np.ascontiguousarray(inputs["L0_ffn_w2"], np.float32)
    f["b2r"] = np.ascontiguousarray(inputs["L0_ffn_b2"], np.float32)
    cw = np.asarray(inputs["L0_attn_conv_w"], np.float32)[:, 0, :]
    f["conv_w_fm"] = np.ascontiguousarray(cw.reshape(16, 128, 4).transpose(1, 0, 2).reshape(128, 64))
    f["conv_b_fm"] = np.ascontiguousarray(np.asarray(inputs["L0_attn_conv_b"], np.float32).reshape(16, 128).T)
    al = np.asarray(inputs["L0_attn_A_log"], np.float32)
    f["alog_fm"] = np.ascontiguousarray(al.reshape(16, 128, 16).transpose(1, 0, 2).reshape(128, 256))
    f["dskip_fm"] = np.ascontiguousarray(np.asarray(inputs["L0_attn_Dskip"], np.float32).reshape(16, 128).T)
    f["dtw_fm"] = np.ascontiguousarray(np.asarray(inputs["L0_attn_dt_w"], np.float32)[0].reshape(16, 128).T)
    f["dtb_fm"] = np.ascontiguousarray(np.asarray(inputs["L0_attn_dt_b"], np.float32).reshape(16, 128).T)
    f["ssmw_fm"] = np.ascontiguousarray(np.asarray(inputs["L0_attn_ssm_norm_w"], np.float32).reshape(8, 128).T)
    f["lifw_fm"] = np.ascontiguousarray(np.asarray(inputs["L0_attn_lif_norm_w"], np.float32).reshape(8, 128).T)
    f["moew_fm"] = np.ascontiguousarray(np.asarray(inputs["L0_attn_moe_norm_w"], np.float32).reshape(8, 128).T)
    b1 = np.asarray(inputs["L0_ffn_b1"], np.float32)
    f["b1_fm"] = np.ascontiguousarray(b1.reshape(8, 32, 128).transpose(2, 0, 1).reshape(128, 256))
    f["gb_rep"] = np.ascontiguousarray(np.tile(np.asarray(inputs["L0_ffn_gate_b"], np.float32)[None, :], (128, 1)))
    f["c_inv1024"] = np.full((128, 1), 1.0 / 1024, np.float32)
    f["eps5"] = np.full((128, 1), 1e-5, np.float32)
    f["eps6"] = np.full((128, 1), 1e-6, np.float32)
    f["ones1"] = np.ones((1, 128), np.float32)
    f["iden"] = np.eye(128, dtype=np.float32)

    feeds = []
    for c in range(8):
        b, h = c // 2, c % 2
        fc = dict(f)
        fc["x_fm"] = np.ascontiguousarray(x[b].T.astype(np.float32))
        fc["hoff"] = np.full((1, 1), h * th, np.uint32)
        feeds.append(fc)
    return feeds


def kernel(**inputs):
    t = 4096
    th = t // 2
    feeds = _stage(inputs, t)
    nc = build_program(t)
    res = run_bass_kernel_spmd(nc, feeds, core_ids=list(range(8)))
    out = np.empty((4, t, D), np.float32)
    for c in range(8):
        b, h = c // 2, c % 2
        out[b, h * th:(h + 1) * th, :] = np.asarray(res.results[c]["out_fm"]).T
    return out


# version 09
# speedup vs baseline: 1.5738x; 1.0108x over previous
import numpy as np
import concourse.bass as bass
import concourse.mybir as mybir
from concourse.bass_utils import run_bass_kernel_spmd
from contextlib import ExitStack

F32 = mybir.dt.float32
BF16 = mybir.dt.bfloat16
U32 = mybir.dt.uint32
AF = mybir.ActivationFunctionType
ALU = mybir.AluOpType
AX = mybir.AxisListType.X

DEV_DEBUG = False
D = 1024
DI = 2048
DS = 16
NE = 8
DE = 2048
DFF = 4096

CA = 256
CB = 256
CC = 512

ENGS = ("sync", "vector", "scalar", "gpsimd", "tensor")


class Plan:
    def __init__(self, nc):
        self.nc = nc
        self.ops = {e: [] for e in ENGS}
        self.ctr = {}
        self.sem_handles = {}
        self.waited = {e: {} for e in ENGS}
        self.pending = {e: [] for e in ENGS}
        self.since_bar = {e: False for e in ENGS}

    def sem(self, name):
        self.ctr[name] = 0
        return name

    def tok(self, s):
        return (s, self.ctr[s])

    def emit(self, eng, fn, waits=(), inc=None, dma=False):
        ws = list(self.pending[eng]) + [w for w in waits if w is not None]
        self.pending[eng] = []
        eff = []
        for (s, v) in ws:
            if v <= 0 or self.waited[eng].get(s, -1) >= v:
                continue
            self.waited[eng][s] = v
            eff.append((s, v))
        incs = []
        t = None
        if inc is not None:
            n = 16 if dma else 1
            self.ctr[inc] += n
            incs.append((inc, n))
            t = (inc, self.ctr[inc])
        self.ops[eng].append([eff, fn, incs, dma])
        self.since_bar[eng] = True
        return t

    def mark(self, eng, sem):
        item = self.ops[eng][-1]
        n = 16 if item[3] else 1
        self.ctr[sem] += n
        for i, (s0, n0) in enumerate(item[2]):
            if s0 == sem:
                item[2][i] = (s0, n0 + n)
                break
        else:
            item[2].append((sem, n))
        return (sem, self.ctr[sem])

    ENG_SEM = {"vector": "sV", "scalar": "sA", "tensor": "sM"}

    def barrier(self, bar):
        waits = []
        for e in ("vector", "scalar", "tensor"):
            if self.since_bar[e] and self.ops[e]:
                item = self.ops[e][-1]
                if item[2]:
                    s0 = item[2][0][0]
                    waits.append((s0, self.ctr[s0]))
                else:
                    waits.append(self.mark(e, self.ENG_SEM[e]))
        for e in ENGS:
            self.since_bar[e] = False
        for s in ("dW", "dG"):
            if s in self.ctr:
                waits.append((s, self.ctr[s]))
        for e in ENGS:
            self.pending[e].extend(waits)

    def replay(self, block, regfns):
        def mk(ename):
            def body(eng):
                if ename in regfns:
                    regfns[ename](eng)
                for ws, fn, incs, _dma in self.ops[ename]:
                    for (s, v) in ws:
                        eng.wait_ge(self.sem_handles[s], v)
                    inst = fn(eng)
                    for (s, n) in incs:
                        inst.then_inc(self.sem_handles[s], n)
            return body
        for ename in ENGS:
            getattr(block, ename)(mk(ename))


def _r128(ap_2d):
    return ap_2d.rearrange("(k p) m -> p k m", p=128)


def _fm3(sbuf_buf, a):
    return sbuf_buf.rearrange("p (a q) -> p a q", a=a)


def _dr3(dram_2d, colslice, a=None):
    ap = dram_2d[:, colslice]
    n = ap.shape[0] // 128
    return ap.rearrange("(a p) q -> p a q", p=128)


def build_program(t=4096):
    nca = t // CA
    ncb = t // CB
    th = t // 2
    cc = min(CC, th)
    ncc = th // cc
    ntt = th // 128
    ntc = cc // 128

    nc = bass.Bass("TRN2", target_bir_lowering=False, detect_race_conditions=False)

    def di(name, shape, dtype=F32):
        return nc.dram_tensor(name, shape, dtype, kind="ExternalInput").ap()

    x_fm = di("x_fm", [D, t])
    hoff = di("hoff", [1, 1], U32)
    in_w = di("in_w", [D, 2 * DI])
    xproj_w = di("xproj_w", [DI, 33])
    out_w = di("out_w", [DI, D])
    lif_in_w = di("lif_in_w", [D, D])
    lif_out_w = di("lif_out_w", [D, D])
    router_w = di("router_w", [D, NE])
    gate_w = di("gate_w", [NE, D, DE])
    up_w = di("up_w", [NE, D, DE])
    down_w = di("down_w", [NE, DE, D])
    fgate_w = di("fgate_w", [D, NE])
    w1 = di("w1", [NE, D, DFF])
    w2 = di("w2", [NE, DFF, D])
    b2r = di("b2r", [NE, D])
    conv_w_fm = di("conv_w_fm", [128, 64])
    conv_b_fm = di("conv_b_fm", [128, 16])
    alog_fm = di("alog_fm", [128, 256])
    dskip_fm = di("dskip_fm", [128, 16])
    dtw_fm = di("dtw_fm", [128, 16])
    dtb_fm = di("dtb_fm", [128, 16])
    ssmw_fm = di("ssmw_fm", [128, 8])
    lifw_fm = di("lifw_fm", [128, 8])
    moew_fm = di("moew_fm", [128, 8])
    b1_fm = di("b1_fm", [128, 256])
    gb_rep = di("gb_rep", [128, NE])
    c_inv1024 = di("c_inv1024", [128, 1])
    eps5 = di("eps5", [128, 1])
    eps6 = di("eps6", [128, 1])
    ones1 = di("ones1", [1, 128])
    iden = di("iden", [128, 128])

    out_fm = nc.dram_tensor("out_fm", [D, th], F32, kind="ExternalOutput").ap()

    s_xc = nc.dram_tensor("s_xc", [DI, t], BF16).ap()
    s_zs = nc.dram_tensor("s_zs", [DI, t], BF16).ap()
    s_res1 = nc.dram_tensor("s_res1", [D, t], BF16).ap()
    s_x2 = nc.dram_tensor("s_x2", [D, t], F32).ap()
    s_x4 = nc.dram_tensor("s_x4", [D, th], F32).ap()
    s_dbg1 = nc.dram_tensor("s_dbg1", [D, t], F32).ap()
    s_dbg2 = nc.dram_tensor("s_dbg2", [D, t], BF16).ap()
    s_xdbl = nc.dram_tensor("s_xdbl", [33, CB], F32).ap()
    s_xdbl_bf = nc.dram_tensor("s_xdbl_bf", [32, CB], BF16).ap()

    es = ExitStack()
    sb = lambda *a: es.enter_context(nc.sbuf_tensor(*a))
    psb = lambda *a: es.enter_context(nc.psum_tensor(*a))

    pl = Plan(nc)
    dW = pl.sem("dW")
    dG = pl.sem("dG")
    sV = pl.sem("sV")
    sA = pl.sem("sA")
    sM = pl.sem("sM")
    bar = pl.sem("bar")
    SEMS = [dW, dG, sV, sA, sM, bar]
    E = pl.emit

    C = {}
    C["c1k_bf"] = sb("c1k_bf", [128, 1], BF16)
    C["ones_bf"] = sb("ones_bf", [1, 128], BF16)
    C["iden_f"] = sb("iden_f", [128, 128], F32)
    C["convw"] = sb("convw", [128, 64], F32)
    C["convb"] = sb("convb", [128, 16], F32)
    C["A_sb"] = sb("A_sb", [128, 256], F32)
    C["dskip"] = sb("dskip", [128, 16], F32)
    C["dtw"] = sb("dtw", [128, 16], F32)
    C["dtb"] = sb("dtb", [128, 16], F32)
    C["ssmw"] = sb("ssmw", [128, 8], F32)
    C["lifw"] = sb("lifw", [128, 8], F32)
    C["moew"] = sb("moew", [128, 8], F32)
    C["b1c"] = sb("b1c", [128, 256], F32)
    C["gbr"] = sb("gbr", [128, NE], F32)
    C["b2_bf"] = sb("b2_bf", [NE, D], BF16)
    C["rout_bf"] = sb("rout_bf", [128, 8 * NE], BF16)
    C["fgat_bf"] = sb("fgat_bf", [128, 8 * NE], BF16)
    C["hoff_sb"] = sb("hoff_sb", [1, 1], U32)
    C["eps5"] = sb("eps5c", [128, 1], F32)
    C["eps6"] = sb("eps6c", [128, 1], F32)

    pm = [psb(f"pm{i}", [128, 512], F32) for i in range(3)]
    pt = psb("pt", [128, 128], F32)
    prow = psb("prow", [1, 1024], F32)
    prep = psb("prep", [128, 512], F32)
    px33 = psb("px33", [33, 512], F32)

    for nm, src in [("iden_f", iden), ("convw", conv_w_fm), ("convb", conv_b_fm),
                    ("dskip", dskip_fm), ("dtw", dtw_fm), ("dtb", dtb_fm),
                    ("ssmw", ssmw_fm), ("lifw", lifw_fm), ("moew", moew_fm),
                    ("b1c", b1_fm), ("gbr", gb_rep)]:
        E("sync", (lambda d_, s_: lambda e: e.dma_start(out=d_[:], in_=s_))(C[nm], src), inc=dW, dma=True)
    E("sync", lambda e: e.dma_start(out=C["hoff_sb"][:], in_=hoff), inc=dW, dma=True)
    t_hoff = pl.tok(dW)
    E("sync", lambda e: e.dma_start(out=C["eps5"][:], in_=eps5), inc=dW, dma=True)
    E("sync", lambda e: e.dma_start(out=C["eps6"][:], in_=eps6), inc=dW, dma=True)
    E("gpsimd", lambda e: e.dma_start(out=C["c1k_bf"][:], in_=c_inv1024), inc=dG, dma=True)
    E("gpsimd", lambda e: e.dma_start(out=C["ones_bf"][:], in_=ones1), inc=dG, dma=True)
    E("gpsimd", lambda e: e.dma_start(out=C["b2_bf"][:], in_=b2r), inc=dG, dma=True)
    E("gpsimd", lambda e: e.dma_start(out=_fm3(C["rout_bf"], 8), in_=_r128(router_w)), inc=dG, dma=True)
    E("gpsimd", lambda e: e.dma_start(out=_fm3(C["fgat_bf"], 8), in_=_r128(fgate_w)), inc=dG, dma=True)
    E("sync", lambda e: e.dma_start(out=C["A_sb"][:], in_=alog_fm), inc=dW, dma=True)
    tA = pl.tok(dW)
    tA2 = E("scalar", lambda e: e.activation(out=C["A_sb"][:], in_=C["A_sb"][:], func=AF.Exp),
            waits=[tA], inc=sA)
    E("vector", lambda e: e.tensor_scalar(out=C["A_sb"][:], in0=C["A_sb"][:], scalar1=-1.0,
                                          scalar2=None, op0=ALU.mult), waits=[tA2], inc=sV)
    t_consts = pl.tok(dW)

    regholder = {}

    with ExitStack() as phA:
        asb = lambda *a: phA.enter_context(nc.sbuf_tensor(*a))
        inw_bf = asb("inw_bf", [128, 8 * 2 * DI], BF16)
        xA = asb("xA", [128, 8 * CA], F32)
        xbfA = asb("xbfA", [128, 8 * CA], BF16)
        sqA = asb("sqA", [128, 8 * CA], BF16)
        rows = asb("rows", [1, 8 * CA], F32)
        rows_bf = asb("rows_bf", [1, 4 * CA], BF16)
        murep = asb("murep", [128, CA], F32)
        arep = asb("arep", [128, CA], F32)
        abrep = asb("abrep", [128, CA], F32)
        tmpA = asb("tmpA", [128, 8 * CA], F32)
        res1A = asb("res1A", [128, 8 * CA], BF16)
        xrA = asb("xrA", [128, 8 * CA], BF16)
        xbA = asb("xbA", [128, 16 * (CA + 3)], F32)
        ctmp = asb("ctmp", [128, 2 * CA], F32)
        xcA = asb("xcA", [128, 16 * CA], BF16)
        zsA = asb("zsA", [128, 16 * CA], BF16)

        E("gpsimd", lambda e: e.dma_start(out=_fm3(inw_bf, 8), in_=_r128(in_w)), inc=dG, dma=True)
        t_inw = pl.tok(dG)
        E("vector", lambda e: e.memset(xbA.rearrange("p (j q) -> p j q", j=16)[:, :, 0:3], 0.0), inc=sV)

        t_ln_done = None
        t_rowact_prev = None
        t_sq_done = None
        t_mm_done = None
        t_conv_done = None
        t_zs_st = None
        t_xc_st = None
        t_res_st = None
        bank_free = {}

        for c in range(nca):
            col = slice(c * CA, (c + 1) * CA)
            tx = E("sync", (lambda cl: lambda e: e.dma_start(
                out=_fm3(xA, 8), in_=_dr3(x_fm, cl)))(col),
                waits=[t_ln_done, t_sq_done, t_res_st], inc=dW, dma=True)
            tcast = E("vector", lambda e: e.tensor_copy(out=xbfA[:], in_=xA[:]), waits=[tx], inc=sV)
            tsq = E("scalar", lambda e: e.activation(out=sqA[:], in_=xA[:], func=AF.Square),
                    waits=[tx], inc=sA)
            t_sq_done = tsq
            for k in range(8):
                E("tensor", (lambda k_: lambda e: e.matmul(
                    prow[:, 0:CA], C["c1k_bf"][:], xbfA[:, k_ * CA:(k_ + 1) * CA],
                    start=(k_ == 0), stop=(k_ == 7)))(k),
                    waits=[tcast, pl.tok(dG), bank_free.get("prow")] if k == 0 else ())
            for k in range(8):
                E("tensor", (lambda k_: lambda e: e.matmul(
                    prow[:, CA:2 * CA], C["c1k_bf"][:], sqA[:, k_ * CA:(k_ + 1) * CA],
                    start=(k_ == 0), stop=(k_ == 7)))(k), waits=[tsq] if k == 0 else ())
            tstat = pl.mark("tensor", sM)
            R = lambda i: rows[:, i * CA:(i + 1) * CA]
            E("vector", lambda e: e.tensor_copy(out=R(0), in_=prow[:, 0:CA]),
              waits=[tstat, t_rowact_prev])
            tevr = E("vector", lambda e: e.tensor_copy(out=R(1), in_=prow[:, CA:2 * CA]), inc=sV)
            bank_free["prow"] = tevr
            tv0 = E("vector", lambda e: e.tensor_tensor(out=R(2), in0=R(0), in1=R(0), op=ALU.mult), inc=sV)
            tv = E("vector", lambda e: e.tensor_tensor(out=R(2), in0=R(1), in1=R(2), op=ALU.subtract), waits=[tv0], inc=sV)
            E("scalar", lambda e: e.activation(out=R(3), in_=R(2), func=AF.Ln, bias=C["eps5"][0:1, 0:1]), waits=[tv])
            ta = E("scalar", lambda e: e.activation(out=R(3), in_=R(3), func=AF.Exp, scale=-0.5), inc=sA)
            tv0 = E("vector", lambda e: e.tensor_tensor(out=R(5), in0=R(3), in1=R(3), op=ALU.mult), waits=[ta], inc=sV)
            tv = E("vector", lambda e: e.tensor_tensor(out=R(5), in0=R(2), in1=R(5), op=ALU.mult), waits=[tv0], inc=sV)
            E("scalar", lambda e: e.activation(out=R(5), in_=R(5), func=AF.Ln, bias=C["eps6"][0:1, 0:1]), waits=[tv])
            tb = E("scalar", lambda e: e.activation(out=R(5), in_=R(5), func=AF.Exp, scale=-0.5), inc=sA)
            E("vector", lambda e: e.tensor_tensor(out=R(4), in0=R(3), in1=R(5), op=ALU.mult), waits=[tb])
            t_rowact_prev = tb
            RB = lambda i: rows_bf[:, i * CA:(i + 1) * CA]
            E("vector", lambda e: e.tensor_copy(out=RB(0), in_=R(0)))
            E("vector", lambda e: e.tensor_copy(out=RB(1), in_=R(3)))
            trb = E("vector", lambda e: e.tensor_copy(out=RB(2), in_=R(4)), inc=sV)
            tm1 = E("tensor", lambda e: e.matmul(prep[:, 0:CA], C["ones_bf"][:], RB(0),
                                                 start=True, stop=True),
                    waits=[trb, bank_free.get("prep")], inc=sM)
            tev1 = E("vector", lambda e: e.tensor_copy(out=murep[:], in_=prep[:, 0:CA]), waits=[tm1], inc=sV)
            tm2 = E("tensor", lambda e: e.matmul(prep[:, 0:CA], C["ones_bf"][:], RB(1),
                                                 start=True, stop=True), waits=[tev1], inc=sM)
            tev2 = E("vector", lambda e: e.tensor_copy(out=arep[:], in_=prep[:, 0:CA]), waits=[tm2], inc=sV)
            tm3 = E("tensor", lambda e: e.matmul(prep[:, 0:CA], C["ones_bf"][:], RB(2),
                                                 start=True, stop=True), waits=[tev2], inc=sM)
            tev3 = E("vector", lambda e: e.tensor_copy(out=abrep[:], in_=prep[:, 0:CA]), waits=[tm3], inc=sV)
            bank_free["prep"] = tev3
            for k in range(8):
                kc = slice(k * CA, (k + 1) * CA)
                E("vector", (lambda kc_: lambda e: e.tensor_tensor(
                    out=tmpA[:, kc_], in0=xA[:, kc_], in1=murep[:], op=ALU.subtract))(kc),
                    waits=[t_mm_done] if k == 0 else ())
                E("vector", (lambda kc_: lambda e: e.tensor_tensor(
                    out=res1A[:, kc_], in0=tmpA[:, kc_], in1=arep[:], op=ALU.mult))(kc))
                E("vector", (lambda kc_: lambda e: e.tensor_tensor(
                    out=tmpA[:, kc_], in0=tmpA[:, kc_], in1=abrep[:], op=ALU.mult))(kc))
                E("vector", (lambda kc_, k_: lambda e: e.tensor_scalar(
                    out=xrA[:, kc_], in0=tmpA[:, kc_], scalar1=C["ssmw"][:, k_:k_ + 1],
                    scalar2=None, op0=ALU.mult))(kc, k))
            t_xr = pl.mark("vector", sV)
            t_ln_done = t_xr
            t_res_st = E("sync", (lambda cl: lambda e: e.dma_start(
                out=_dr3(s_res1, cl), in_=_fm3(res1A, 8)))(col),
                waits=[t_xr], inc=dW, dma=True)
            if c > 0:
                E("vector", lambda e: e.tensor_copy(
                    out=xbA.rearrange("p (j q) -> p j q", j=16)[:, :, 0:3],
                    in_=xbA.rearrange("p (j q) -> p j q", j=16)[:, :, CA:CA + 3]),
                    waits=[t_conv_done])
            t_halo = pl.mark("vector", sV)
            for m in range(32):
                bank = pm[m % 3]
                for k in range(8):
                    E("tensor", (lambda m_, k_, bank_: lambda e: e.matmul(
                        bank_[:, 0:CA],
                        inw_bf[:, k_ * 4096 + m_ * 128: k_ * 4096 + m_ * 128 + 128],
                        xrA[:, k_ * CA:(k_ + 1) * CA],
                        start=(k_ == 0), stop=(k_ == 7)))(m, k, bank),
                        waits=[t_xr, t_inw, t_halo, bank_free.get(m % 3)] if k == 0 else ())
                tmm = pl.mark("tensor", sM)
                if m < 16:
                    tev = E("vector", (lambda m_, bank_: lambda e: e.tensor_copy(
                        out=xbA.rearrange("p (j q) -> p j q", j=16)[:, m_, 3:3 + CA],
                        in_=bank_[:, 0:CA]))(m, bank), waits=[tmm], inc=sV)
                else:
                    tev = E("scalar", (lambda m_, bank_: lambda e: e.activation(
                        out=zsA[:, (m_ - 16) * CA:(m_ - 15) * CA], in_=bank_[:, 0:CA],
                        func=AF.Silu))(m, bank), waits=[tmm], inc=sA)
                bank_free[m % 3] = tev
            t_mm_done = pl.tok(sM)
            t_xb = pl.tok(sV)
            t_zs = pl.tok(sA)
            t_zs_st = E("sync", (lambda cl: lambda e: e.dma_start(
                out=_dr3(s_zs, cl), in_=_fm3(zsA, 16)))(col),
                waits=[t_zs], inc=dW, dma=True)
            t_silu_prev = None
            for j in range(16):
                xbj = xbA.rearrange("p (j q) -> p j q", j=16)[:, j, :]
                E("vector", (lambda j_, xbj_: lambda e: e.tensor_scalar(
                    out=ctmp[:, 0:CA], in0=xbj_[:, 0:CA],
                    scalar1=C["convw"][:, j_ * 4:j_ * 4 + 1], scalar2=None, op0=ALU.mult))(j, xbj),
                    waits=[t_xb] if j == 0 else ())
                for kk in range(1, 4):
                    E("vector", (lambda j_, kk_, xbj_: lambda e: e.scalar_tensor_tensor(
                        out=ctmp[:, 0:CA], in0=xbj_[:, kk_:kk_ + CA],
                        scalar=C["convw"][:, j_ * 4 + kk_:j_ * 4 + kk_ + 1],
                        in1=ctmp[:, 0:CA], op0=ALU.mult, op1=ALU.add))(j, kk, xbj))
                tc2 = E("vector", lambda e: e.tensor_copy(out=ctmp[:, CA:2 * CA], in_=ctmp[:, 0:CA]),
                        waits=[t_silu_prev], inc=sV)
                t_silu_prev = E("scalar", (lambda j_: lambda e: e.activation(
                    out=xcA[:, j_ * CA:(j_ + 1) * CA], in_=ctmp[:, CA:2 * CA], func=AF.Silu,
                    bias=C["convb"][:, j_:j_ + 1]))(j), waits=[tc2], inc=sA)
            t_conv_done = pl.mark("vector", sV)
            t_xc = pl.tok(sA)
            t_xc_st = E("sync", (lambda cl: lambda e: e.dma_start(
                out=_dr3(s_xc, cl), in_=_fm3(xcA, 16)))(col),
                waits=[t_xc], inc=dW, dma=True)
        pl.barrier(bar)

    with ExitStack() as phB:
        bsb = lambda *a: phB.enter_context(nc.sbuf_tensor(*a))
        outw_bf = bsb("outw_bf", [128, 16 * D], BF16)
        lifi_bf = bsb("lifi_bf", [128, 8 * D], BF16)
        lifo_bf = bsb("lifo_bf", [128, 8 * D], BF16)
        xprj_bf = bsb("xprj_bf", [128, 16 * 33], BF16)
        xcB = bsb("xcB", [128, 16 * CB], BF16)
        zsB = bsb("zsB", [128, 16 * CB], BF16)
        res1B = bsb("res1B", [128, 8 * CB], BF16)
        xdblB = bsb("xdblB", [33, CB], F32)
        dt0rep = bsb("dt0rep", [128, CB], F32)
        bcrep = bsb("bcrep", [128, 32 * CB], BF16)
        dtB = bsb("dtB", [128, 16 * CB], F32)
        wB = bsb("wB", [128, CB], BF16)
        bufA = bsb("bufA", [128, 16 * (CB + 1)], F32)
        bufB = bsb("bufB", [128, 16 * (CB + 1)], F32)
        bufC = bsb("bufC", [128, 16 * (CB + 1)], F32)
        hstate = bsb("hstate", [128, 256], F32)
        yj = bsb("yj", [128, CB], F32)
        y2B = bsb("y2B", [128, 16 * CB], BF16)
        x1B = bsb("x1B", [128, 8 * CB], F32)
        rowsB = bsb("rowsB", [1, 2 * CB], F32)
        rowsBbf = bsb("rowsBbf", [1, CB], BF16)
        brepB = bsb("brepB", [128, CB], F32)
        xr1B = bsb("xr1B", [128, 8 * CB], BF16)
        mstate = bsb("mstate", [128, 8], F32)
        cBuf = bufC
        x2B = bsb("x2B", [128, 8 * CB], F32)
        spkB = bufB.bitcast(BF16)[:, 0:8 * CB]

        E("gpsimd", lambda e: e.dma_start(out=_fm3(outw_bf, 16), in_=_r128(out_w)), inc=dG, dma=True)
        E("gpsimd", lambda e: e.dma_start(out=_fm3(lifi_bf, 8), in_=_r128(lif_in_w)), inc=dG, dma=True)
        E("gpsimd", lambda e: e.dma_start(out=_fm3(lifo_bf, 8), in_=_r128(lif_out_w)), inc=dG, dma=True)
        E("gpsimd", lambda e: e.dma_start(out=_fm3(xprj_bf, 16), in_=_r128(xproj_w)), inc=dG, dma=True)
        t_wB = pl.tok(dG)
        E("vector", lambda e: e.memset(hstate[:], 0.0))
        E("vector", lambda e: e.memset(mstate[:], 0.0))
        E("vector", lambda e: e.memset(
            bufA.rearrange("p (n q) -> p n q", n=16)[:, :, 0:1], 0.0))
        t_init = pl.mark("vector", sV)

        t_v_done = None
        t_rowactB = None
        t_xp_prev = None
        t_red_prev = None
        t_x1_prev = None
        t_dt_prev = None
        t_bc_used = None
        t_c_done = None
        t_x2_st = None
        t_spk_used = None
        t_xr1_used = None
        bfree = {}

        for c in range(ncb):
            col = slice(c * CB, (c + 1) * CB)
            tld1 = E("sync", (lambda cl: lambda e: e.dma_start(
                out=_fm3(xcB, 16), in_=_dr3(s_xc, cl)))(col),
                waits=[t_v_done, t_xp_prev] + ([pl.tok(dW), pl.tok(dG)] if c == 0 else []),
                inc=dW, dma=True)
            tld2 = E("sync", (lambda cl: lambda e: e.dma_start(
                out=_fm3(zsB, 16), in_=_dr3(s_zs, cl)))(col),
                inc=dW, dma=True)
            tld3 = E("sync", (lambda cl: lambda e: e.dma_start(
                out=_fm3(res1B, 8), in_=_dr3(s_res1, cl)))(col),
                waits=[t_x1_prev], inc=dW, dma=True)
            for k in range(16):
                E("tensor", (lambda k_: lambda e: e.matmul(
                    px33[:, 0:CB], xprj_bf[:, k_ * 33:(k_ + 1) * 33],
                    xcB[:, k_ * CB:(k_ + 1) * CB], start=(k_ == 0), stop=(k_ == 15)))(k),
                    waits=[tld1, t_wB, bfree.get("px")] if k == 0 else ())
            t_xp = pl.mark("tensor", sM)
            t_xp_prev = t_xp
            tevx = E("vector", lambda e: e.tensor_copy(out=xdblB[:], in_=px33[:, 0:CB]),
                     waits=[t_xp], inc=sV)
            bfree["px"] = tevx
            tw1 = E("sync", lambda e: e.dma_start(out=s_xdbl[:, 0:CB], in_=xdblB[:]),
                    waits=[tevx], inc=dW, dma=True)
            tw2 = E("gpsimd", lambda e: e.dma_start(out=s_xdbl_bf[:, 0:CB], in_=xdblB[1:33, :]),
                    waits=[tevx, t_bc_used], inc=dG, dma=True)
            tb1 = E("sync", lambda e: e.dma_start(
                out=dt0rep[:],
                in_=s_xdbl.rearrange("r q -> (r q)")[0:CB][None, :].partition_broadcast(128)),
                waits=[tw1, t_dt_prev], inc=dW, dma=True)
            tb2 = E("sync", lambda e: e.dma_start(
                out=bcrep[:],
                in_=s_xdbl_bf.rearrange("r q -> (r q)")[None, :].partition_broadcast(128)),
                waits=[tw2], inc=dW, dma=True)
            for j in range(16):
                E("vector", (lambda j_: lambda e: e.tensor_scalar(
                    out=dtB[:, j_ * CB:(j_ + 1) * CB], in0=dt0rep[:],
                    scalar1=C["dtw"][:, j_:j_ + 1], scalar2=C["dtb"][:, j_:j_ + 1],
                    op0=ALU.mult, op1=ALU.add))(j), waits=[tb1] if j == 0 else ())
            tpre = pl.mark("vector", sV)
            E("vector", lambda e: e.tensor_scalar(out=bufC[:, 0:16 * CB], in0=dtB[:], scalar1=0.0,
                                                  scalar2=None, op0=ALU.max))
            E("scalar", lambda e: e.activation(out=dtB[:], in_=dtB[:], func=AF.Abs), waits=[tpre])
            E("scalar", lambda e: e.activation(out=dtB[:], in_=dtB[:], func=AF.Exp, scale=-1.0))
            tsp = E("scalar", lambda e: e.activation(out=dtB[:], in_=dtB[:], func=AF.Ln, bias=1.0), inc=sA)
            tdt = E("vector", lambda e: e.tensor_tensor(out=dtB[:], in0=dtB[:], in1=bufC[:, 0:16 * CB],
                                                        op=ALU.add), waits=[tsp], inc=sV)
            t_dt_prev = tdt
            slotted = lambda buf: buf.rearrange("p (n q) -> p n q", n=16)
            for j in range(16):
                jc = slice(j * CB, (j + 1) * CB)
                for n in range(16):
                    tda = E("scalar", (lambda j_, n_: lambda e: e.activation(
                        out=slotted(bufB)[:, n_, 1:CB + 1], in_=dtB[:, j_ * CB:(j_ + 1) * CB],
                        func=AF.Exp, scale=C["A_sb"][:, j_ * 16 + n_:j_ * 16 + n_ + 1]))(j, n),
                        waits=([tdt, t_red_prev] + ([t_spk_used] if j == 0 else [])) if n == 0 else (),
                        inc=sA if n == 15 else None)
                E("vector", lambda e: e.memset(slotted(bufB)[:, :, 0:1], 0.0), waits=[tda])
                E("vector", (lambda jc_: lambda e: e.tensor_tensor(
                    out=wB[:], in0=dtB[:, jc_], in1=xcB[:, jc_], op=ALU.mult))(jc))
                for n in range(16):
                    E("vector", (lambda n_: lambda e: e.tensor_tensor(
                        out=slotted(bufA)[:, n_, 1:CB + 1],
                        in0=bcrep[:, n_ * CB:(n_ + 1) * CB], in1=wB[:], op=ALU.mult))(n),
                        waits=[tb2] if n == 0 else ())
                t_h0 = E("vector", (lambda j_: lambda e: e.tensor_copy(
                    out=slotted(bufA)[:, :, 0], in_=hstate[:, j_ * 16:(j_ + 1) * 16]))(j),
                    waits=[t_init] if (c == 0 and j == 0) else (), inc=sV)
                E("vector", lambda e: e.tensor_tensor_scan(
                    out=bufC[:], data0=bufB[:], data1=bufA[:], initial=0.0,
                    op0=ALU.mult, op1=ALU.add), waits=[t_h0])
                bfree["bufA"] = pl.mark("vector", sV)
                E("vector", (lambda j_: lambda e: e.tensor_copy(
                    out=hstate[:, j_ * 16:(j_ + 1) * 16], in_=slotted(bufC)[:, :, CB]))(j),
                    waits=[bfree["bufA"]])
                for n in range(16):
                    E("vector", (lambda n_: lambda e: e.tensor_tensor(
                        out=bufB[:, n_ * CB:(n_ + 1) * CB], in0=slotted(bufC)[:, n_, 1:CB + 1],
                        in1=bcrep[:, (16 + n_) * CB:(17 + n_) * CB], op=ALU.mult))(n))
                tym = pl.mark("vector", sV)
                trd = E("vector", lambda e: e.reduce_sum(
                    out=yj[:], in_=bufB[:, 0:16 * CB].rearrange("p (n q) -> p q n", n=16),
                    axis=AX), waits=[tym], inc=sV)
                t_red_prev = trd
                t_yd = E("vector", (lambda j_, jc_: lambda e: e.scalar_tensor_tensor(
                    out=yj[:], in0=xcB[:, jc_], scalar=C["dskip"][:, j_:j_ + 1], in1=yj[:],
                    op0=ALU.mult, op1=ALU.add))(j, jc), waits=[trd], inc=sV)
                E("vector", (lambda jc_: lambda e: e.tensor_tensor(
                    out=y2B[:, jc_], in0=yj[:], in1=zsB[:, jc_], op=ALU.mult))(jc),
                    waits=[tld2, t_yd])
            t_y2 = pl.mark("vector", sV)
            for m in range(8):
                bank = pm[m % 2]
                for k in range(16):
                    E("tensor", (lambda m_, k_, bank_: lambda e: e.matmul(
                        bank_[:, 0:CB], outw_bf[:, k_ * D + m_ * 128: k_ * D + m_ * 128 + 128],
                        y2B[:, k_ * CB:(k_ + 1) * CB], start=(k_ == 0), stop=(k_ == 15)))(m, k, bank),
                        waits=[t_y2, bfree.get(m % 2)] if k == 0 else ())
                tmm = pl.mark("tensor", sM)
                tev = E("vector", (lambda m_, bank_: lambda e: e.tensor_tensor(
                    out=x1B[:, m_ * CB:(m_ + 1) * CB], in0=bank_[:, 0:CB],
                    in1=res1B[:, m_ * CB:(m_ + 1) * CB], op=ALU.add))(m, bank),
                    waits=[tmm, tld3], inc=sV)
                bfree[m % 2] = tev
            t_x1 = pl.tok(sV)
            t_x1_prev = t_x1
            sqB = y2B
            tsq = E("scalar", lambda e: e.activation(out=sqB[:, 0:8 * CB], in_=x1B[:], func=AF.Square),
                    waits=[t_x1], inc=sA)
            for k in range(8):
                E("tensor", (lambda k_: lambda e: e.matmul(
                    prow[:, 0:CB], C["c1k_bf"][:], sqB[:, k_ * CB:(k_ + 1) * CB],
                    start=(k_ == 0), stop=(k_ == 7)))(k),
                    waits=[tsq, bfree.get("prow")] if k == 0 else ())
            tms = pl.mark("tensor", sM)
            RB2 = lambda i: rowsB[:, i * CB:(i + 1) * CB]
            tev = E("vector", lambda e: e.tensor_copy(out=RB2(0), in_=prow[:, 0:CB]),
                    waits=[tms, t_rowactB], inc=sV)
            bfree["prow"] = tev
            E("scalar", lambda e: e.activation(out=RB2(1), in_=RB2(0), func=AF.Ln, bias=C["eps6"][0:1, 0:1]),
              waits=[tev])
            tr2 = E("scalar", lambda e: e.activation(out=RB2(1), in_=RB2(1), func=AF.Exp, scale=-0.5), inc=sA)
            t_rowactB = tr2
            trb = E("vector", lambda e: e.tensor_copy(out=rowsBbf[:], in_=RB2(1)), waits=[tr2], inc=sV)
            tm1 = E("tensor", lambda e: e.matmul(prep[:, 0:CB], C["ones_bf"][:], rowsBbf[:],
                                                 start=True, stop=True),
                    waits=[trb, bfree.get("prep")], inc=sM)
            tbr = E("vector", lambda e: e.tensor_copy(out=brepB[:], in_=prep[:, 0:CB]),
                    waits=[tm1], inc=sV)
            bfree["prep"] = tbr
            for k in range(8):
                kc = slice(k * CB, (k + 1) * CB)
                E("vector", (lambda kc_: lambda e: e.tensor_tensor(
                    out=x2B[:, kc_], in0=x1B[:, kc_], in1=brepB[:], op=ALU.mult))(kc),
                    waits=[tbr, t_xr1_used, t_x2_st] if k == 0 else ())
                E("vector", (lambda kc_, k_: lambda e: e.tensor_scalar(
                    out=xr1B[:, kc_], in0=x2B[:, kc_], scalar1=C["lifw"][:, k_:k_ + 1],
                    scalar2=None, op0=ALU.mult))(kc, k))
            t_xr1 = pl.mark("vector", sV)
            for m in range(8):
                bank = pm[m % 2]
                for k in range(8):
                    E("tensor", (lambda m_, k_, bank_: lambda e: e.matmul(
                        bank_[:, 0:CB], lifi_bf[:, k_ * D + m_ * 128: k_ * D + m_ * 128 + 128],
                        xr1B[:, k_ * CB:(k_ + 1) * CB], start=(k_ == 0), stop=(k_ == 7)))(m, k, bank),
                        waits=[t_xr1, bfree.get(m % 2), t_c_done] if k == 0 else ())
                tmm = pl.mark("tensor", sM)
                tev = E("vector", (lambda m_, bank_: lambda e: e.tensor_copy(
                    out=cBuf[:, m_ * CB:(m_ + 1) * CB], in_=bank_[:, 0:CB]))(m, bank),
                    waits=[tmm], inc=sV)
                bfree[m % 2] = tev
            t_xr1_used = pl.tok(sM)
            t_c = pl.tok(sV)
            cview = cBuf[:, 0:8 * CB].rearrange("p (m q) -> p q m", m=8)
            sview = spkB.rearrange("p (m q) -> p q m", m=8)
            for tt in range(CB):
                for gi in range(2):
                    sl = slice(gi * 4, (gi + 1) * 4)
                    E("vector", (lambda tt_, sl_: lambda e: e.scalar_tensor_tensor(
                        out=mstate[:, sl_], in0=mstate[:, sl_], scalar=0.9,
                        in1=cview[:, tt_, sl_], op0=ALU.mult, op1=ALU.add))(tt, sl),
                        waits=[t_c, t_spk_used] if (tt == 0 and gi == 0) else ())
                for gi in range(2):
                    sl = slice(gi * 4, (gi + 1) * 4)
                    E("vector", (lambda tt_, sl_: lambda e: e.tensor_scalar(
                        out=sview[:, tt_, sl_], in0=mstate[:, sl_], scalar1=1.0,
                        scalar2=None, op0=ALU.is_ge))(tt, sl))
                for gi in range(2):
                    sl = slice(gi * 4, (gi + 1) * 4)
                    E("vector", (lambda tt_, sl_: lambda e: e.tensor_tensor(
                        out=mstate[:, sl_], in0=mstate[:, sl_],
                        in1=sview[:, tt_, sl_], op=ALU.subtract))(tt, sl))
            t_spk = pl.mark("vector", sV)
            t_c_done = t_spk
            for m in range(8):
                bank = pm[m % 2]
                for k in range(8):
                    E("tensor", (lambda m_, k_, bank_: lambda e: e.matmul(
                        bank_[:, 0:CB], lifo_bf[:, k_ * D + m_ * 128: k_ * D + m_ * 128 + 128],
                        spkB[:, k_ * CB:(k_ + 1) * CB], start=(k_ == 0), stop=(k_ == 7)))(m, k, bank),
                        waits=[t_spk, bfree.get(m % 2)] if k == 0 else ())
                tmm = pl.mark("tensor", sM)
                tev = E("vector", (lambda m_, bank_: lambda e: e.tensor_tensor(
                    out=x2B[:, m_ * CB:(m_ + 1) * CB], in0=bank_[:, 0:CB],
                    in1=x1B[:, m_ * CB:(m_ + 1) * CB], op=ALU.add))(m, bank),
                    waits=[tmm], inc=sV)
                bfree[m % 2] = tev
            t_spk_used = pl.tok(sM)
            t_x2 = pl.tok(sV)
            t_v_done = t_x2
            t_x2_st = E("sync", (lambda cl: lambda e: e.dma_start(
                out=_dr3(s_x2, cl), in_=_fm3(x2B, 8)))(col),
                waits=[t_x2], inc=dW, dma=True)
            if DEV_DEBUG:
                E("sync", (lambda cl: lambda e: e.dma_start(
                    out=_dr3(s_dbg1, cl), in_=_fm3(x1B, 8)))(col), waits=[t_x2], inc=dW, dma=True)
                E("sync", (lambda cl: lambda e: e.dma_start(
                    out=_dr3(s_dbg2, cl), in_=_fm3(spkB, 8)))(col), waits=[t_x2], inc=dW, dma=True)
        pl.barrier(bar)

    def moe_ffn_phase(is_ffn, half, npass):
        nf = DFF if is_ffn else DE
        nfs = nf // 512
        thp = th // npass
        nttp = thp // 128
        nccp = max(thp // cc, 1)
        q0 = half * nccp
        pref = ("d_" if is_ffn else "c_") + str(half) + "_"
        with ExitStack() as ph:
            csb = lambda nm, *a: ph.enter_context(nc.sbuf_tensor(pref + nm, *a))
            xn = csb("xn", [128, 8 * thp], BF16)
            acc = csb("acc", [128, nttp * D], F32)
            wtm = csb("wtm", [128, nttp * 16], F32)
            xw = csb("xw", [128, 8 * cc], F32)
            xwb = csb("xwb", [128, 8 * cc], BF16)
            sqC = csb("sqC", [128, 8 * cc], BF16)
            rowsC = csb("rowsC", [1, 4 * cc], F32)
            rowsCbf = csb("rowsCbf", [1, 2 * cc], BF16)
            repC = csb("repC", [128, 2 * cc], F32)
            wsl = [csb(f"wsl{i}", [128, 8 * 512], BF16) for i in range(2)]
            usl = [csb(f"usl{i}", [128, 8 * 512], BF16) for i in range(2)]
            dsl = [csb(f"dsl{i}", [128, 4 * 1024], BF16) for i in range(2)]
            hb = csb("hb", [128, 4 * 512], BF16)
            gsb = csb("gsb", [128, 4 * 512], BF16)
            usb = csb("usb", [128, 4 * 512], BF16)
            smt = csb("smt", [128, 64], F32)
            wrow_bf = csb("wrow_bf", [8, 128], BF16)
            outc = csb("outc", [128, 8 * cc], F32)

            tE = {}
            t_src_used = None
            t_q0 = [pl.tok(dW), pl.tok(dG)]
            for qq in range(nccp):
                q = q0 + qq
                qa = slice(q * cc, (q + 1) * cc)
                if is_ffn:
                    tld = E("sync", (lambda qa_: lambda e: e.dma_start(
                        out=_fm3(xw, 8), in_=_dr3(s_x4, qa_)))(qa),
                        waits=[t_src_used] + (t_q0 if q == 0 else []), inc=dW, dma=True)
                else:
                    base = _dr3(s_x2, slice(0, cc))
                    tld = E("gpsimd", (lambda base_, q_: lambda e: e.dma_start(
                        out=_fm3(xw, 8), in_=bass.AP(base_.tensor, regholder[f"r{q_}"], base_.ap)))(base, q),
                        waits=[t_hoff, t_src_used] + (t_q0 if q == 0 else []), inc=dG, dma=True)
                tsq = E("scalar", lambda e: e.activation(out=sqC[:], in_=xw[:], func=AF.Square),
                        waits=[tld], inc=sA)
                tcst = E("vector", lambda e: e.tensor_copy(out=xwb[:], in_=xw[:]),
                         waits=[tld], inc=sV)
                for k in range(8):
                    E("tensor", (lambda k_: lambda e: e.matmul(
                        prow[:, 0:cc], C["c1k_bf"][:], sqC[:, k_ * cc:(k_ + 1) * cc],
                        start=(k_ == 0), stop=(k_ == 7)))(k),
                        waits=[tsq, tE.get("prow")] if k == 0 else ())
                if is_ffn:
                    for k in range(8):
                        E("tensor", (lambda k_: lambda e: e.matmul(
                            prow[:, cc:2 * cc], C["c1k_bf"][:], xwb[:, k_ * cc:(k_ + 1) * cc],
                            start=(k_ == 0), stop=(k_ == 7)))(k), waits=[tcst] if k == 0 else ())
                tms = pl.mark("tensor", sM)
                RC = lambda i: rowsC[:, i * cc:(i + 1) * cc]
                tev = E("vector", lambda e: e.tensor_copy(out=RC(0), in_=prow[:, 0:cc]),
                        waits=[tms, tE.get("rowact")], inc=sV)
                if is_ffn:
                    tev = E("vector", lambda e: e.tensor_copy(out=RC(1), in_=prow[:, cc:2 * cc]), inc=sV)
                tE["prow"] = tev
                if is_ffn:
                    tv0 = E("vector", lambda e: e.tensor_tensor(out=RC(2), in0=RC(1), in1=RC(1), op=ALU.mult), waits=[tev], inc=sV)
                    tv = E("vector", lambda e: e.tensor_tensor(out=RC(2), in0=RC(0), in1=RC(2), op=ALU.subtract), waits=[tv0], inc=sV)
                    E("scalar", lambda e: e.activation(out=RC(3), in_=RC(2), func=AF.Ln, bias=C["eps5"][0:1, 0:1]), waits=[tv])
                    ta = E("scalar", lambda e: e.activation(out=RC(3), in_=RC(3), func=AF.Exp, scale=-0.5), inc=sA)
                    tE["rowact"] = ta
                    E("vector", lambda e: e.tensor_copy(out=rowsCbf[:, 0:cc], in_=RC(1)), waits=[ta])
                    trb = E("vector", lambda e: e.tensor_copy(out=rowsCbf[:, cc:2 * cc], in_=RC(3)), inc=sV)
                    tm1 = E("tensor", lambda e: e.matmul(prep[:, 0:cc], C["ones_bf"][:],
                                                         rowsCbf[:, 0:cc], start=True, stop=True),
                            waits=[trb, tE.get("prep")], inc=sM)
                    te1 = E("vector", lambda e: e.tensor_copy(out=repC[:, 0:cc], in_=prep[:, 0:cc]),
                            waits=[tm1], inc=sV)
                    tm2 = E("tensor", lambda e: e.matmul(prep[:, 0:cc], C["ones_bf"][:],
                                                         rowsCbf[:, cc:2 * cc], start=True, stop=True),
                            waits=[te1], inc=sM)
                    te2 = E("vector", lambda e: e.tensor_copy(out=repC[:, cc:2 * cc], in_=prep[:, 0:cc]),
                            waits=[tm2], inc=sV)
                    tE["prep"] = te2
                    for k in range(8):
                        kc = slice(k * cc, (k + 1) * cc)
                        xnc = slice(k * thp + qq * cc, k * thp + (qq + 1) * cc)
                        E("vector", (lambda kc_: lambda e: e.tensor_tensor(
                            out=xw[:, kc_], in0=xw[:, kc_], in1=repC[:, 0:cc], op=ALU.subtract))(kc),
                            waits=[te2] if k == 0 else ())
                        E("vector", (lambda kc_, xnc_: lambda e: e.tensor_tensor(
                            out=xn[:, xnc_], in0=xw[:, kc_], in1=repC[:, cc:2 * cc], op=ALU.mult))(kc, xnc))
                else:
                    E("scalar", lambda e: e.activation(out=RC(1), in_=RC(0), func=AF.Ln, bias=C["eps6"][0:1, 0:1]),
                      waits=[tev])
                    tr2 = E("scalar", lambda e: e.activation(out=RC(1), in_=RC(1), func=AF.Exp, scale=-0.5), inc=sA)
                    tE["rowact"] = tr2
                    trb = E("vector", lambda e: e.tensor_copy(out=rowsCbf[:, 0:cc], in_=RC(1)),
                            waits=[tr2], inc=sV)
                    tm1 = E("tensor", lambda e: e.matmul(prep[:, 0:cc], C["ones_bf"][:],
                                                         rowsCbf[:, 0:cc], start=True, stop=True),
                            waits=[trb, tE.get("prep")], inc=sM)
                    te1 = E("vector", lambda e: e.tensor_copy(out=repC[:, 0:cc], in_=prep[:, 0:cc]),
                            waits=[tm1], inc=sV)
                    tE["prep"] = te1
                    for k in range(8):
                        kc = slice(k * cc, (k + 1) * cc)
                        xnc = slice(k * thp + qq * cc, k * thp + (qq + 1) * cc)
                        E("vector", (lambda kc_: lambda e: e.tensor_tensor(
                            out=xw[:, kc_], in0=xw[:, kc_], in1=repC[:, 0:cc], op=ALU.mult))(kc),
                            waits=[te1] if k == 0 else ())
                        E("vector", (lambda kc_, xnc_, k_: lambda e: e.tensor_scalar(
                            out=xn[:, xnc_], in0=xw[:, kc_], scalar1=C["moew"][:, k_:k_ + 1],
                            scalar2=None, op0=ALU.mult))(kc, xnc, k))
                t_src_used = pl.mark("vector", sV)
            t_xn = t_src_used
            gmat = C["fgat_bf"] if is_ffn else C["rout_bf"]
            for ti in range(nttp):
                for k in range(8):
                    E("tensor", (lambda ti_, k_: lambda e: e.matmul(
                        pt[:, 0:NE], xn[:, k_ * thp + ti_ * 128: k_ * thp + ti_ * 128 + 128],
                        gmat[:, k_ * NE:(k_ + 1) * NE], start=(k_ == 0), stop=(k_ == 7)))(ti, k),
                        waits=[t_xn, tE.get("pt")] if k == 0 else ())
                tmg = pl.mark("tensor", sM)
                S = lambda i: smt[:, i * 8:(i + 1) * 8]
                tev = E("vector", lambda e: e.tensor_copy(out=S(0), in_=pt[:, 0:NE]),
                        waits=[tmg], inc=sV)
                tE["pt"] = tev
                if is_ffn:
                    tev = E("vector", lambda e: e.tensor_tensor(out=S(0), in0=S(0), in1=C["gbr"][:],
                                                                op=ALU.add), waits=[tev], inc=sV)
                tmx = E("vector", lambda e: e.reduce_max(out=smt[:, 56:57], in_=S(0), axis=AX),
                        waits=[tev], inc=sV)
                tsb = E("vector", lambda e: e.tensor_tensor(
                    out=S(1), in0=S(0), in1=smt[:, 56:57].broadcast_to((128, 8)), op=ALU.subtract),
                    waits=[tmx], inc=sV)
                tex = E("scalar", lambda e: e.activation(out=S(2), in_=S(1), func=AF.Exp),
                        waits=[tsb], inc=sA)
                tsm = E("vector", lambda e: e.reduce_sum(out=smt[:, 57:58], in_=S(2), axis=AX),
                        waits=[tex], inc=sV)
                trc = E("vector", lambda e: e.reciprocal(out=smt[:, 58:59], in_=smt[:, 57:58]),
                        waits=[tsm], inc=sV)
                tpb = E("vector", lambda e: e.tensor_tensor(
                    out=S(3), in0=S(2), in1=smt[:, 58:59].broadcast_to((128, 8)), op=ALU.mult),
                    waits=[trc], inc=sV)
                if is_ffn:
                    tm2_ = E("vector", lambda e: e.reduce_max(out=smt[:, 59:60], in_=S(3), axis=AX),
                             waits=[tpb], inc=sV)
                    tmk = E("vector", lambda e: e.tensor_tensor(
                        out=S(4), in0=S(3), in1=smt[:, 59:60].broadcast_to((128, 8)), op=ALU.is_ge),
                        waits=[tm2_], inc=sV)
                else:
                    tm1_ = E("vector", lambda e: e.reduce_max(out=smt[:, 59:60], in_=S(3), axis=AX),
                             waits=[tpb], inc=sV)
                    tlt = E("vector", lambda e: e.tensor_tensor(
                        out=S(5), in0=S(3), in1=smt[:, 59:60].broadcast_to((128, 8)), op=ALU.is_lt),
                        waits=[tm1_], inc=sV)
                    tz = E("vector", lambda e: e.tensor_tensor(out=S(5), in0=S(3), in1=S(5),
                                                               op=ALU.mult), waits=[tlt], inc=sV)
                    tm2_ = E("vector", lambda e: e.reduce_max(out=smt[:, 60:61], in_=S(5), axis=AX),
                             waits=[tz], inc=sV)
                    tmk = E("vector", lambda e: e.tensor_tensor(
                        out=S(4), in0=S(3), in1=smt[:, 60:61].broadcast_to((128, 8)), op=ALU.is_ge),
                        waits=[tm2_], inc=sV)
                E("vector", (lambda ti_: lambda e: e.tensor_tensor(
                    out=wtm[:, ti_ * 16:ti_ * 16 + 8], in0=S(3), in1=S(4), op=ALU.mult))(ti),
                    waits=[tmk])
            t_w = pl.mark("vector", sV)
            tE[("pd", 1)] = tE.get("prep")
            E("vector", lambda e: e.memset(acc[:], 0.0))
            t_acc0 = pl.mark("vector", sV)
            t_par_used = [None, None]
            for ee in range(NE):
                for fs in range(nfs):
                    par = (ee * nfs + fs) % 2
                    if is_ffn:
                        wsrc = w1[ee, :, fs * 512:(fs + 1) * 512]
                        dsrc = w2[ee, fs * 512:(fs + 1) * 512, :]
                    else:
                        wsrc = gate_w[ee, :, fs * 512:(fs + 1) * 512]
                        dsrc = down_w[ee, fs * 512:(fs + 1) * 512, :]
                    tl1 = E("gpsimd", (lambda w_, p_: lambda e: e.dma_start(
                        out=_fm3(wsl[p_], 8), in_=_r128(w_)))(wsrc, par),
                        waits=[t_par_used[par]], inc=dG, dma=True)
                    if not is_ffn:
                        tl2 = E("gpsimd", (lambda w_, p_: lambda e: e.dma_start(
                            out=_fm3(usl[p_], 8), in_=_r128(w_)))(up_w[ee, :, fs * 512:(fs + 1) * 512], par),
                            inc=dG, dma=True)
                    tl3 = E("gpsimd", (lambda w_, p_: lambda e: e.dma_start(
                        out=_fm3(dsl[p_], 4), in_=_r128(w_)))(dsrc, par), inc=dG, dma=True)
                    t_wload = pl.tok(dG)
                    for qq in range(nccp):
                        q = q0 + qq
                        for mf in range(4):
                            hbk = mf % 2
                            bank = pm[hbk]
                            for k in range(8):
                                E("tensor", (lambda k_, mf_, q_, p_, bank_: lambda e: e.matmul(
                                    bank_[:, 0:cc],
                                    wsl[p_][:, k_ * 512 + mf_ * 128: k_ * 512 + mf_ * 128 + 128],
                                    xn[:, k_ * thp + q_ * cc: k_ * thp + (q_ + 1) * cc],
                                    start=(k_ == 0), stop=(k_ == 7)))(k, mf, qq, par, bank),
                                    waits=[t_wload, t_xn, t_acc0, tE.get(("pb", hbk))] if k == 0 else ())
                            tmmg = pl.mark("tensor", sM)
                            if is_ffn:
                                tg = E("scalar", (lambda mf_, bank_, e_, fs_: lambda e: e.activation(
                                    out=hb[:, mf_ * cc:(mf_ + 1) * cc], in_=bank_[:, 0:cc],
                                    func=AF.Gelu, bias=C["b1c"][:, e_ * 32 + fs_ * 4 + mf_:
                                                               e_ * 32 + fs_ * 4 + mf_ + 1]))(mf, bank, ee, fs),
                                    waits=[tmmg, tE.get("hbfree")], inc=sA)
                            else:
                                tg = E("scalar", (lambda mf_, bank_: lambda e: e.activation(
                                    out=gsb[:, mf_ * cc:(mf_ + 1) * cc], in_=bank_[:, 0:cc],
                                    func=AF.Silu))(mf, bank), waits=[tmmg], inc=sA)
                            tE[("pb", hbk)] = tg
                            tE["h"] = tg
                        if not is_ffn:
                            for mf in range(4):
                                ubk = mf % 2
                                bank = pm[ubk]
                                for k in range(8):
                                    E("tensor", (lambda k_, mf_, q_, p_, bank_: lambda e: e.matmul(
                                        bank_[:, 0:cc],
                                        usl[p_][:, k_ * 512 + mf_ * 128: k_ * 512 + mf_ * 128 + 128],
                                        xn[:, k_ * thp + q_ * cc: k_ * thp + (q_ + 1) * cc],
                                        start=(k_ == 0), stop=(k_ == 7)))(k, mf, qq, par, bank),
                                        waits=[tE.get(("pb", ubk))] if k == 0 else ())
                                tmmg = pl.mark("tensor", sM)
                                tu = E("vector", (lambda mf_, bank_: lambda e: e.tensor_copy(
                                    out=usb[:, mf_ * cc:(mf_ + 1) * cc], in_=bank_[:, 0:cc]))(mf, bank),
                                    waits=[tmmg], inc=sV)
                                tE[("pb", ubk)] = tu
                            tgs = pl.tok(sA)
                            for mf in range(4):
                                E("vector", (lambda mf_: lambda e: e.tensor_tensor(
                                    out=hb[:, mf_ * cc:(mf_ + 1) * cc],
                                    in0=gsb[:, mf_ * cc:(mf_ + 1) * cc],
                                    in1=usb[:, mf_ * cc:(mf_ + 1) * cc], op=ALU.mult))(mf),
                                    waits=[tgs, tE.get("hbfree")] if mf == 0 else ())
                            t_h = pl.mark("vector", sV)
                        else:
                            t_h = pl.tok(sA)
                        for tm in range(ntc):
                            for dh in range(2):
                                dbk = (tm * 2 + dh) % 2
                                bank = (pm[2], prep)[dbk]
                                for kf in range(4):
                                    E("tensor", (lambda kf_, tm_, dh_, p_, bank_: lambda e: e.matmul(
                                        bank_[:, 0:512],
                                        hb[:, kf_ * cc + tm_ * 128: kf_ * cc + tm_ * 128 + 128],
                                        dsl[p_][:, kf_ * 1024 + dh_ * 512: kf_ * 1024 + (dh_ + 1) * 512],
                                        start=(kf_ == 0), stop=(kf_ == 3)))(kf, tm, dh, par, bank),
                                        waits=[t_h, t_w, tE.get(("pd", dbk))] if kf == 0 else ())
                                tmmg = pl.mark("tensor", sM)
                                ti_g = qq * ntc + tm
                                td = E("vector", (lambda ti__, dh_, e_, bank_: lambda e: e.scalar_tensor_tensor(
                                    out=acc[:, ti__ * D + dh_ * 512: ti__ * D + (dh_ + 1) * 512],
                                    in0=bank_[:, 0:512],
                                    scalar=wtm[:, ti__ * 16 + e_:ti__ * 16 + e_ + 1],
                                    in1=acc[:, ti__ * D + dh_ * 512: ti__ * D + (dh_ + 1) * 512],
                                    op0=ALU.mult, op1=ALU.add))(ti_g, dh, ee, bank),
                                    waits=[tmmg], inc=sV)
                                tE[("pd", dbk)] = td
                                tE["hbfree"] = pl.tok(sM)
                    t_par_used[par] = pl.tok(sM)
            t_expert_done = pl.tok(sV)
            if is_ffn:
                for ti in range(nttp):
                    ttr = E("tensor", (lambda ti_: lambda e: e.transpose(
                        pt[0:8, 0:128], wtm[:, ti_ * 16:ti_ * 16 + 8], C["iden_f"][:]))(ti),
                        waits=[t_expert_done, tE.get("pt2")], inc=sM)
                    tevt = E("vector", lambda e: e.tensor_copy(out=wrow_bf[:], in_=pt[0:8, 0:128]),
                             waits=[ttr], inc=sV)
                    tE["pt2"] = tevt
                    for dh in range(2):
                        bank = pm[dh]
                        tmb = E("tensor", (lambda dh_, bank_: lambda e: e.matmul(
                            bank_[:, 0:512], wrow_bf[:],
                            C["b2_bf"][:, dh_ * 512:(dh_ + 1) * 512],
                            start=True, stop=True))(dh, bank),
                            waits=[tevt, tE.get(("b2", dh))], inc=sM)
                        tb2e = E("vector", (lambda ti_, dh_, bank_: lambda e: e.tensor_tensor(
                            out=acc[:, ti_ * D + dh_ * 512: ti_ * D + (dh_ + 1) * 512],
                            in0=acc[:, ti_ * D + dh_ * 512: ti_ * D + (dh_ + 1) * 512],
                            in1=bank_[:, 0:512], op=ALU.add))(ti, dh, bank),
                            waits=[tmb], inc=sV)
                        tE[("b2", dh)] = tb2e
            t_acc_done = pl.mark("vector", sV)
            t_out_used = None
            for qq in range(nccp):
                q = q0 + qq
                qa = slice(q * cc, (q + 1) * cc)
                if is_ffn:
                    tld = E("sync", (lambda qa_: lambda e: e.dma_start(
                        out=_fm3(xw, 8), in_=_dr3(s_x4, qa_)))(qa),
                        waits=[t_out_used, t_xn], inc=dW, dma=True)
                    tld2 = None
                else:
                    base = _dr3(s_x2, slice(0, cc))
                    tld = E("gpsimd", (lambda base_, q_: lambda e: e.dma_start(
                        out=_fm3(xw, 8), in_=bass.AP(base_.tensor, regholder[f"r{q_}"], base_.ap)))(base, q),
                        waits=[t_hoff, t_out_used, t_xn], inc=dG, dma=True)
                    base2 = _dr3(x_fm, slice(0, cc))
                    tld2 = E("gpsimd", (lambda base_, q_: lambda e: e.dma_start(
                        out=_fm3(outc, 8), in_=bass.AP(base_.tensor, regholder[f"r{q_}"], base_.ap)))(base2, q),
                        inc=dG, dma=True)
                for tm in range(ntc):
                    ti_g = qq * ntc + tm
                    for j in range(8):
                        ttr = E("tensor", (lambda ti__, j_: lambda e: e.transpose(
                            pt[:], acc[:, ti__ * D + j_ * 128: ti__ * D + (j_ + 1) * 128],
                            C["iden_f"][:]))(ti_g, j),
                            waits=[t_acc_done, tE.get("ptt")], inc=sM)
                        if is_ffn:
                            tvv = E("vector", (lambda j_, tm_: lambda e: e.tensor_tensor(
                                out=outc[:, j_ * cc + tm_ * 128: j_ * cc + tm_ * 128 + 128],
                                in0=pt[:], in1=xw[:, j_ * cc + tm_ * 128: j_ * cc + tm_ * 128 + 128],
                                op=ALU.add))(j, tm), waits=[ttr, tld], inc=sV)
                        else:
                            E("vector", (lambda j_, tm_: lambda e: e.tensor_tensor(
                                out=xw[:, j_ * cc + tm_ * 128: j_ * cc + tm_ * 128 + 128],
                                in0=pt[:], in1=xw[:, j_ * cc + tm_ * 128: j_ * cc + tm_ * 128 + 128],
                                op=ALU.add))(j, tm), waits=[ttr, tld, tld2])
                            tvv = E("vector", (lambda j_, tm_: lambda e: e.tensor_tensor(
                                out=outc[:, j_ * cc + tm_ * 128: j_ * cc + tm_ * 128 + 128],
                                in0=outc[:, j_ * cc + tm_ * 128: j_ * cc + tm_ * 128 + 128],
                                in1=xw[:, j_ * cc + tm_ * 128: j_ * cc + tm_ * 128 + 128],
                                op=ALU.add))(j, tm), inc=sV)
                        tE["ptt"] = tvv
                t_wr = pl.mark("vector", sV)
                if is_ffn:
                    t_st = E("sync", (lambda qa_: lambda e: e.dma_start(
                        out=_dr3(out_fm, qa_), in_=_fm3(outc, 8)))(qa),
                        waits=[t_wr], inc=dW, dma=True)
                else:
                    t_st = E("sync", (lambda qa_: lambda e: e.dma_start(
                        out=_dr3(s_x4, qa_), in_=_fm3(outc, 8)))(qa),
                        waits=[t_wr], inc=dW, dma=True)
                t_out_used = t_st
        pl.barrier(bar)

    npass = 2 if th >= 2 * cc else 1
    for hp in range(npass):
        moe_ffn_phase(False, hp, npass)
    for hp in range(npass):
        moe_ffn_phase(True, hp, npass)

    tall_w = pl.tok(dW)
    tall_g = pl.tok(dG)
    E("sync", lambda e: e.nop(), waits=[tall_w, tall_g])

    def gpregs(eng):
        regs = [eng.register(f"roff{q}").__enter__() for q in range(ncc)]
        eng.wait_ge(pl.sem_handles[dW], t_hoff[1])
        eng.reg_load(regs[0], C["hoff_sb"][0:1, 0:1])
        for q in range(1, ncc):
            eng.reg_alu(regs[q], regs[0], q * cc, ALU.add)
        for q in range(ncc):
            regholder[f"r{q}"] = regs[q]

    with nc.Block() as block:
        for s in SEMS:
            pl.sem_handles[s] = es.enter_context(nc.semaphore(s))
        pl.replay(block, {"gpsimd": gpregs})

    es.close()
    return nc


def _stage(inputs, t):
    th = t // 2
    x = np.asarray(inputs["L0_attn_x"])
    f = {}
    f["in_w"] = np.ascontiguousarray(inputs["L0_attn_in_w"], np.float32)
    f["xproj_w"] = np.ascontiguousarray(inputs["L0_attn_x_proj_w"], np.float32)
    f["out_w"] = np.ascontiguousarray(inputs["L0_attn_out_w"], np.float32)
    f["lif_in_w"] = np.ascontiguousarray(inputs["L0_attn_lif_in_w"], np.float32)
    f["lif_out_w"] = np.ascontiguousarray(inputs["L0_attn_lif_out_w"], np.float32)
    f["router_w"] = np.ascontiguousarray(inputs["L0_attn_router_w"], np.float32)
    f["gate_w"] = np.ascontiguousarray(inputs["L0_attn_gate_w"], np.float32)
    f["up_w"] = np.ascontiguousarray(inputs["L0_attn_up_w"], np.float32)
    f["down_w"] = np.ascontiguousarray(inputs["L0_attn_down_w"], np.float32)
    f["fgate_w"] = np.ascontiguousarray(inputs["L0_ffn_gate_w"], np.float32)
    f["w1"] = np.ascontiguousarray(inputs["L0_ffn_w1"], np.float32)
    f["w2"] = np.ascontiguousarray(inputs["L0_ffn_w2"], np.float32)
    f["b2r"] = np.ascontiguousarray(inputs["L0_ffn_b2"], np.float32)
    cw = np.asarray(inputs["L0_attn_conv_w"], np.float32)[:, 0, :]
    f["conv_w_fm"] = np.ascontiguousarray(cw.reshape(16, 128, 4).transpose(1, 0, 2).reshape(128, 64))
    f["conv_b_fm"] = np.ascontiguousarray(np.asarray(inputs["L0_attn_conv_b"], np.float32).reshape(16, 128).T)
    al = np.asarray(inputs["L0_attn_A_log"], np.float32)
    f["alog_fm"] = np.ascontiguousarray(al.reshape(16, 128, 16).transpose(1, 0, 2).reshape(128, 256))
    f["dskip_fm"] = np.ascontiguousarray(np.asarray(inputs["L0_attn_Dskip"], np.float32).reshape(16, 128).T)
    f["dtw_fm"] = np.ascontiguousarray(np.asarray(inputs["L0_attn_dt_w"], np.float32)[0].reshape(16, 128).T)
    f["dtb_fm"] = np.ascontiguousarray(np.asarray(inputs["L0_attn_dt_b"], np.float32).reshape(16, 128).T)
    f["ssmw_fm"] = np.ascontiguousarray(np.asarray(inputs["L0_attn_ssm_norm_w"], np.float32).reshape(8, 128).T)
    f["lifw_fm"] = np.ascontiguousarray(np.asarray(inputs["L0_attn_lif_norm_w"], np.float32).reshape(8, 128).T)
    f["moew_fm"] = np.ascontiguousarray(np.asarray(inputs["L0_attn_moe_norm_w"], np.float32).reshape(8, 128).T)
    b1 = np.asarray(inputs["L0_ffn_b1"], np.float32)
    f["b1_fm"] = np.ascontiguousarray(b1.reshape(8, 32, 128).transpose(2, 0, 1).reshape(128, 256))
    f["gb_rep"] = np.ascontiguousarray(np.tile(np.asarray(inputs["L0_ffn_gate_b"], np.float32)[None, :], (128, 1)))
    f["c_inv1024"] = np.full((128, 1), 1.0 / 1024, np.float32)
    f["eps5"] = np.full((128, 1), 1e-5, np.float32)
    f["eps6"] = np.full((128, 1), 1e-6, np.float32)
    f["ones1"] = np.ones((1, 128), np.float32)
    f["iden"] = np.eye(128, dtype=np.float32)

    feeds = []
    for c in range(8):
        b, h = c // 2, c % 2
        fc = dict(f)
        fc["x_fm"] = np.ascontiguousarray(x[b].T.astype(np.float32))
        fc["hoff"] = np.full((1, 1), h * th, np.uint32)
        feeds.append(fc)
    return feeds


def kernel(**inputs):
    t = 4096
    th = t // 2
    feeds = _stage(inputs, t)
    nc = build_program(t)
    res = run_bass_kernel_spmd(nc, feeds, core_ids=list(range(8)))
    out = np.empty((4, t, D), np.float32)
    for c in range(8):
        b, h = c // 2, c % 2
        out[b, h * th:(h + 1) * th, :] = np.asarray(res.results[c]["out_fm"]).T
    return out
